# version 18
# baseline (speedup 1.0000x reference)
import contextlib
import numpy as np

import concourse.bass as bass
import concourse.mybir as mybir
from concourse.bass_utils import run_bass_kernel_spmd

F32 = mybir.dt.float32
BF16 = mybir.dt.bfloat16
I8 = mybir.dt.int8
AF = mybir.ActivationFunctionType
ALU = mybir.AluOpType
AX = mybir.AxisListType

D = 2048
S = 2048
H = 16
INTER = 8192
P_SPLIT = 1152
CSC = float(128.0 ** -0.25)
EPS = 1e-5

ENGS = ("pe", "act", "dve", "pool", "sp")
SEMS = ("sp", "aq", "gq", "st", "pe", "ac", "dv", "pl",
        "d0", "d1", "d2", "d3", "d4", "d5", "d6", "d7", "d8", "d9", "da", "db",
        "e0", "e1", "e2", "e3", "e4", "e5")
SEM_OWNER = {"pe": "pe", "ac": "act", "dv": "dve", "pl": "pool"}

_STAGE = 3


FENCED = {"act": "ac", "dve": "dv", "pool": "pl"}


class RoleBuild:

    def __init__(self, nc, sems):
        self.nc = nc
        self.sems = sems
        self.q = {k: [] for k in ENGS}
        self.ct = {k: 0 for k in SEMS}

    def do(self, eng, fn):
        if eng in FENCED:
            return self.inc(eng, FENCED[eng], fn)
        self.q[eng].append(fn)

    def wait(self, eng, sem, val):
        if val > 0:
            h = self.sems[sem]
            self.q[eng].append(lambda e, h=h, v=val: e.wait_ge(h, v))

    def inc(self, eng, sem, fn, n=1):
        if eng in FENCED:
            own = FENCED[eng]
            self.wait(eng, own, self.ct[own])
        h = self.sems[sem]
        self.ct[sem] += n
        self.q[eng].append(lambda e, fn=fn, h=h, n=n: fn(e).then_inc(h, n))
        return self.ct[sem]

    def dma(self, eng, sem, out, in_, **kw):
        return self.inc(eng, sem,
                        lambda e, o=out, i=in_, kw=kw: e.dma_start(out=o, in_=i, **kw),
                        n=16)

    def barrier(self, engs=ENGS):
        snap = dict(self.ct)
        for eng in engs:
            for s, v in snap.items():
                if v > 0 and SEM_OWNER.get(s) != eng:
                    self.wait(eng, s, v)


def _ln_rows(R, PSsum, PSsq, rows, W, ninv):
    rA = rows[0:1, 0:W]
    rB = rows[0:1, 512:512 + W]
    rC = rows[0:1, 1024:1024 + W]
    rD = rows[0:1, 1536:1536 + W]
    R.inc("act", "ac", lambda e: e.activation(out=rA, in_=PSsum[0:1, 0:W], func=AF.Identity, bias=0.0, scale=ninv))
    R.inc("act", "ac", lambda e: e.activation(out=rC, in_=PSsq[0:1, 0:W], func=AF.Identity, bias=0.0, scale=ninv))
    R.wait("dve", "ac", R.ct["ac"])
    R.inc("dve", "dv", lambda e: e.tensor_mul(rB, rA, rA))
    R.inc("dve", "dv", lambda e: e.scalar_tensor_tensor(out=rD, in0=rC, scalar=EPS, in1=rB, op0=ALU.add, op1=ALU.subtract))
    R.wait("act", "dv", R.ct["dv"])
    R.inc("act", "ac", lambda e: e.activation(out=rB, in_=rD, func=AF.Sqrt, bias=0.0, scale=1.0))
    R.wait("dve", "ac", R.ct["ac"])
    R.inc("dve", "dv", lambda e: e.reciprocal(out=rC, in_=rB))
    R.inc("dve", "dv", lambda e: e.tensor_mul(rD, rA, rC))
    return rC, rD


def _grp(total, width):
    out, o = [], 0
    while o < total:
        out.append((o, min(width, total - o)))
        o += min(width, total - o)
    return out


def build_role(nc, sems, es, own_start, own_end, nkeys, dram, pfx):
    R = RoleBuild(nc, sems)
    TOWN = own_end - own_start
    NK = nkeys
    NKC = NK // 128
    NOT = TOWN // 128
    ninv = 1.0 / 2048.0

    key_groups = _grp(NK, 512)
    own_groups = _grp(TOWN, 512)
    ffn_groups = _grp(TOWN, 384) if TOWN == 1152 else _grp(TOWN, 512)
    if TOWN == 1152:
        P1G = [(0, 512), (512, 512), (1024, 128)]
    else:
        P1G = [(0, 256), (256, 256), (512, 256), (768, 256), (1024, 128),
               (1152, 256), (1408, 256), (1664, 256), (1920, 128)]
    sb = lambda st, name, shape, dt: st.enter_context(nc.sbuf_tensor(pfx + name, shape, dt))
    onesk1 = sb(es, "onesk1", [1, 128], F32)
    onescol = sb(es, "onescol", [128, 1], BF16)
    ones128 = sb(es, "ones128", [128, 128], BF16)
    bo_c = sb(es, "bo_c", [128, 16], F32)
    sd_c = sb(es, "sd_c", [128, 16], F32)
    sg_c = sb(es, "sg_c", [128, 64], F32)
    su_c = sb(es, "su_c", [128, 64], F32)
    rows = sb(es, "rows", [1, 2048], F32)
    tscr = sb(es, "tscr", [128, 512], F32)
    x_fm = sb(es, "x_fm", [128, 16 * TOWN], F32)
    PS = [es.enter_context(nc.psum_tensor(f"{pfx}ps{i}", [128, 512], F32)) for i in range(8)]

    att = contextlib.ExitStack()
    atri = sb(att, "atri", [128, 128], BF16)
    bmv = sb(att, "bmv", [128, 4 * 512], BF16)
    bq_c = sb(att, "bq_c", [128, 16], F32)
    bk_c = sb(att, "bk_c", [128, 16], F32)
    negBb = sb(att, "negBb", [128, 16], F32)
    knmax = sb(att, "knmax", [1, 16], F32)
    qnmax = sb(att, "qnmax", [1, 16], F32)
    q_sb = sb(att, "q_sb", [128, 16 * TOWN], BF16)

    def pool_step(fn):
        c = R.inc("pool", "pl", fn)
        R.wait("pool", "pl", c)

    R.do("pool", lambda e: e.memset(onesk1[:], 1.0))
    R.do("pool", lambda e: e.memset(onescol[:], 1.0))
    R.do("pool", lambda e: e.memset(ones128[:], 1.0))
    R.do("pool", lambda e: e.memset(knmax[:], 0.0))
    R.do("pool", lambda e: e.memset(qnmax[:], 0.0))
    pool_step(lambda e: e.memset(atri[:], 1.0))
    pool_step(lambda e: e.affine_select(out=atri[:], in_=atri[:], compare_op=ALU.is_ge,
                                        fill=0.0, base=0, pattern=[[1, 128]], channel_multiplier=-1))
    for jd in range(4):
        if jd > 0:
            R.do("pool", lambda e, jd=jd: e.memset(bmv[:, jd * 512: jd * 512 + jd * 128], -1e9))
        pool_step(lambda e, jd=jd: e.memset(bmv[:, jd * 512 + jd * 128: jd * 512 + jd * 128 + 128], -1e9))
        pool_step(lambda e, jd=jd: e.affine_select(
            out=bmv[:, jd * 512 + jd * 128: jd * 512 + jd * 128 + 128],
            in_=bmv[:, jd * 512 + jd * 128: jd * 512 + jd * 128 + 128],
            compare_op=ALU.is_ge, fill=0.0, base=-1, pattern=[[-1, 128]], channel_multiplier=1))
        if jd < 3:
            R.do("pool", lambda e, jd=jd: e.memset(bmv[:, jd * 512 + (jd + 1) * 128: (jd + 1) * 512], 0.0))


    R.dma("act", "aq", bq_c[:, :], dram["bq"])
    R.dma("act", "aq", bk_c[:, :], dram["bk"])
    R.dma("act", "aq", bo_c[:, :], dram["bo"])
    R.dma("act", "aq", sd_c[:, :], dram["sd"])
    R.dma("act", "aq", sg_c[:, :], dram["sg"])
    R.dma("act", "aq", su_c[:, :], dram["su"])
    R.dma("act", "aq", rows[0:1, :], dram["bv"])
    R.wait("act", "aq", R.ct["aq"])
    R.inc("act", "ac", lambda e: e.activation(out=bq_c[:, :], in_=bq_c[:, :], func=AF.Identity, bias=0.0, scale=CSC))
    R.inc("act", "ac", lambda e: e.activation(out=bk_c[:, :], in_=bk_c[:, :], func=AF.Identity, bias=0.0, scale=CSC))
    R.barrier()

    with contextlib.ExitStack() as p1:
        ln1 = sb(p1, "ln1", [128, 16 * NK], BF16)
        bvb = sb(p1, "bvb", [128, 2048], BF16)

        for o4 in range(4):
            R.wait("pe", "ac", R.ct["ac"])
            R.inc("pe", "pe", lambda e, o4=o4: e.matmul(
                PS[1 + (o4 % 2)][:, 0:512], onesk1[:], rows[0:1, o4 * 512:(o4 + 1) * 512], start=True, stop=True))
            R.wait("act", "pe", R.ct["pe"])
            R.inc("act", "ac", lambda e, o4=o4: e.activation(
                out=bvb[:, o4 * 512:(o4 + 1) * 512], in_=PS[1 + (o4 % 2)][:, 0:512],
                func=AF.Identity, bias=0.0, scale=1.0))

        with contextlib.ExitStack() as p1a:
            xfg = [sb(p1a, f"xfg{k}", [128, 16 * 256], F32) for k in range(2)]
            cb = [sb(p1a, f"cb{k}", [128, 512], BF16) for k in range(2)]
            sq = [sb(p1a, f"sq{k}", [128, 512], BF16) for k in range(2)]
            fg_dv = [0, 0]

            p1_groups = P1G
            fgi = 0
            mm_pe_after = [0, 0]
            for gidx, (g0, gw) in enumerate(p1_groups):
                own = g0 >= own_start
                if not own:
                    fk = fgi % 2
                    fgi += 1

                def xwin(i, own=own, g0=g0, gw=gw, fk=(fgi - 1) % 2):
                    if own:
                        o = g0 - own_start
                        return x_fm[:, i * TOWN + o: i * TOWN + o + gw]
                    return xfg[fk][:, i * 256: i * 256 + gw]

                ds = "d0" if gidx % 2 == 0 else "d1"
                R.wait("sp", ds, R.ct[ds])
                if not own:
                    R.wait("sp", "dv", fg_dv[fk])
                for i in range(16):
                    R.dma("sp", ds, xwin(i), dram["xin"][i * 128:(i + 1) * 128, g0:g0 + gw])
                ld_sp = (ds, R.ct[ds])
                if _STAGE < 2:
                    continue
                R.wait("act", ld_sp[0], ld_sp[1])
                R.wait("dve", ld_sp[0], ld_sp[1])
                for i in range(16):
                    ib = i % 2
                    R.wait("act", "pe", mm_pe_after[ib])
                    R.inc("act", "ac", lambda e, i=i, f=xwin, ib=ib, gw=gw: e.activation(
                        out=cb[ib][:, 0:gw], in_=f(i), func=AF.Identity, bias=0.0, scale=1.0))
                    R.wait("dve", "pe", mm_pe_after[ib])
                    R.inc("dve", "dv", lambda e, i=i, f=xwin, ib=ib, gw=gw: e.tensor_mul(
                        sq[ib][:, 0:gw], f(i), f(i)))
                    R.wait("pe", "ac", R.ct["ac"])
                    R.wait("pe", "dv", R.ct["dv"])
                    R.inc("pe", "pe", lambda e, i=i, ib=ib, gw=gw: e.matmul(
                        PS[4][:, 0:gw], ones128[:], cb[ib][:, 0:gw], start=(i == 0), stop=(i == 15)))
                    R.inc("pe", "pe", lambda e, i=i, ib=ib, gw=gw: e.matmul(
                        PS[5][:, 0:gw], ones128[:], sq[ib][:, 0:gw], start=(i == 0), stop=(i == 15)))
                    mm_pe_after[ib] = R.ct["pe"]
                R.wait("act", "pe", R.ct["pe"])
                rC, rD = _ln_rows(R, PS[4], PS[5], rows, gw, ninv)
                R.wait("pe", "dv", R.ct["dv"])
                R.inc("pe", "pe", lambda e, rC=rC, gw=gw: e.matmul(PS[6][:, 0:gw], onesk1[:], rC, start=True, stop=True))
                R.inc("pe", "pe", lambda e, rD=rD, gw=gw: e.matmul(PS[7][:, 0:gw], onesk1[:], rD, start=True, stop=True))
                R.wait("dve", "pe", R.ct["pe"])
                for i in range(16):
                    R.inc("dve", "dv", lambda e, i=i, f=xwin, gw=gw: e.tensor_mul(
                        tscr[:, 0:gw], f(i), PS[6][:, 0:gw]))
                    R.inc("dve", "dv", lambda e, i=i, g0=g0, gw=gw: e.tensor_sub(
                        ln1[:, i * NK + g0: i * NK + g0 + gw], tscr[:, 0:gw], PS[7][:, 0:gw]))
                if not own:
                    fg_dv[fk] = R.ct["dv"]
            R.barrier()

        if _STAGE >= 2:
            with contextlib.ExitStack() as p1b:
                slab = [sb(p1b, f"slab{k}", [128, 16 * 512], BF16) for k in range(2)]
                ksb = [sb(p1b, f"ksb{k}", [128, 512], BF16) for k in range(2)]
                vsb = [sb(p1b, f"vsb{k}", [128, 512], BF16) for k in range(2)]
                nsc = sb(p1b, "nsc", [128, 512], BF16)
                slab_use_pe = [0, 0]

                def load_slab(k, mat, s4):
                    ds = "e0" if k == 0 else "e1"
                    R.wait("pool", ds, R.ct[ds])
                    R.wait("pool", "pe", slab_use_pe[k])
                    R.dma("pool", ds, slab[k][:].rearrange("p (c o) -> p c o", o=512),
                          dram[mat][s4])
                    return (ds, R.ct[ds])

                proj_rd = [("ac", 0), ("ac", 0)]
                for mat, tiles, qbase, is_q in (("wk", key_groups, 0, False),
                                                ("wq", own_groups, own_start, True)):
                    seq = 0
                    for s4 in range(4):
                        sl = load_slab(s4 % 2, mat, s4)
                        for oc in range(4):
                            o = s4 * 4 + oc
                            for t0, W in tiles:
                                b = seq % 2
                                seq += 1
                                R.wait("pe", sl[0], sl[1])
                                R.wait("pe", proj_rd[b][0], proj_rd[b][1])
                                for i in range(16):
                                    R.inc("pe", "pe", lambda e, i=i, oc=oc, t0=t0, W=W, k=s4 % 2, qb=qbase, b=b: e.matmul(
                                        PS[b][:, 0:W], slab[k][:, i * 512 + oc * 128: i * 512 + oc * 128 + 128],
                                        ln1[:, i * NK + qb + t0: i * NK + qb + t0 + W],
                                        start=(i == 0), stop=(i == 15)))
                                slab_use_pe[s4 % 2] = R.ct["pe"]
                                R.wait("act", "pe", R.ct["pe"])
                                if is_q:
                                    R.inc("act", "ac", lambda e, o=o, t0=t0, W=W, b=b: e.activation(
                                        out=q_sb[:, o * TOWN + t0: o * TOWN + t0 + W], in_=PS[b][:, 0:W],
                                        func=AF.Identity, bias=bq_c[:, o:o + 1], scale=CSC))
                                    proj_rd[b] = ("ac", R.ct["ac"])
                                else:
                                    kbuf = ksb[b]
                                    dsb = "da" if b == 0 else "db"
                                    R.wait("act", dsb, R.ct[dsb])
                                    R.inc("act", "ac", lambda e, o=o, W=W, b=b, kbuf=kbuf: e.activation(
                                        out=kbuf[:, 0:W], in_=PS[b][:, 0:W],
                                        func=AF.Identity, bias=bk_c[:, o:o + 1], scale=CSC))
                                    proj_rd[b] = ("ac", R.ct["ac"])
                                    R.dma("act", dsb, dram["kdram"][o * 128:(o + 1) * 128, t0:t0 + W], kbuf[:, 0:W])

                seq = 0
                for s4 in range(4):
                    sl = load_slab(s4 % 2, "wv", s4)
                    for tt in range(NKC):
                        b = seq % 2
                        seq += 1
                        R.wait("pe", sl[0], sl[1])
                        R.wait("pe", proj_rd[b][0], proj_rd[b][1])
                        for i in range(16):
                            R.inc("pe", "pe", lambda e, i=i, tt=tt, k=s4 % 2, b=b: e.matmul(
                                PS[b][:, 0:512], ln1[:, i * NK + tt * 128: i * NK + tt * 128 + 128],
                                slab[k][:, i * 512:(i + 1) * 512], start=(i == 0), stop=(i == 15)))
                        slab_use_pe[s4 % 2] = R.ct["pe"]
                        dsb = "da" if b == 0 else "db"
                        R.wait("dve", "pe", R.ct["pe"])
                        R.wait("dve", dsb, R.ct[dsb])
                        R.inc("dve", "dv", lambda e, s4=s4, b=b: e.tensor_add(
                            vsb[b][:], PS[b][:, 0:512], bvb[:, s4 * 512:(s4 + 1) * 512]))
                        proj_rd[b] = ("dv", R.ct["dv"])
                        R.wait("act", "dv", R.ct["dv"])
                        R.dma("act", dsb, dram["vdram"][tt * 128:(tt + 1) * 128, s4 * 512:(s4 + 1) * 512], vsb[b][:])

                nrm_pe = [0, 0]
                bank_dv = [0, 0, 0, 0]
                nb = 0
                for o in range(16):
                    for t0, W in own_groups:
                        sbq = nb % 2
                        bk4 = nb % 4
                        nb += 1
                        R.wait("act", "pe", nrm_pe[sbq])
                        R.inc("act", "ac", lambda e, o=o, t0=t0, W=W, sbq=sbq: e.activation(
                            out=ksb[sbq][:, 0:W], in_=q_sb[:, o * TOWN + t0: o * TOWN + t0 + W],
                            func=AF.Square, bias=0.0, scale=1.0))
                        R.wait("pe", "ac", R.ct["ac"])
                        R.wait("pe", "dv", bank_dv[bk4])
                        R.inc("pe", "pe", lambda e, W=W, bk4=bk4, sbq=sbq: e.matmul(
                            PS[4 + bk4][:, 0:W], ones128[:], ksb[sbq][:, 0:W], start=True, stop=True))
                        nrm_pe[sbq] = R.ct["pe"]
                        R.wait("dve", "pe", R.ct["pe"])
                        R.inc("dve", "dv", lambda e, W=W, bk4=bk4: e.tensor_reduce(
                            out=rows[0:1, 504 + bk4:505 + bk4], in_=PS[4 + bk4][0:1, 0:W], axis=AX.X, op=ALU.max))
                        R.inc("dve", "dv", lambda e, o=o, bk4=bk4: e.tensor_tensor(
                            out=qnmax[0:1, o:o + 1], in0=qnmax[0:1, o:o + 1],
                            in1=rows[0:1, 504 + bk4:505 + bk4], op=ALU.max))
                        bank_dv[bk4] = R.ct["dv"]
                kl = slab[0]
                ksq = slab[1]
                R.wait("sp", "pe", slab_use_pe[0])
                R.wait("sp", "da", R.ct["da"])
                R.wait("sp", "db", R.ct["db"])
                slab1_free = slab_use_pe[1]
                kn_sp = [("d0", 0)] * 4
                kn_ac = [0, 0, 0, 0]
                kn_pe = [0, 0, 0, 0]

                def kn_load(o):
                    sl4 = o % 4
                    ds = ("d0", "d1", "d2", "d3")[sl4]
                    R.wait("sp", ds, R.ct[ds])
                    R.wait("sp", "ac", kn_ac[sl4])
                    kn_sp[sl4] = (ds, R.dma("sp", ds, kl[:, sl4 * 2048: sl4 * 2048 + NK],
                                            dram["kdram"][o * 128:(o + 1) * 128, 0:NK]))

                for o in range(4):
                    kn_load(o)
                for o in range(16):
                    sl4 = o % 4
                    R.wait("act", kn_sp[sl4][0], kn_sp[sl4][1])
                    R.wait("act", "pe", slab1_free if o == 0 else 0)
                    R.wait("act", "pe", kn_pe[sl4])
                    R.inc("act", "ac", lambda e, sl4=sl4: e.activation(
                        out=ksq[:, sl4 * 2048: sl4 * 2048 + NK], in_=kl[:, sl4 * 2048: sl4 * 2048 + NK],
                        func=AF.Square, bias=0.0, scale=1.0))
                    kn_ac[sl4] = R.ct["ac"]
                    R.wait("pe", "ac", R.ct["ac"])
                    for t4 in range(NK // 512):
                        bk4 = (o * (NK // 512) + t4) % 4
                        R.wait("pe", "dv", bank_dv[bk4])
                        R.inc("pe", "pe", lambda e, sl4=sl4, t4=t4, bk4=bk4: e.matmul(
                            PS[4 + bk4][:, 0:512], ones128[:],
                            ksq[:, sl4 * 2048 + t4 * 512: sl4 * 2048 + t4 * 512 + 512],
                            start=True, stop=True))
                        R.wait("dve", "pe", R.ct["pe"])
                        R.inc("dve", "dv", lambda e, bk4=bk4: e.tensor_reduce(
                            out=rows[0:1, 504 + bk4:505 + bk4], in_=PS[4 + bk4][0:1, 0:512], axis=AX.X, op=ALU.max))
                        R.inc("dve", "dv", lambda e, o=o, bk4=bk4: e.tensor_tensor(
                            out=knmax[0:1, o:o + 1], in0=knmax[0:1, o:o + 1],
                            in1=rows[0:1, 504 + bk4:505 + bk4], op=ALU.max))
                        bank_dv[bk4] = R.ct["dv"]
                    kn_pe[sl4] = R.ct["pe"]
                    if o + 4 < 16:
                        kn_load(o + 4)
                R.barrier()

    if _STAGE >= 2:
        rT = rows[0:1, 0:16]
        R.inc("dve", "dv", lambda e: e.tensor_mul(rT, qnmax[0:1, :], knmax[0:1, :]))
        R.wait("act", "dv", R.ct["dv"])
        R.inc("act", "ac", lambda e: e.activation(out=rT, in_=rT, func=AF.Sqrt, bias=0.0, scale=1.0))
        R.wait("dve", "ac", R.ct["ac"])
        R.inc("dve", "dv", lambda e: e.tensor_scalar_mul(rT, rT, -1.0))
        R.wait("pe", "dv", R.ct["dv"])
        R.inc("pe", "pe", lambda e: e.matmul(PS[6][:, 0:16], onesk1[:], rT, start=True, stop=True))
        R.wait("act", "pe", R.ct["pe"])
        R.inc("act", "ac", lambda e: e.activation(out=negBb[:], in_=PS[6][:, 0:16],
                                                  func=AF.Identity, bias=0.0, scale=1.0))
        R.barrier()

        with contextlib.ExitStack() as p4:
            o_sb = sb(p4, "o_sb", [128, 16 * TOWN], BF16)
            slab2 = [sb(p4, f"slb2{k}", [128, 16 * 256], BF16) for k in range(2)]
            slab2_use = [0, 0]
            slab2_sl = [("e0", 0)] * 8

            def load_slab2(s8):
                kk = s8 % 2
                ds = "e0" if kk == 0 else "e1"
                R.wait("pool", ds, R.ct[ds])
                R.wait("pool", "pe", slab2_use[kk])
                R.dma("pool", ds, slab2[kk][:].rearrange("p (c o) -> p c o", o=256),
                      dram["wo"][s8])
                slab2_sl[s8] = (ds, R.ct[ds])

            with contextlib.ExitStack() as p4b:
                kh = [sb(p4b, f"kh{k}", [128, NK], BF16) for k in range(2)]
                vh = [sb(p4b, f"vh{k}", [128, NK], BF16) for k in range(2)]
                pt = [sb(p4b, f"pt{k}", [128, 512], BF16) for k in range(3)]
                rbuf = [sb(p4b, f"rbuf{k}", [128, 512], F32) for k in range(2)]
                kv_use_pe = [0, 0]
                pt_use_pe = [0, 0, 0]

                def load_head(h):
                    k = h % 2
                    ds = "d4" if k == 0 else "d5"
                    R.wait("sp", ds, R.ct[ds])
                    R.wait("sp", "pe", kv_use_pe[k])
                    R.dma("sp", ds, kh[k][:], dram["kdram"][h * 128:(h + 1) * 128, 0:NK])
                    half = NKC // 2
                    for c0, n in ((0, half), (half, NKC - half)):
                        R.dma("sp", ds,
                              vh[k][:, c0 * 128:(c0 + n) * 128].rearrange("p (c d) -> p c d", d=128),
                              dram["vdram"][c0 * 128:(c0 + n) * 128, h * 128:(h + 1) * 128]
                              .rearrange("(c p) d -> p c d", p=128))
                    return (ds, R.ct[ds])

                loaded = [("d4", 0)] * (H + 1)
                loaded[0] = load_head(0)
                gctr = 0
                pair_dv = [0, 0]
                sc_bank_ac = [0, 0, 0]
                for h in range(H):
                    if h + 1 < H:
                        loaded[h + 1] = load_head(h + 1)
                    k = h % 2
                    for gi, (t0, W) in enumerate(own_groups):
                        tg = own_start + t0
                        nch = (tg + W + 127) // 128
                        exp_ac = [0] * nch
                        par = gctr % 2
                        gctr += 1
                        sps = PS[4 + 2 * par]
                        aps = PS[5 + 2 * par]
                        grp_dv = pair_dv[par]

                        def emit_sigav(c, t0=t0, W=W, nch=nch, k=k, exp_ac=exp_ac,
                                       grp_dv=grp_dv, sps=sps, aps=aps):
                            pi = c % 3
                            R.wait("pe", "ac", exp_ac[c])
                            if c == 0:
                                R.wait("pe", "dv", grp_dv)
                            R.inc("pe", "pe", lambda e, pi=pi, W=W, c=c, nch=nch, sps=sps: e.matmul(
                                sps[:, 0:W], ones128[:], pt[pi][:, 0:W],
                                start=(c == 0), stop=(c == nch - 1)))
                            R.inc("pe", "pe", lambda e, pi=pi, W=W, c=c, k=k, nch=nch, aps=aps: e.matmul(
                                aps[:, 0:W], vh[k][:, c * 128:(c + 1) * 128], pt[pi][:, 0:W],
                                start=(c == 0), stop=(c == nch - 1)))
                            pt_use_pe[pi] = R.ct["pe"]

                        pend = []
                        for c in range(nch):
                            sb_ = PS[c % 3]
                            R.wait("pe", loaded[h][0], loaded[h][1])
                            R.wait("pe", "ac", sc_bank_ac[c % 3])
                            jd = c - tg // 128
                            diag = 0 <= jd < W // 128
                            R.inc("pe", "pe", lambda e, c=c, t0=t0, W=W, h=h, k=k, sb_=sb_, diag=diag: e.matmul(
                                sb_[:, 0:W], kh[k][:, c * 128:(c + 1) * 128],
                                q_sb[:, h * TOWN + t0: h * TOWN + t0 + W], start=True, stop=not diag))
                            if diag:
                                R.inc("pe", "pe", lambda e, W=W, jd=jd, sb_=sb_: e.matmul(
                                    sb_[:, 0:W], atri[:], bmv[:, jd * 512: jd * 512 + W],
                                    start=False, stop=True))
                            sc_pe = R.ct["pe"]
                            pi = c % 3
                            R.wait("act", "pe", sc_pe)
                            R.wait("act", "pe", pt_use_pe[pi])
                            R.inc("act", "ac", lambda e, pi=pi, W=W, h=h, sb_=sb_: e.activation(
                                out=pt[pi][:, 0:W], in_=sb_[:, 0:W], func=AF.Exp,
                                bias=negBb[:, h:h + 1], scale=1.0))
                            exp_ac[c] = R.ct["ac"]
                            sc_bank_ac[c % 3] = R.ct["ac"]
                            pend.append(c)
                            if len(pend) >= 3:
                                emit_sigav(pend.pop(0))
                        while pend:
                            emit_sigav(pend.pop(0))
                        kv_use_pe[k] = R.ct["pe"]
                        R.wait("dve", "pe", R.ct["pe"])
                        R.inc("dve", "dv", lambda e, W=W, sps=sps, par=par: e.reciprocal(
                            out=rbuf[par][:, 0:W], in_=sps[:, 0:W]))
                        R.inc("dve", "dv", lambda e, h=h, t0=t0, W=W, aps=aps, par=par: e.tensor_mul(
                            o_sb[:, h * TOWN + t0: h * TOWN + t0 + W], aps[:, 0:W], rbuf[par][:, 0:W]))
                        pair_dv[par] = R.ct["dv"]
                load_slab2(0)
                load_slab2(1)
                R.barrier()

            with contextlib.ExitStack() as p5:
                sq2 = [sb(p5, f"sq2{k}", [128, 512], BF16) for k in range(2)]
                ao = nc.alloc_sbuf_tensor_at(pfx + "ao", [128, 16 * TOWN], BF16,
                                             offset=nc.lookup_mloc(q_sb).addr)
                bank_rd2 = [0, 0]
                seq = 0
                for s8 in range(8):
                    kk = s8 % 2
                    if s8 >= 2:
                        load_slab2(s8)
                    sl = slab2_sl[s8]
                    for oc in range(2):
                        o = s8 * 2 + oc
                        for t0, W in own_groups:
                            b = seq % 2
                            seq += 1
                            R.wait("pe", sl[0], sl[1])
                            R.wait("pe", "ac", bank_rd2[b])
                            for i in range(16):
                                R.inc("pe", "pe", lambda e, i=i, oc=oc, t0=t0, W=W, kk=kk, b=b: e.matmul(
                                    PS[b][:, 0:W], slab2[kk][:, i * 256 + oc * 128: i * 256 + oc * 128 + 128],
                                    o_sb[:, i * TOWN + t0: i * TOWN + t0 + W],
                                    start=(i == 0), stop=(i == 15)))
                            R.wait("act", "pe", R.ct["pe"])
                            R.inc("act", "ac", lambda e, o=o, t0=t0, W=W, b=b: e.activation(
                                out=ao[:, o * TOWN + t0: o * TOWN + t0 + W], in_=PS[b][:, 0:W],
                                func=AF.Identity, bias=bo_c[:, o:o + 1], scale=1.0))
                            bank_rd2[b] = R.ct["ac"]
                    slab2_use[kk] = R.ct["pe"]
                lnb_pe = [0, 0]
                first_lnb = True
                ao_done_ac = R.ct["ac"]
                R.wait("dve", "ac", ao_done_ac)
                for t0, W in own_groups:
                    if first_lnb:
                        R.wait("pe", "ac", R.ct["ac"])
                        R.wait("pe", "dv", R.ct["dv"])
                        first_lnb = False
                    for i in range(16):
                        ib = i % 2
                        if ib == 0:
                            R.wait("act", "pe", lnb_pe[0])
                            R.inc("act", "ac", lambda e, i=i, t0=t0, W=W: e.activation(
                                out=sq2[0][:, 0:W], in_=ao[:, i * TOWN + t0: i * TOWN + t0 + W],
                                func=AF.Square, bias=0.0, scale=1.0))
                        else:
                            R.wait("dve", "pe", lnb_pe[1])
                            R.inc("dve", "dv", lambda e, i=i, t0=t0, W=W: e.tensor_mul(
                                sq2[1][:, 0:W], ao[:, i * TOWN + t0: i * TOWN + t0 + W],
                                ao[:, i * TOWN + t0: i * TOWN + t0 + W]))
                        R.wait("pe", "ac", R.ct["ac"])
                        R.wait("pe", "dv", R.ct["dv"])
                        R.inc("pe", "pe", lambda e, i=i, t0=t0, W=W: e.matmul(
                            PS[4][:, 0:W], ones128[:], ao[:, i * TOWN + t0: i * TOWN + t0 + W],
                            start=(i == 0), stop=(i == 15)))
                        R.inc("pe", "pe", lambda e, i=i, W=W, ib=ib: e.matmul(
                            PS[5][:, 0:W], ones128[:], sq2[ib][:, 0:W], start=(i == 0), stop=(i == 15)))
                        lnb_pe[ib] = R.ct["pe"]
                    R.wait("act", "pe", R.ct["pe"])
                    rC, rD = _ln_rows(R, PS[4], PS[5], rows, W, ninv)
                    R.wait("pe", "dv", R.ct["dv"])
                    R.inc("pe", "pe", lambda e, rC=rC, W=W: e.matmul(PS[6][:, 0:W], onesk1[:], rC, start=True, stop=True))
                    R.inc("pe", "pe", lambda e, rD=rD, W=W: e.matmul(PS[7][:, 0:W], onesk1[:], rD, start=True, stop=True))
                    R.wait("dve", "pe", R.ct["pe"])
                    for i in range(16):
                        R.inc("dve", "dv", lambda e, i=i, t0=t0, W=W: e.tensor_mul(
                            tscr[:, 0:W], ao[:, i * TOWN + t0: i * TOWN + t0 + W], PS[6][:, 0:W]))
                        R.inc("dve", "dv", lambda e, W=W: e.tensor_sub(
                            tscr[:, 0:W], tscr[:, 0:W], PS[7][:, 0:W]))
                        R.inc("dve", "dv", lambda e, i=i, t0=t0, W=W: e.tensor_add(
                            x_fm[:, i * TOWN + t0: i * TOWN + t0 + W],
                            x_fm[:, i * TOWN + t0: i * TOWN + t0 + W], tscr[:, 0:W]))
                R.barrier()
    att.close()

    if _STAGE >= 3:
        with contextlib.ExitStack() as p6:
            HTS = ffn_groups[0][1]
            ln2 = sb(p6, "ln2", [128, 16 * HTS], BF16)
            ht = sb(p6, "ht", [128, 64 * HTS], BF16)
            yb = sb(p6, "yb", [128, 16 * HTS], BF16)
            wgs = [sb(p6, f"wgs{k}", [128, 2048], BF16) for k in range(3)]
            wus = [sb(p6, f"wus{k}", [128, 2048], BF16) for k in range(3)]
            wds = [sb(p6, f"wds{k}", [128, 2 * 1024], BF16) for k in range(2)]
            gsb = sb(p6, "gsb", [128, 2 * 512], BF16)
            usb = sb(p6, "usb", [128, 2 * 512], BF16)
            lncb = sb(p6, "lncb", [128, 4 * 512], BF16)
            wg_use, wd_use = [0, 0, 0], [0, 0]
            lnc_pe = [0, 0]
            lnd_pe = [0, 0]
            gu_ac2 = [0, 0]
            gu_dv2 = [0, 0]

            first_lnc = True
            for t0, W in ffn_groups:
                if first_lnc:
                    R.wait("pe", "dv", R.ct["dv"])
                    first_lnc = False
                for i in range(16):
                    ib = i % 2
                    cbs = lncb[:, ib * 512: ib * 512 + W]
                    sqs = lncb[:, (2 + ib) * 512: (2 + ib) * 512 + W]
                    R.wait("act", "pe", lnc_pe[ib])
                    if i < 2:
                        R.wait("act", "dv", R.ct["dv"])
                    R.inc("act", "ac", lambda e, i=i, t0=t0, W=W, cbs=cbs: e.activation(
                        out=cbs[:, 0:W], in_=x_fm[:, i * TOWN + t0: i * TOWN + t0 + W],
                        func=AF.Identity, bias=0.0, scale=1.0))
                    R.wait("dve", "pe", lnc_pe[ib])
                    R.inc("dve", "dv", lambda e, i=i, t0=t0, W=W, sqs=sqs: e.tensor_mul(
                        sqs[:, 0:W], x_fm[:, i * TOWN + t0: i * TOWN + t0 + W],
                        x_fm[:, i * TOWN + t0: i * TOWN + t0 + W]))
                    R.wait("pe", "ac", R.ct["ac"])
                    R.wait("pe", "dv", R.ct["dv"])
                    R.inc("pe", "pe", lambda e, i=i, W=W, cbs=cbs: e.matmul(
                        PS[4][:, 0:W], ones128[:], cbs[:, 0:W], start=(i == 0), stop=(i == 15)))
                    R.inc("pe", "pe", lambda e, i=i, W=W, sqs=sqs: e.matmul(
                        PS[5][:, 0:W], ones128[:], sqs[:, 0:W], start=(i == 0), stop=(i == 15)))
                    lnc_pe[ib] = R.ct["pe"]
                R.wait("act", "pe", R.ct["pe"])
                rC, rD = _ln_rows(R, PS[4], PS[5], rows, W, ninv)
                R.wait("pe", "dv", R.ct["dv"])
                R.inc("pe", "pe", lambda e, rC=rC, W=W: e.matmul(PS[6][:, 0:W], onesk1[:], rC, start=True, stop=True))
                R.inc("pe", "pe", lambda e, rD=rD, W=W: e.matmul(PS[7][:, 0:W], onesk1[:], rD, start=True, stop=True))
                R.wait("dve", "pe", R.ct["pe"])
                for i in range(16):
                    R.inc("dve", "dv", lambda e, i=i, t0=t0, W=W: e.tensor_mul(
                        tscr[:, 0:W], x_fm[:, i * TOWN + t0: i * TOWN + t0 + W], PS[6][:, 0:W]))
                    R.inc("dve", "dv", lambda e, i=i, W=W: e.tensor_sub(
                        ln2[:, i * HTS: i * HTS + W], tscr[:, 0:W], PS[7][:, 0:W]))
                ln2_dv = R.ct["dv"]

                gu_ac = gu_ac2
                gu_dv = gu_dv2
                for ib in range(64):
                    k = ib % 3
                    ds = ("e0", "e1", "e2")[k]
                    pb = (ib % 2) * 2
                    R.wait("pool", ds, R.ct[ds])
                    R.wait("pool", "pe", wg_use[k])
                    R.dma("pool", ds, wgs[k][:], dram["wgt"][ib, :, :])
                    R.dma("pool", ds, wus[k][:], dram["wut"][ib, :, :])
                    gqc = R.ct[ds]
                    R.wait("pe", ds, gqc)
                    R.wait("pe", "ac", gu_ac[ib % 2])
                    if ib == 0:
                        R.wait("pe", "dv", ln2_dv)
                    for i in range(16):
                        R.inc("pe", "pe", lambda e, i=i, k=k, W=W, pb=pb: e.matmul(
                            PS[pb][:, 0:W], wgs[k][:, i * 128:(i + 1) * 128],
                            ln2[:, i * HTS: i * HTS + W], start=(i == 0), stop=(i == 15)))
                    for i in range(16):
                        R.inc("pe", "pe", lambda e, i=i, k=k, W=W, pb=pb: e.matmul(
                            PS[pb + 1][:, 0:W], wus[k][:, i * 128:(i + 1) * 128],
                            ln2[:, i * HTS: i * HTS + W], start=(i == 0), stop=(i == 15)))
                    wg_use[k] = R.ct["pe"]
                    gp = (ib % 2) * 512
                    R.wait("act", "pe", R.ct["pe"])
                    R.wait("act", "dv", gu_dv[ib % 2])
                    R.inc("act", "ac", lambda e, ib=ib, W=W, pb=pb, gp=gp: e.activation(
                        out=gsb[:, gp:gp + W], in_=PS[pb][:, 0:W], func=AF.Silu,
                        bias=0.0, scale=sg_c[:, ib:ib + 1]))
                    R.inc("act", "ac", lambda e, ib=ib, W=W, pb=pb, gp=gp: e.activation(
                        out=usb[:, gp:gp + W], in_=PS[pb + 1][:, 0:W], func=AF.Identity,
                        bias=0.0, scale=su_c[:, ib:ib + 1]))
                    gu_ac[ib % 2] = R.ct["ac"]
                    R.wait("dve", "ac", R.ct["ac"])
                    R.inc("dve", "dv", lambda e, ib=ib, W=W, gp=gp: e.tensor_mul(
                        ht[:, ib * HTS: ib * HTS + W], gsb[:, gp:gp + W], usb[:, gp:gp + W]))
                    gu_dv[ib % 2] = R.ct["dv"]
                ht_dv = R.ct["dv"]

                for oh in range(2):
                    for ib2 in range(32):
                        k = ib2 % 2
                        ds = ("e3", "e4")[k]
                        R.wait("pool", ds, R.ct[ds])
                        R.wait("pool", "pe", wd_use[k])
                        R.dma("pool", ds,
                              wds[k][:].rearrange("p (c o) -> p c o", o=1024),
                              dram["wdt"][ib2, :, :].rearrange("p (c o) -> p c o", o=2048)
                              [:, :, oh * 1024:(oh + 1) * 1024])
                        gqc = R.ct[ds]
                        R.wait("pe", ds, gqc)
                        if ib2 == 0:
                            R.wait("pe", "ac", R.ct["ac"])
                            R.wait("pe", "dv", ht_dv)
                        for o8 in range(8):
                            for c in range(2):
                                R.inc("pe", "pe", lambda e, o8=o8, c=c, ib2=ib2, k=k, W=W: e.matmul(
                                    PS[o8][:, 0:W], wds[k][:, c * 1024 + o8 * 128: c * 1024 + o8 * 128 + 128],
                                    ht[:, (ib2 * 2 + c) * HTS:(ib2 * 2 + c) * HTS + W],
                                    start=(ib2 == 0 and c == 0), stop=(ib2 == 31 and c == 1)))
                        wd_use[k] = R.ct["pe"]
                    R.wait("act", "pe", R.ct["pe"])
                    for o8 in range(8):
                        o = oh * 8 + o8
                        R.inc("act", "ac", lambda e, o=o, o8=o8, W=W: e.activation(
                            out=yb[:, o * HTS: o * HTS + W], in_=PS[o8][:, 0:W],
                            func=AF.Identity, bias=0.0, scale=sd_c[:, o:o + 1]))

                R.wait("dve", "ac", R.ct["ac"])
                for i in range(16):
                    ib = i % 2
                    sqs = ln2[:, ib * HTS: ib * HTS + W]
                    if ib == 0:
                        R.wait("act", "pe", lnd_pe[0])
                        R.inc("act", "ac", lambda e, i=i, W=W, sqs=sqs: e.activation(
                            out=sqs[:, 0:W], in_=yb[:, i * HTS: i * HTS + W],
                            func=AF.Square, bias=0.0, scale=1.0))
                    else:
                        R.wait("dve", "pe", lnd_pe[1])
                        R.inc("dve", "dv", lambda e, i=i, W=W, sqs=sqs: e.tensor_mul(
                            sqs[:, 0:W], yb[:, i * HTS: i * HTS + W], yb[:, i * HTS: i * HTS + W]))
                    R.wait("pe", "ac", R.ct["ac"])
                    R.wait("pe", "dv", R.ct["dv"])
                    R.inc("pe", "pe", lambda e, i=i, W=W: e.matmul(
                        PS[4][:, 0:W], ones128[:], yb[:, i * HTS: i * HTS + W], start=(i == 0), stop=(i == 15)))
                    R.inc("pe", "pe", lambda e, i=i, W=W, sqs=sqs: e.matmul(
                        PS[5][:, 0:W], ones128[:], sqs[:, 0:W], start=(i == 0), stop=(i == 15)))
                    lnd_pe[ib] = R.ct["pe"]
                R.wait("act", "pe", R.ct["pe"])
                rC, rD = _ln_rows(R, PS[4], PS[5], rows, W, ninv)
                R.wait("pe", "dv", R.ct["dv"])
                R.inc("pe", "pe", lambda e, rC=rC, W=W: e.matmul(PS[6][:, 0:W], onesk1[:], rC, start=True, stop=True))
                R.inc("pe", "pe", lambda e, rD=rD, W=W: e.matmul(PS[7][:, 0:W], onesk1[:], rD, start=True, stop=True))
                R.wait("dve", "pe", R.ct["pe"])
                for i in range(16):
                    R.inc("dve", "dv", lambda e, i=i, W=W: e.tensor_mul(
                        tscr[:, 0:W], yb[:, i * HTS: i * HTS + W], PS[6][:, 0:W]))
                    R.inc("dve", "dv", lambda e, W=W: e.tensor_sub(
                        tscr[:, 0:W], tscr[:, 0:W], PS[7][:, 0:W]))
                    R.inc("dve", "dv", lambda e, i=i, t0=t0, W=W: e.tensor_add(
                        x_fm[:, i * TOWN + t0: i * TOWN + t0 + W],
                        x_fm[:, i * TOWN + t0: i * TOWN + t0 + W], tscr[:, 0:W]))

                R.wait("sp", "dv", R.ct["dv"])
                for i in range(16):
                    R.dma("sp", "st", dram["yout"][i * 128:(i + 1) * 128, t0:t0 + W],
                          x_fm[:, i * TOWN + t0: i * TOWN + t0 + W])

    R.wait("sp", "st", R.ct["st"])
    return R


def build_nc():
    nc = bass.Bass("TRN2", target_bir_lowering=False)
    dram = {}
    dram["xin"] = nc.dram_tensor("xin", [S, D], F32, kind="ExternalInput").ap()
    for nm in ("wq", "wk", "wv"):
        dram[nm] = nc.dram_tensor(nm, [4, 128, 16, 512], F32, kind="ExternalInput").ap()
    dram["wo"] = nc.dram_tensor("wo", [8, 128, 16, 256], F32, kind="ExternalInput").ap()
    for nm in ("bq", "bk", "bo"):
        dram[nm] = nc.dram_tensor(nm, [128, 16], F32, kind="ExternalInput").ap()
    dram["bv"] = nc.dram_tensor("bv", [1, D], F32, kind="ExternalInput").ap()
    dram["wgt"] = nc.dram_tensor("wgt", [64, 128, 2048], I8, kind="ExternalInput").ap()
    dram["wut"] = nc.dram_tensor("wut", [64, 128, 2048], I8, kind="ExternalInput").ap()
    dram["wdt"] = nc.dram_tensor("wdt", [32, 128, 4096], I8, kind="ExternalInput").ap()
    dram["sg"] = nc.dram_tensor("sg", [128, 64], F32, kind="ExternalInput").ap()
    dram["su"] = nc.dram_tensor("su", [128, 64], F32, kind="ExternalInput").ap()
    dram["sd"] = nc.dram_tensor("sd", [128, 16], F32, kind="ExternalInput").ap()
    dram["yout"] = nc.dram_tensor("yout", [D, P_SPLIT], F32, kind="ExternalOutput").ap()
    dram["kdram"] = nc.dram_tensor("kdram", [D, S], BF16, kind="Internal").ap()
    dram["vdram"] = nc.dram_tensor("vdram", [S, D], BF16, kind="Internal").ap()

    es = contextlib.ExitStack()
    with es:
        sems = {s: es.enter_context(nc.semaphore(s)) for s in SEMS}
        with contextlib.ExitStack() as esA:
            roleA = build_role(nc, sems, esA, 0, P_SPLIT, P_SPLIT, dram, "A_")
        with contextlib.ExitStack() as esB:
            roleB = build_role(nc, sems, esB, P_SPLIT, S, S, dram, "B_")

        blk = es.enter_context(nc.Block())
        eng_map = {"pe": blk.tensor, "act": blk.scalar, "dve": blk.vector,
                   "pool": blk.gpsimd, "sp": blk.sync}
        for key in ENGS:
            def body(e, key=key):
                pid = e.partition_id()
                with e.If(pid < 4):
                    for f in roleA.q[key]:
                        f(e)
                with e.Else():
                    for f in roleB.q[key]:
                        f(e)
            eng_map[key](body)
    return nc


_NC_CACHE = {}


def _get_nc():
    if _STAGE not in _NC_CACHE:
        _NC_CACHE[_STAGE] = build_nc()
    return _NC_CACHE[_STAGE]


def _tile_ffn(wg, wu, wd):
    wgt = np.ascontiguousarray(
        wg.reshape(16, 128, 64, 128).transpose(2, 1, 0, 3).reshape(64, 128, 2048))
    wut = np.ascontiguousarray(
        wu.reshape(16, 128, 64, 128).transpose(2, 1, 0, 3).reshape(64, 128, 2048))
    wdt = np.ascontiguousarray(
        wd.reshape(32, 2, 128, 2048).transpose(0, 2, 1, 3).reshape(32, 128, 4096))
    return wgt, wut, wdt


def kernel(L0_attn_X, L0_attn_Wq, L0_attn_bq, L0_attn_Wk, L0_attn_bk,
           L0_attn_Wv, L0_attn_bv, L0_attn_Wo, L0_attn_bo,
           L0_ffn_w_gate, L0_ffn_w_up, L0_ffn_w_down,
           L0_ffn_s_gate, L0_ffn_s_up, L0_ffn_s_down):
    nc = _get_nc()
    X = np.ascontiguousarray(np.asarray(L0_attn_X, dtype=np.float32))
    wgt, wut, wdt = _tile_ffn(np.asarray(L0_ffn_w_gate, np.int8),
                              np.asarray(L0_ffn_w_up, np.int8),
                              np.asarray(L0_ffn_w_down, np.int8))
    col = lambda v, n: np.ascontiguousarray(np.asarray(v, np.float32).reshape(n, 128).T)
    tile = lambda w, ow: np.ascontiguousarray(
        np.asarray(w, np.float32).T.reshape(16, 128, D // ow, ow).transpose(2, 1, 0, 3))
    common = {
        "wq": tile(L0_attn_Wq, 512),
        "wk": tile(L0_attn_Wk, 512),
        "wv": tile(L0_attn_Wv, 512),
        "wo": tile(L0_attn_Wo, 256),
        "bq": col(L0_attn_bq, 16),
        "bk": col(L0_attn_bk, 16),
        "bv": np.asarray(L0_attn_bv, np.float32).reshape(1, D),
        "bo": col(L0_attn_bo, 16),
        "wgt": wgt, "wut": wut, "wdt": wdt,
        "sg": col(L0_ffn_s_gate, 64),
        "su": col(L0_ffn_s_up, 64),
        "sd": col(L0_ffn_s_down, 16),
    }
    in_maps = []
    for c in range(8):
        m = dict(common)
        m["xin"] = np.ascontiguousarray(X[c % 4].T)
        in_maps.append(m)
    res = run_bass_kernel_spmd(nc, in_maps, core_ids=list(range(8)))
    out = np.empty((4, S, D), np.float32)
    for b in range(4):
        out[b, :P_SPLIT] = res.results[b]["yout"][:, :P_SPLIT].T
        out[b, P_SPLIT:] = res.results[4 + b]["yout"][:, :S - P_SPLIT].T
    return out


# version 19
# speedup vs baseline: 1.0567x; 1.0567x over previous
import contextlib
import numpy as np

import concourse.bass as bass
import concourse.mybir as mybir
from concourse.bass_utils import run_bass_kernel_spmd

F32 = mybir.dt.float32
BF16 = mybir.dt.bfloat16
I8 = mybir.dt.int8
AF = mybir.ActivationFunctionType
ALU = mybir.AluOpType
AX = mybir.AxisListType

D = 2048
S = 2048
H = 16
INTER = 8192
P_SPLIT = 1152
CSC = float(128.0 ** -0.25)
EPS = 1e-5

ENGS = ("pe", "act", "dve", "pool", "sp")
SEMS = ("sp", "aq", "gq", "st", "pe", "ac", "dv", "pl",
        "d0", "d1", "d2", "d3", "d4", "d5", "d6", "d7", "d8", "d9", "da", "db",
        "e0", "e1", "e2", "e3", "e4", "e5")
SEM_OWNER = {"pe": "pe", "ac": "act", "dv": "dve", "pl": "pool"}

_STAGE = 3


FENCED = {"act": "ac", "dve": "dv", "pool": "pl"}


class RoleBuild:

    def __init__(self, nc, sems):
        self.nc = nc
        self.sems = sems
        self.q = {k: [] for k in ENGS}
        self.ct = {k: 0 for k in SEMS}

    def do(self, eng, fn):
        if eng in FENCED:
            return self.inc(eng, FENCED[eng], fn)
        self.q[eng].append(fn)

    def wait(self, eng, sem, val):
        if val > 0:
            h = self.sems[sem]
            self.q[eng].append(lambda e, h=h, v=val: e.wait_ge(h, v))

    def inc(self, eng, sem, fn, n=1):
        if eng in FENCED:
            own = FENCED[eng]
            self.wait(eng, own, self.ct[own])
        h = self.sems[sem]
        self.ct[sem] += n
        self.q[eng].append(lambda e, fn=fn, h=h, n=n: fn(e).then_inc(h, n))
        return self.ct[sem]

    def dma(self, eng, sem, out, in_, **kw):
        return self.inc(eng, sem,
                        lambda e, o=out, i=in_, kw=kw: e.dma_start(out=o, in_=i, **kw),
                        n=16)

    def barrier(self, engs=ENGS):
        snap = dict(self.ct)
        for eng in engs:
            for s, v in snap.items():
                if v > 0 and SEM_OWNER.get(s) != eng:
                    self.wait(eng, s, v)


def _ln_rows(R, PSsum, PSsq, rows, W, ninv):
    rA = rows[0:1, 0:W]
    rB = rows[0:1, 512:512 + W]
    rC = rows[0:1, 1024:1024 + W]
    rD = rows[0:1, 1536:1536 + W]
    R.inc("act", "ac", lambda e: e.activation(out=rA, in_=PSsum[0:1, 0:W], func=AF.Identity, bias=0.0, scale=ninv))
    R.inc("act", "ac", lambda e: e.activation(out=rC, in_=PSsq[0:1, 0:W], func=AF.Identity, bias=0.0, scale=ninv))
    R.wait("dve", "ac", R.ct["ac"])
    R.inc("dve", "dv", lambda e: e.tensor_mul(rB, rA, rA))
    R.inc("dve", "dv", lambda e: e.scalar_tensor_tensor(out=rD, in0=rC, scalar=EPS, in1=rB, op0=ALU.add, op1=ALU.subtract))
    R.wait("act", "dv", R.ct["dv"])
    R.inc("act", "ac", lambda e: e.activation(out=rB, in_=rD, func=AF.Sqrt, bias=0.0, scale=1.0))
    R.wait("dve", "ac", R.ct["ac"])
    R.inc("dve", "dv", lambda e: e.reciprocal(out=rC, in_=rB))
    R.inc("dve", "dv", lambda e: e.tensor_mul(rD, rA, rC))
    return rC, rD


def _grp(total, width):
    out, o = [], 0
    while o < total:
        out.append((o, min(width, total - o)))
        o += min(width, total - o)
    return out


def build_role(nc, sems, es, own_start, own_end, nkeys, dram, pfx):
    R = RoleBuild(nc, sems)
    TOWN = own_end - own_start
    NK = nkeys
    NKC = NK // 128
    NOT = TOWN // 128
    ninv = 1.0 / 2048.0

    key_groups = _grp(NK, 512)
    own_groups = _grp(TOWN, 512)
    ffn_groups = _grp(TOWN, 384) if TOWN == 1152 else _grp(TOWN, 512)
    if TOWN == 1152:
        P1G = [(0, 512), (512, 512), (1024, 128)]
    else:
        P1G = [(0, 256), (256, 256), (512, 256), (768, 256), (1024, 128),
               (1152, 256), (1408, 256), (1664, 256), (1920, 128)]
    sb = lambda st, name, shape, dt: st.enter_context(nc.sbuf_tensor(pfx + name, shape, dt))
    onesk1 = sb(es, "onesk1", [1, 128], F32)
    onescol = sb(es, "onescol", [128, 1], BF16)
    ones128 = sb(es, "ones128", [128, 128], BF16)
    bo_c = sb(es, "bo_c", [128, 16], F32)
    sd_c = sb(es, "sd_c", [128, 16], F32)
    sg_c = sb(es, "sg_c", [128, 64], F32)
    su_c = sb(es, "su_c", [128, 64], F32)
    rows = sb(es, "rows", [1, 2048], F32)
    tscr = sb(es, "tscr", [128, 512], F32)
    x_fm = sb(es, "x_fm", [128, 16 * TOWN], F32)
    PS = [es.enter_context(nc.psum_tensor(f"{pfx}ps{i}", [128, 512], F32)) for i in range(8)]

    att = contextlib.ExitStack()
    atri = sb(att, "atri", [128, 128], BF16)
    bmv = sb(att, "bmv", [128, 4 * 512], BF16)
    bq_c = sb(att, "bq_c", [128, 16], F32)
    bk_c = sb(att, "bk_c", [128, 16], F32)
    negBb = sb(att, "negBb", [128, 16], F32)
    knmax = sb(att, "knmax", [1, 16], F32)
    qnmax = sb(att, "qnmax", [1, 16], F32)
    q_sb = sb(att, "q_sb", [128, 16 * TOWN], BF16)

    def pool_step(fn):
        c = R.inc("pool", "pl", fn)
        R.wait("pool", "pl", c)

    R.do("pool", lambda e: e.memset(onesk1[:], 1.0))
    R.do("pool", lambda e: e.memset(onescol[:], 1.0))
    R.do("pool", lambda e: e.memset(ones128[:], 1.0))
    R.do("pool", lambda e: e.memset(knmax[:], 0.0))
    R.do("pool", lambda e: e.memset(qnmax[:], 0.0))
    pool_step(lambda e: e.memset(atri[:], 1.0))
    pool_step(lambda e: e.affine_select(out=atri[:], in_=atri[:], compare_op=ALU.is_ge,
                                        fill=0.0, base=0, pattern=[[1, 128]], channel_multiplier=-1))
    for jd in range(4):
        if jd > 0:
            R.do("pool", lambda e, jd=jd: e.memset(bmv[:, jd * 512: jd * 512 + jd * 128], -1e9))
        pool_step(lambda e, jd=jd: e.memset(bmv[:, jd * 512 + jd * 128: jd * 512 + jd * 128 + 128], -1e9))
        pool_step(lambda e, jd=jd: e.affine_select(
            out=bmv[:, jd * 512 + jd * 128: jd * 512 + jd * 128 + 128],
            in_=bmv[:, jd * 512 + jd * 128: jd * 512 + jd * 128 + 128],
            compare_op=ALU.is_ge, fill=0.0, base=-1, pattern=[[-1, 128]], channel_multiplier=1))
        if jd < 3:
            R.do("pool", lambda e, jd=jd: e.memset(bmv[:, jd * 512 + (jd + 1) * 128: (jd + 1) * 512], 0.0))


    R.dma("act", "aq", bq_c[:, :], dram["bq"])
    R.dma("act", "aq", bk_c[:, :], dram["bk"])
    R.dma("act", "aq", bo_c[:, :], dram["bo"])
    R.dma("act", "aq", sd_c[:, :], dram["sd"])
    R.dma("act", "aq", sg_c[:, :], dram["sg"])
    R.dma("act", "aq", su_c[:, :], dram["su"])
    R.dma("act", "aq", rows[0:1, :], dram["bv"])
    R.wait("act", "aq", R.ct["aq"])
    R.inc("act", "ac", lambda e: e.activation(out=bq_c[:, :], in_=bq_c[:, :], func=AF.Identity, bias=0.0, scale=CSC))
    R.inc("act", "ac", lambda e: e.activation(out=bk_c[:, :], in_=bk_c[:, :], func=AF.Identity, bias=0.0, scale=CSC))
    R.barrier()

    with contextlib.ExitStack() as p1:
        ln1 = sb(p1, "ln1", [128, 16 * NK], BF16)
        bvb = sb(p1, "bvb", [128, 2048], BF16)

        for o4 in range(4):
            R.wait("pe", "ac", R.ct["ac"])
            R.inc("pe", "pe", lambda e, o4=o4: e.matmul(
                PS[1 + (o4 % 2)][:, 0:512], onesk1[:], rows[0:1, o4 * 512:(o4 + 1) * 512], start=True, stop=True))
            R.wait("act", "pe", R.ct["pe"])
            R.inc("act", "ac", lambda e, o4=o4: e.activation(
                out=bvb[:, o4 * 512:(o4 + 1) * 512], in_=PS[1 + (o4 % 2)][:, 0:512],
                func=AF.Identity, bias=0.0, scale=1.0))

        with contextlib.ExitStack() as p1a:
            xfg = [sb(p1a, f"xfg{k}", [128, 16 * 256], F32) for k in range(2)]
            cb = [sb(p1a, f"cb{k}", [128, 512], BF16) for k in range(2)]
            sq = [sb(p1a, f"sq{k}", [128, 512], BF16) for k in range(2)]
            fg_dv = [0, 0]

            p1_groups = P1G
            fgi = 0
            mm_pe_after = [0, 0]
            for gidx, (g0, gw) in enumerate(p1_groups):
                own = g0 >= own_start
                if not own:
                    fk = fgi % 2
                    fgi += 1

                def xwin(i, own=own, g0=g0, gw=gw, fk=(fgi - 1) % 2):
                    if own:
                        o = g0 - own_start
                        return x_fm[:, i * TOWN + o: i * TOWN + o + gw]
                    return xfg[fk][:, i * 256: i * 256 + gw]

                ds = "d0" if gidx % 2 == 0 else "d1"
                R.wait("sp", ds, R.ct[ds])
                if not own:
                    R.wait("sp", "dv", fg_dv[fk])
                for i in range(16):
                    R.dma("sp", ds, xwin(i), dram["xin"][i * 128:(i + 1) * 128, g0:g0 + gw])
                ld_sp = (ds, R.ct[ds])
                if _STAGE < 2:
                    continue
                R.wait("act", ld_sp[0], ld_sp[1])
                R.wait("dve", ld_sp[0], ld_sp[1])
                for i in range(16):
                    ib = i % 2
                    R.wait("act", "pe", mm_pe_after[ib])
                    R.inc("act", "ac", lambda e, i=i, f=xwin, ib=ib, gw=gw: e.activation(
                        out=cb[ib][:, 0:gw], in_=f(i), func=AF.Identity, bias=0.0, scale=1.0))
                    R.wait("dve", "pe", mm_pe_after[ib])
                    R.inc("dve", "dv", lambda e, i=i, f=xwin, ib=ib, gw=gw: e.tensor_mul(
                        sq[ib][:, 0:gw], f(i), f(i)))
                    R.wait("pe", "ac", R.ct["ac"])
                    R.wait("pe", "dv", R.ct["dv"])
                    R.inc("pe", "pe", lambda e, i=i, ib=ib, gw=gw: e.matmul(
                        PS[4][:, 0:gw], ones128[:], cb[ib][:, 0:gw], start=(i == 0), stop=(i == 15)))
                    R.inc("pe", "pe", lambda e, i=i, ib=ib, gw=gw: e.matmul(
                        PS[5][:, 0:gw], ones128[:], sq[ib][:, 0:gw], start=(i == 0), stop=(i == 15)))
                    mm_pe_after[ib] = R.ct["pe"]
                R.wait("act", "pe", R.ct["pe"])
                rC, rD = _ln_rows(R, PS[4], PS[5], rows, gw, ninv)
                R.wait("pe", "dv", R.ct["dv"])
                R.inc("pe", "pe", lambda e, rC=rC, gw=gw: e.matmul(PS[6][:, 0:gw], onesk1[:], rC, start=True, stop=True))
                R.inc("pe", "pe", lambda e, rD=rD, gw=gw: e.matmul(PS[7][:, 0:gw], onesk1[:], rD, start=True, stop=True))
                R.wait("dve", "pe", R.ct["pe"])
                for i in range(16):
                    R.inc("dve", "dv", lambda e, i=i, f=xwin, gw=gw: e.tensor_mul(
                        tscr[:, 0:gw], f(i), PS[6][:, 0:gw]))
                    R.inc("dve", "dv", lambda e, i=i, g0=g0, gw=gw: e.tensor_sub(
                        ln1[:, i * NK + g0: i * NK + g0 + gw], tscr[:, 0:gw], PS[7][:, 0:gw]))
                if not own:
                    fg_dv[fk] = R.ct["dv"]
            R.barrier()

        if _STAGE >= 2:
            with contextlib.ExitStack() as p1b:
                slab = [sb(p1b, f"slab{k}", [128, 16 * 512], BF16) for k in range(2)]
                ksb = [sb(p1b, f"ksb{k}", [128, 512], BF16) for k in range(2)]
                vsb = [sb(p1b, f"vsb{k}", [128, 512], BF16) for k in range(2)]
                nsc = sb(p1b, "nsc", [128, 512], BF16)
                nsc2 = sb(p1b, "nsc2", [128, 512], BF16)
                slab_use_pe = [0, 0]

                def load_slab(k, mat, s4):
                    ds = "e0" if k == 0 else "e1"
                    R.wait("pool", ds, R.ct[ds])
                    R.wait("pool", "pe", slab_use_pe[k])
                    R.dma("pool", ds, slab[k][:].rearrange("p (c o) -> p c o", o=512),
                          dram[mat][s4])
                    return (ds, R.ct[ds])

                proj_rd = [("ac", 0), ("ac", 0)]
                bank_dv = [0, 0, 0, 0]
                nrm_scr_pe = [0, 0]
                nb = 0
                pend_norm = []

                def emit_norm(ent):
                    scr, W, o, dst, bk4, sq_ac = ent
                    R.wait("pe", "ac", sq_ac)
                    R.wait("pe", "dv", bank_dv[bk4])
                    R.inc("pe", "pe", lambda e, scr=scr, W=W, bk4=bk4: e.matmul(
                        PS[4 + bk4][:, 0:W], ones128[:], scr[:, 0:W], start=True, stop=True))
                    nrm_scr_pe[0 if scr is nsc else 1] = R.ct["pe"]
                    R.wait("dve", "pe", R.ct["pe"])
                    R.inc("dve", "dv", lambda e, W=W, bk4=bk4: e.tensor_reduce(
                        out=rows[0:1, 504 + bk4:505 + bk4], in_=PS[4 + bk4][0:1, 0:W], axis=AX.X, op=ALU.max))
                    R.inc("dve", "dv", lambda e, o=o, bk4=bk4, dst=dst: e.tensor_tensor(
                        out=dst[0:1, o:o + 1], in0=dst[0:1, o:o + 1],
                        in1=rows[0:1, 504 + bk4:505 + bk4], op=ALU.max))
                    bank_dv[bk4] = R.ct["dv"]

                for mat, tiles, qbase, is_q in (("wk", key_groups, 0, False),
                                                ("wq", own_groups, own_start, True)):
                    seq = 0
                    for s4 in range(4):
                        sl = load_slab(s4 % 2, mat, s4)
                        for oc in range(4):
                            o = s4 * 4 + oc
                            for t0, W in tiles:
                                b = seq % 2
                                seq += 1
                                R.wait("pe", sl[0], sl[1])
                                R.wait("pe", proj_rd[b][0], proj_rd[b][1])
                                for i in range(16):
                                    R.inc("pe", "pe", lambda e, i=i, oc=oc, t0=t0, W=W, k=s4 % 2, qb=qbase, b=b: e.matmul(
                                        PS[b][:, 0:W], slab[k][:, i * 512 + oc * 128: i * 512 + oc * 128 + 128],
                                        ln1[:, i * NK + qb + t0: i * NK + qb + t0 + W],
                                        start=(i == 0), stop=(i == 15)))
                                slab_use_pe[s4 % 2] = R.ct["pe"]
                                if pend_norm:
                                    emit_norm(pend_norm.pop(0))
                                R.wait("act", "pe", R.ct["pe"])
                                scr = nsc if b == 0 else nsc2
                                if is_q:
                                    R.inc("act", "ac", lambda e, o=o, t0=t0, W=W, b=b: e.activation(
                                        out=q_sb[:, o * TOWN + t0: o * TOWN + t0 + W], in_=PS[b][:, 0:W],
                                        func=AF.Identity, bias=bq_c[:, o:o + 1], scale=CSC))
                                    proj_rd[b] = ("ac", R.ct["ac"])
                                    R.wait("act", "pe", nrm_scr_pe[b])
                                    R.inc("act", "ac", lambda e, o=o, t0=t0, W=W, scr=scr: e.activation(
                                        out=scr[:, 0:W], in_=q_sb[:, o * TOWN + t0: o * TOWN + t0 + W],
                                        func=AF.Square, bias=0.0, scale=1.0))
                                    pend_norm.append((scr, W, o, qnmax, nb % 4, R.ct["ac"]))
                                else:
                                    kbuf = ksb[b]
                                    dsb = "da" if b == 0 else "db"
                                    R.wait("act", dsb, R.ct[dsb])
                                    R.inc("act", "ac", lambda e, o=o, W=W, b=b, kbuf=kbuf: e.activation(
                                        out=kbuf[:, 0:W], in_=PS[b][:, 0:W],
                                        func=AF.Identity, bias=bk_c[:, o:o + 1], scale=CSC))
                                    proj_rd[b] = ("ac", R.ct["ac"])
                                    R.dma("act", dsb, dram["kdram"][o * 128:(o + 1) * 128, t0:t0 + W], kbuf[:, 0:W])
                                    R.wait("act", "pe", nrm_scr_pe[b])
                                    R.inc("act", "ac", lambda e, W=W, kbuf=kbuf, scr=scr: e.activation(
                                        out=scr[:, 0:W], in_=kbuf[:, 0:W],
                                        func=AF.Square, bias=0.0, scale=1.0))
                                    pend_norm.append((scr, W, o, knmax, nb % 4, R.ct["ac"]))
                                nb += 1
                    while pend_norm:
                        emit_norm(pend_norm.pop(0))

                seq = 0
                for s4 in range(4):
                    sl = load_slab(s4 % 2, "wv", s4)
                    for tt in range(NKC):
                        b = seq % 2
                        seq += 1
                        R.wait("pe", sl[0], sl[1])
                        R.wait("pe", proj_rd[b][0], proj_rd[b][1])
                        for i in range(16):
                            R.inc("pe", "pe", lambda e, i=i, tt=tt, k=s4 % 2, b=b: e.matmul(
                                PS[b][:, 0:512], ln1[:, i * NK + tt * 128: i * NK + tt * 128 + 128],
                                slab[k][:, i * 512:(i + 1) * 512], start=(i == 0), stop=(i == 15)))
                        slab_use_pe[s4 % 2] = R.ct["pe"]
                        dsb = "da" if b == 0 else "db"
                        R.wait("dve", "pe", R.ct["pe"])
                        R.wait("dve", dsb, R.ct[dsb])
                        R.inc("dve", "dv", lambda e, s4=s4, b=b: e.tensor_add(
                            vsb[b][:], PS[b][:, 0:512], bvb[:, s4 * 512:(s4 + 1) * 512]))
                        proj_rd[b] = ("dv", R.ct["dv"])
                        R.wait("act", "dv", R.ct["dv"])
                        R.dma("act", dsb, dram["vdram"][tt * 128:(tt + 1) * 128, s4 * 512:(s4 + 1) * 512], vsb[b][:])

                R.barrier()

    if _STAGE >= 2:
        rT = rows[0:1, 0:16]
        R.inc("dve", "dv", lambda e: e.tensor_mul(rT, qnmax[0:1, :], knmax[0:1, :]))
        R.wait("act", "dv", R.ct["dv"])
        R.inc("act", "ac", lambda e: e.activation(out=rT, in_=rT, func=AF.Sqrt, bias=0.0, scale=1.0))
        R.wait("dve", "ac", R.ct["ac"])
        R.inc("dve", "dv", lambda e: e.tensor_scalar_mul(rT, rT, -1.0))
        R.wait("pe", "dv", R.ct["dv"])
        R.inc("pe", "pe", lambda e: e.matmul(PS[6][:, 0:16], onesk1[:], rT, start=True, stop=True))
        R.wait("act", "pe", R.ct["pe"])
        R.inc("act", "ac", lambda e: e.activation(out=negBb[:], in_=PS[6][:, 0:16],
                                                  func=AF.Identity, bias=0.0, scale=1.0))
        R.barrier()

        with contextlib.ExitStack() as p4:
            o_sb = sb(p4, "o_sb", [128, 16 * TOWN], BF16)
            slab2 = [sb(p4, f"slb2{k}", [128, 16 * 256], BF16) for k in range(2)]
            slab2_use = [0, 0]
            slab2_sl = [("e0", 0)] * 8

            def load_slab2(s8):
                kk = s8 % 2
                ds = "e0" if kk == 0 else "e1"
                R.wait("pool", ds, R.ct[ds])
                R.wait("pool", "pe", slab2_use[kk])
                R.dma("pool", ds, slab2[kk][:].rearrange("p (c o) -> p c o", o=256),
                      dram["wo"][s8])
                slab2_sl[s8] = (ds, R.ct[ds])

            with contextlib.ExitStack() as p4b:
                kh = [sb(p4b, f"kh{k}", [128, NK], BF16) for k in range(2)]
                vh = [sb(p4b, f"vh{k}", [128, NK], BF16) for k in range(2)]
                pt = [sb(p4b, f"pt{k}", [128, 512], BF16) for k in range(3)]
                rbuf = [sb(p4b, f"rbuf{k}", [128, 512], F32) for k in range(2)]
                kv_use_pe = [0, 0]
                pt_use_pe = [0, 0, 0]

                def load_head(h):
                    k = h % 2
                    ds = "d4" if k == 0 else "d5"
                    R.wait("sp", ds, R.ct[ds])
                    R.wait("sp", "pe", kv_use_pe[k])
                    R.dma("sp", ds, kh[k][:], dram["kdram"][h * 128:(h + 1) * 128, 0:NK])
                    half = NKC // 2
                    for c0, n in ((0, half), (half, NKC - half)):
                        R.dma("sp", ds,
                              vh[k][:, c0 * 128:(c0 + n) * 128].rearrange("p (c d) -> p c d", d=128),
                              dram["vdram"][c0 * 128:(c0 + n) * 128, h * 128:(h + 1) * 128]
                              .rearrange("(c p) d -> p c d", p=128))
                    return (ds, R.ct[ds])

                loaded = [("d4", 0)] * (H + 1)
                loaded[0] = load_head(0)
                gctr = 0
                pair_dv = [0, 0]
                sc_bank_ac = [0, 0, 0]
                for h in range(H):
                    if h + 1 < H:
                        loaded[h + 1] = load_head(h + 1)
                    k = h % 2
                    for gi, (t0, W) in enumerate(own_groups):
                        tg = own_start + t0
                        nch = (tg + W + 127) // 128
                        exp_ac = [0] * nch
                        par = gctr % 2
                        gctr += 1
                        sps = PS[4 + 2 * par]
                        aps = PS[5 + 2 * par]
                        grp_dv = pair_dv[par]

                        def emit_sigav(c, t0=t0, W=W, nch=nch, k=k, exp_ac=exp_ac,
                                       grp_dv=grp_dv, sps=sps, aps=aps):
                            pi = c % 3
                            R.wait("pe", "ac", exp_ac[c])
                            if c == 0:
                                R.wait("pe", "dv", grp_dv)
                            R.inc("pe", "pe", lambda e, pi=pi, W=W, c=c, nch=nch, sps=sps: e.matmul(
                                sps[:, 0:W], ones128[:], pt[pi][:, 0:W],
                                start=(c == 0), stop=(c == nch - 1)))
                            R.inc("pe", "pe", lambda e, pi=pi, W=W, c=c, k=k, nch=nch, aps=aps: e.matmul(
                                aps[:, 0:W], vh[k][:, c * 128:(c + 1) * 128], pt[pi][:, 0:W],
                                start=(c == 0), stop=(c == nch - 1)))
                            pt_use_pe[pi] = R.ct["pe"]

                        pend = []
                        for c in range(nch):
                            sb_ = PS[c % 3]
                            R.wait("pe", loaded[h][0], loaded[h][1])
                            R.wait("pe", "ac", sc_bank_ac[c % 3])
                            jd = c - tg // 128
                            diag = 0 <= jd < W // 128
                            R.inc("pe", "pe", lambda e, c=c, t0=t0, W=W, h=h, k=k, sb_=sb_, diag=diag: e.matmul(
                                sb_[:, 0:W], kh[k][:, c * 128:(c + 1) * 128],
                                q_sb[:, h * TOWN + t0: h * TOWN + t0 + W], start=True, stop=not diag))
                            if diag:
                                R.inc("pe", "pe", lambda e, W=W, jd=jd, sb_=sb_: e.matmul(
                                    sb_[:, 0:W], atri[:], bmv[:, jd * 512: jd * 512 + W],
                                    start=False, stop=True))
                            sc_pe = R.ct["pe"]
                            pi = c % 3
                            R.wait("act", "pe", sc_pe)
                            R.wait("act", "pe", pt_use_pe[pi])
                            R.inc("act", "ac", lambda e, pi=pi, W=W, h=h, sb_=sb_: e.activation(
                                out=pt[pi][:, 0:W], in_=sb_[:, 0:W], func=AF.Exp,
                                bias=negBb[:, h:h + 1], scale=1.0))
                            exp_ac[c] = R.ct["ac"]
                            sc_bank_ac[c % 3] = R.ct["ac"]
                            pend.append(c)
                            if len(pend) >= 3:
                                emit_sigav(pend.pop(0))
                        while pend:
                            emit_sigav(pend.pop(0))
                        kv_use_pe[k] = R.ct["pe"]
                        R.wait("dve", "pe", R.ct["pe"])
                        R.inc("dve", "dv", lambda e, W=W, sps=sps, par=par: e.reciprocal(
                            out=rbuf[par][:, 0:W], in_=sps[:, 0:W]))
                        R.inc("dve", "dv", lambda e, h=h, t0=t0, W=W, aps=aps, par=par: e.tensor_mul(
                            o_sb[:, h * TOWN + t0: h * TOWN + t0 + W], aps[:, 0:W], rbuf[par][:, 0:W]))
                        pair_dv[par] = R.ct["dv"]
                load_slab2(0)
                load_slab2(1)
                R.barrier()

            with contextlib.ExitStack() as p5:
                sq2 = [sb(p5, f"sq2{k}", [128, 512], BF16) for k in range(2)]
                ao = nc.alloc_sbuf_tensor_at(pfx + "ao", [128, 16 * TOWN], BF16,
                                             offset=nc.lookup_mloc(q_sb).addr)
                bank_rd2 = [0, 0]
                seq = 0
                for s8 in range(8):
                    kk = s8 % 2
                    if s8 >= 2:
                        load_slab2(s8)
                    sl = slab2_sl[s8]
                    for oc in range(2):
                        o = s8 * 2 + oc
                        for t0, W in own_groups:
                            b = seq % 2
                            seq += 1
                            R.wait("pe", sl[0], sl[1])
                            R.wait("pe", "ac", bank_rd2[b])
                            for i in range(16):
                                R.inc("pe", "pe", lambda e, i=i, oc=oc, t0=t0, W=W, kk=kk, b=b: e.matmul(
                                    PS[b][:, 0:W], slab2[kk][:, i * 256 + oc * 128: i * 256 + oc * 128 + 128],
                                    o_sb[:, i * TOWN + t0: i * TOWN + t0 + W],
                                    start=(i == 0), stop=(i == 15)))
                            R.wait("act", "pe", R.ct["pe"])
                            R.inc("act", "ac", lambda e, o=o, t0=t0, W=W, b=b: e.activation(
                                out=ao[:, o * TOWN + t0: o * TOWN + t0 + W], in_=PS[b][:, 0:W],
                                func=AF.Identity, bias=bo_c[:, o:o + 1], scale=1.0))
                            bank_rd2[b] = R.ct["ac"]
                    slab2_use[kk] = R.ct["pe"]
                lnb_pe = [0, 0]
                first_lnb = True
                ao_done_ac = R.ct["ac"]
                R.wait("dve", "ac", ao_done_ac)
                for t0, W in own_groups:
                    if first_lnb:
                        R.wait("pe", "ac", R.ct["ac"])
                        R.wait("pe", "dv", R.ct["dv"])
                        first_lnb = False
                    for i in range(16):
                        ib = i % 2
                        if ib == 0:
                            R.wait("act", "pe", lnb_pe[0])
                            R.inc("act", "ac", lambda e, i=i, t0=t0, W=W: e.activation(
                                out=sq2[0][:, 0:W], in_=ao[:, i * TOWN + t0: i * TOWN + t0 + W],
                                func=AF.Square, bias=0.0, scale=1.0))
                        else:
                            R.wait("dve", "pe", lnb_pe[1])
                            R.inc("dve", "dv", lambda e, i=i, t0=t0, W=W: e.tensor_mul(
                                sq2[1][:, 0:W], ao[:, i * TOWN + t0: i * TOWN + t0 + W],
                                ao[:, i * TOWN + t0: i * TOWN + t0 + W]))
                        R.wait("pe", "ac", R.ct["ac"])
                        R.wait("pe", "dv", R.ct["dv"])
                        R.inc("pe", "pe", lambda e, i=i, t0=t0, W=W: e.matmul(
                            PS[4][:, 0:W], ones128[:], ao[:, i * TOWN + t0: i * TOWN + t0 + W],
                            start=(i == 0), stop=(i == 15)))
                        R.inc("pe", "pe", lambda e, i=i, W=W, ib=ib: e.matmul(
                            PS[5][:, 0:W], ones128[:], sq2[ib][:, 0:W], start=(i == 0), stop=(i == 15)))
                        lnb_pe[ib] = R.ct["pe"]
                    R.wait("act", "pe", R.ct["pe"])
                    rC, rD = _ln_rows(R, PS[4], PS[5], rows, W, ninv)
                    R.wait("pe", "dv", R.ct["dv"])
                    R.inc("pe", "pe", lambda e, rC=rC, W=W: e.matmul(PS[6][:, 0:W], onesk1[:], rC, start=True, stop=True))
                    R.inc("pe", "pe", lambda e, rD=rD, W=W: e.matmul(PS[7][:, 0:W], onesk1[:], rD, start=True, stop=True))
                    R.wait("dve", "pe", R.ct["pe"])
                    for i in range(16):
                        R.inc("dve", "dv", lambda e, i=i, t0=t0, W=W: e.tensor_mul(
                            tscr[:, 0:W], ao[:, i * TOWN + t0: i * TOWN + t0 + W], PS[6][:, 0:W]))
                        R.inc("dve", "dv", lambda e, W=W: e.tensor_sub(
                            tscr[:, 0:W], tscr[:, 0:W], PS[7][:, 0:W]))
                        R.inc("dve", "dv", lambda e, i=i, t0=t0, W=W: e.tensor_add(
                            x_fm[:, i * TOWN + t0: i * TOWN + t0 + W],
                            x_fm[:, i * TOWN + t0: i * TOWN + t0 + W], tscr[:, 0:W]))
                R.barrier(("pe", "act", "dve", "sp"))
                R.wait("pool", "pe", R.ct["pe"])
                R.wait("pool", "ac", R.ct["ac"])
                R.wait("pool", "dv", R.ct["dv"])
    att.close()

    if _STAGE >= 3:
        with contextlib.ExitStack() as p6:
            HTS = ffn_groups[0][1]
            ln2 = sb(p6, "ln2", [128, 16 * HTS], BF16)
            ht = sb(p6, "ht", [128, 64 * HTS], BF16)
            yb = sb(p6, "yb", [128, 16 * HTS], BF16)
            wgs = [sb(p6, f"wgs{k}", [128, 2048], BF16) for k in range(3)]
            wus = [sb(p6, f"wus{k}", [128, 2048], BF16) for k in range(3)]
            wds = [sb(p6, f"wds{k}", [128, 2 * 1024], BF16) for k in range(2)]
            gsb = sb(p6, "gsb", [128, 2 * 512], BF16)
            usb = sb(p6, "usb", [128, 2 * 512], BF16)
            lncb = sb(p6, "lncb", [128, 4 * 512], BF16)
            wg_use, wd_use = [0, 0, 0], [0, 0]
            lnc_pe = [0, 0]
            lnd_pe = [0, 0]
            gu_ac2 = [0, 0]
            gu_dv2 = [0, 0]

            first_lnc = True
            for t0, W in ffn_groups:
                if first_lnc:
                    R.wait("pe", "dv", R.ct["dv"])
                    first_lnc = False
                for i in range(16):
                    ib = i % 2
                    cbs = lncb[:, ib * 512: ib * 512 + W]
                    sqs = lncb[:, (2 + ib) * 512: (2 + ib) * 512 + W]
                    R.wait("act", "pe", lnc_pe[ib])
                    if i < 2:
                        R.wait("act", "dv", R.ct["dv"])
                    R.inc("act", "ac", lambda e, i=i, t0=t0, W=W, cbs=cbs: e.activation(
                        out=cbs[:, 0:W], in_=x_fm[:, i * TOWN + t0: i * TOWN + t0 + W],
                        func=AF.Identity, bias=0.0, scale=1.0))
                    R.wait("dve", "pe", lnc_pe[ib])
                    R.inc("dve", "dv", lambda e, i=i, t0=t0, W=W, sqs=sqs: e.tensor_mul(
                        sqs[:, 0:W], x_fm[:, i * TOWN + t0: i * TOWN + t0 + W],
                        x_fm[:, i * TOWN + t0: i * TOWN + t0 + W]))
                    R.wait("pe", "ac", R.ct["ac"])
                    R.wait("pe", "dv", R.ct["dv"])
                    R.inc("pe", "pe", lambda e, i=i, W=W, cbs=cbs: e.matmul(
                        PS[4][:, 0:W], ones128[:], cbs[:, 0:W], start=(i == 0), stop=(i == 15)))
                    R.inc("pe", "pe", lambda e, i=i, W=W, sqs=sqs: e.matmul(
                        PS[5][:, 0:W], ones128[:], sqs[:, 0:W], start=(i == 0), stop=(i == 15)))
                    lnc_pe[ib] = R.ct["pe"]
                R.wait("act", "pe", R.ct["pe"])
                rC, rD = _ln_rows(R, PS[4], PS[5], rows, W, ninv)
                R.wait("pe", "dv", R.ct["dv"])
                R.inc("pe", "pe", lambda e, rC=rC, W=W: e.matmul(PS[6][:, 0:W], onesk1[:], rC, start=True, stop=True))
                R.inc("pe", "pe", lambda e, rD=rD, W=W: e.matmul(PS[7][:, 0:W], onesk1[:], rD, start=True, stop=True))
                R.wait("dve", "pe", R.ct["pe"])
                for i in range(16):
                    R.inc("dve", "dv", lambda e, i=i, t0=t0, W=W: e.tensor_mul(
                        tscr[:, 0:W], x_fm[:, i * TOWN + t0: i * TOWN + t0 + W], PS[6][:, 0:W]))
                    R.inc("dve", "dv", lambda e, i=i, W=W: e.tensor_sub(
                        ln2[:, i * HTS: i * HTS + W], tscr[:, 0:W], PS[7][:, 0:W]))
                ln2_dv = R.ct["dv"]

                gu_ac = gu_ac2
                gu_dv = gu_dv2
                for ib in range(64):
                    k = ib % 3
                    ds = ("e0", "e1", "e2")[k]
                    pb = (ib % 2) * 2
                    R.wait("pool", ds, R.ct[ds])
                    R.wait("pool", "pe", wg_use[k])
                    R.dma("pool", ds, wgs[k][:], dram["wgt"][ib, :, :])
                    R.dma("pool", ds, wus[k][:], dram["wut"][ib, :, :])
                    gqc = R.ct[ds]
                    R.wait("pe", ds, gqc)
                    R.wait("pe", "ac", gu_ac[ib % 2])
                    if ib == 0:
                        R.wait("pe", "dv", ln2_dv)
                    for i in range(16):
                        R.inc("pe", "pe", lambda e, i=i, k=k, W=W, pb=pb: e.matmul(
                            PS[pb][:, 0:W], wgs[k][:, i * 128:(i + 1) * 128],
                            ln2[:, i * HTS: i * HTS + W], start=(i == 0), stop=(i == 15)))
                    for i in range(16):
                        R.inc("pe", "pe", lambda e, i=i, k=k, W=W, pb=pb: e.matmul(
                            PS[pb + 1][:, 0:W], wus[k][:, i * 128:(i + 1) * 128],
                            ln2[:, i * HTS: i * HTS + W], start=(i == 0), stop=(i == 15)))
                    wg_use[k] = R.ct["pe"]
                    gp = (ib % 2) * 512
                    R.wait("act", "pe", R.ct["pe"])
                    R.wait("act", "dv", gu_dv[ib % 2])
                    R.inc("act", "ac", lambda e, ib=ib, W=W, pb=pb, gp=gp: e.activation(
                        out=gsb[:, gp:gp + W], in_=PS[pb][:, 0:W], func=AF.Silu,
                        bias=0.0, scale=sg_c[:, ib:ib + 1]))
                    R.inc("act", "ac", lambda e, ib=ib, W=W, pb=pb, gp=gp: e.activation(
                        out=usb[:, gp:gp + W], in_=PS[pb + 1][:, 0:W], func=AF.Identity,
                        bias=0.0, scale=su_c[:, ib:ib + 1]))
                    gu_ac[ib % 2] = R.ct["ac"]
                    R.wait("dve", "ac", R.ct["ac"])
                    R.inc("dve", "dv", lambda e, ib=ib, W=W, gp=gp: e.tensor_mul(
                        ht[:, ib * HTS: ib * HTS + W], gsb[:, gp:gp + W], usb[:, gp:gp + W]))
                    gu_dv[ib % 2] = R.ct["dv"]
                ht_dv = R.ct["dv"]

                for oh in range(2):
                    for ib2 in range(32):
                        k = ib2 % 2
                        ds = ("e3", "e4")[k]
                        R.wait("pool", ds, R.ct[ds])
                        R.wait("pool", "pe", wd_use[k])
                        R.dma("pool", ds,
                              wds[k][:].rearrange("p (c o) -> p c o", o=1024),
                              dram["wdt"][ib2, :, :].rearrange("p (c o) -> p c o", o=2048)
                              [:, :, oh * 1024:(oh + 1) * 1024])
                        gqc = R.ct[ds]
                        R.wait("pe", ds, gqc)
                        if ib2 == 0:
                            R.wait("pe", "ac", R.ct["ac"])
                            R.wait("pe", "dv", ht_dv)
                        for o8 in range(8):
                            for c in range(2):
                                R.inc("pe", "pe", lambda e, o8=o8, c=c, ib2=ib2, k=k, W=W: e.matmul(
                                    PS[o8][:, 0:W], wds[k][:, c * 1024 + o8 * 128: c * 1024 + o8 * 128 + 128],
                                    ht[:, (ib2 * 2 + c) * HTS:(ib2 * 2 + c) * HTS + W],
                                    start=(ib2 == 0 and c == 0), stop=(ib2 == 31 and c == 1)))
                        wd_use[k] = R.ct["pe"]
                    R.wait("act", "pe", R.ct["pe"])
                    for o8 in range(8):
                        o = oh * 8 + o8
                        R.inc("act", "ac", lambda e, o=o, o8=o8, W=W: e.activation(
                            out=yb[:, o * HTS: o * HTS + W], in_=PS[o8][:, 0:W],
                            func=AF.Identity, bias=0.0, scale=sd_c[:, o:o + 1]))

                R.wait("dve", "ac", R.ct["ac"])
                for i in range(16):
                    ib = i % 2
                    sqs = ln2[:, ib * HTS: ib * HTS + W]
                    if ib == 0:
                        R.wait("act", "pe", lnd_pe[0])
                        R.inc("act", "ac", lambda e, i=i, W=W, sqs=sqs: e.activation(
                            out=sqs[:, 0:W], in_=yb[:, i * HTS: i * HTS + W],
                            func=AF.Square, bias=0.0, scale=1.0))
                    else:
                        R.wait("dve", "pe", lnd_pe[1])
                        R.inc("dve", "dv", lambda e, i=i, W=W, sqs=sqs: e.tensor_mul(
                            sqs[:, 0:W], yb[:, i * HTS: i * HTS + W], yb[:, i * HTS: i * HTS + W]))
                    R.wait("pe", "ac", R.ct["ac"])
                    R.wait("pe", "dv", R.ct["dv"])
                    R.inc("pe", "pe", lambda e, i=i, W=W: e.matmul(
                        PS[4][:, 0:W], ones128[:], yb[:, i * HTS: i * HTS + W], start=(i == 0), stop=(i == 15)))
                    R.inc("pe", "pe", lambda e, i=i, W=W, sqs=sqs: e.matmul(
                        PS[5][:, 0:W], ones128[:], sqs[:, 0:W], start=(i == 0), stop=(i == 15)))
                    lnd_pe[ib] = R.ct["pe"]
                R.wait("act", "pe", R.ct["pe"])
                rC, rD = _ln_rows(R, PS[4], PS[5], rows, W, ninv)
                R.wait("pe", "dv", R.ct["dv"])
                R.inc("pe", "pe", lambda e, rC=rC, W=W: e.matmul(PS[6][:, 0:W], onesk1[:], rC, start=True, stop=True))
                R.inc("pe", "pe", lambda e, rD=rD, W=W: e.matmul(PS[7][:, 0:W], onesk1[:], rD, start=True, stop=True))
                R.wait("dve", "pe", R.ct["pe"])
                for i in range(16):
                    R.inc("dve", "dv", lambda e, i=i, W=W: e.tensor_mul(
                        tscr[:, 0:W], yb[:, i * HTS: i * HTS + W], PS[6][:, 0:W]))
                    R.inc("dve", "dv", lambda e, W=W: e.tensor_sub(
                        tscr[:, 0:W], tscr[:, 0:W], PS[7][:, 0:W]))
                    R.inc("dve", "dv", lambda e, i=i, t0=t0, W=W: e.tensor_add(
                        x_fm[:, i * TOWN + t0: i * TOWN + t0 + W],
                        x_fm[:, i * TOWN + t0: i * TOWN + t0 + W], tscr[:, 0:W]))

                R.wait("sp", "dv", R.ct["dv"])
                for i in range(16):
                    R.dma("sp", "st", dram["yout"][i * 128:(i + 1) * 128, t0:t0 + W],
                          x_fm[:, i * TOWN + t0: i * TOWN + t0 + W])

    R.wait("sp", "st", R.ct["st"])
    return R


def build_nc():
    nc = bass.Bass("TRN2", target_bir_lowering=False)
    dram = {}
    dram["xin"] = nc.dram_tensor("xin", [S, D], F32, kind="ExternalInput").ap()
    for nm in ("wq", "wk", "wv"):
        dram[nm] = nc.dram_tensor(nm, [4, 128, 16, 512], F32, kind="ExternalInput").ap()
    dram["wo"] = nc.dram_tensor("wo", [8, 128, 16, 256], F32, kind="ExternalInput").ap()
    for nm in ("bq", "bk", "bo"):
        dram[nm] = nc.dram_tensor(nm, [128, 16], F32, kind="ExternalInput").ap()
    dram["bv"] = nc.dram_tensor("bv", [1, D], F32, kind="ExternalInput").ap()
    dram["wgt"] = nc.dram_tensor("wgt", [64, 128, 2048], I8, kind="ExternalInput").ap()
    dram["wut"] = nc.dram_tensor("wut", [64, 128, 2048], I8, kind="ExternalInput").ap()
    dram["wdt"] = nc.dram_tensor("wdt", [32, 128, 4096], I8, kind="ExternalInput").ap()
    dram["sg"] = nc.dram_tensor("sg", [128, 64], F32, kind="ExternalInput").ap()
    dram["su"] = nc.dram_tensor("su", [128, 64], F32, kind="ExternalInput").ap()
    dram["sd"] = nc.dram_tensor("sd", [128, 16], F32, kind="ExternalInput").ap()
    dram["yout"] = nc.dram_tensor("yout", [D, P_SPLIT], F32, kind="ExternalOutput").ap()
    dram["kdram"] = nc.dram_tensor("kdram", [D, S], BF16, kind="Internal").ap()
    dram["vdram"] = nc.dram_tensor("vdram", [S, D], BF16, kind="Internal").ap()

    es = contextlib.ExitStack()
    with es:
        sems = {s: es.enter_context(nc.semaphore(s)) for s in SEMS}
        with contextlib.ExitStack() as esA:
            roleA = build_role(nc, sems, esA, 0, P_SPLIT, P_SPLIT, dram, "A_")
        with contextlib.ExitStack() as esB:
            roleB = build_role(nc, sems, esB, P_SPLIT, S, S, dram, "B_")

        blk = es.enter_context(nc.Block())
        eng_map = {"pe": blk.tensor, "act": blk.scalar, "dve": blk.vector,
                   "pool": blk.gpsimd, "sp": blk.sync}
        for key in ENGS:
            def body(e, key=key):
                pid = e.partition_id()
                with e.If(pid < 4):
                    for f in roleA.q[key]:
                        f(e)
                with e.Else():
                    for f in roleB.q[key]:
                        f(e)
            eng_map[key](body)
    return nc


_NC_CACHE = {}


def _get_nc():
    if _STAGE not in _NC_CACHE:
        _NC_CACHE[_STAGE] = build_nc()
    return _NC_CACHE[_STAGE]


def _tile_ffn(wg, wu, wd):
    wgt = np.ascontiguousarray(
        wg.reshape(16, 128, 64, 128).transpose(2, 1, 0, 3).reshape(64, 128, 2048))
    wut = np.ascontiguousarray(
        wu.reshape(16, 128, 64, 128).transpose(2, 1, 0, 3).reshape(64, 128, 2048))
    wdt = np.ascontiguousarray(
        wd.reshape(32, 2, 128, 2048).transpose(0, 2, 1, 3).reshape(32, 128, 4096))
    return wgt, wut, wdt


def kernel(L0_attn_X, L0_attn_Wq, L0_attn_bq, L0_attn_Wk, L0_attn_bk,
           L0_attn_Wv, L0_attn_bv, L0_attn_Wo, L0_attn_bo,
           L0_ffn_w_gate, L0_ffn_w_up, L0_ffn_w_down,
           L0_ffn_s_gate, L0_ffn_s_up, L0_ffn_s_down):
    nc = _get_nc()
    X = np.ascontiguousarray(np.asarray(L0_attn_X, dtype=np.float32))
    wgt, wut, wdt = _tile_ffn(np.asarray(L0_ffn_w_gate, np.int8),
                              np.asarray(L0_ffn_w_up, np.int8),
                              np.asarray(L0_ffn_w_down, np.int8))
    col = lambda v, n: np.ascontiguousarray(np.asarray(v, np.float32).reshape(n, 128).T)
    tile = lambda w, ow: np.ascontiguousarray(
        np.asarray(w, np.float32).T.reshape(16, 128, D // ow, ow).transpose(2, 1, 0, 3))
    common = {
        "wq": tile(L0_attn_Wq, 512),
        "wk": tile(L0_attn_Wk, 512),
        "wv": tile(L0_attn_Wv, 512),
        "wo": tile(L0_attn_Wo, 256),
        "bq": col(L0_attn_bq, 16),
        "bk": col(L0_attn_bk, 16),
        "bv": np.asarray(L0_attn_bv, np.float32).reshape(1, D),
        "bo": col(L0_attn_bo, 16),
        "wgt": wgt, "wut": wut, "wdt": wdt,
        "sg": col(L0_ffn_s_gate, 64),
        "su": col(L0_ffn_s_up, 64),
        "sd": col(L0_ffn_s_down, 16),
    }
    in_maps = []
    for c in range(8):
        m = dict(common)
        m["xin"] = np.ascontiguousarray(X[c % 4].T)
        in_maps.append(m)
    res = run_bass_kernel_spmd(nc, in_maps, core_ids=list(range(8)))
    out = np.empty((4, S, D), np.float32)
    for b in range(4):
        out[b, :P_SPLIT] = res.results[b]["yout"][:, :P_SPLIT].T
        out[b, P_SPLIT:] = res.results[4 + b]["yout"][:, :S - P_SPLIT].T
    return out


# version 21
# speedup vs baseline: 1.0881x; 1.0297x over previous
import contextlib
import numpy as np

import concourse.bass as bass
import concourse.mybir as mybir
from concourse.bass_utils import run_bass_kernel_spmd

F32 = mybir.dt.float32
BF16 = mybir.dt.bfloat16
I8 = mybir.dt.int8
AF = mybir.ActivationFunctionType
ALU = mybir.AluOpType
AX = mybir.AxisListType

D = 2048
S = 2048
H = 16
INTER = 8192
P_SPLIT = 1152
CSC = float(128.0 ** -0.25)
EPS = 1e-5

ENGS = ("pe", "act", "dve", "pool", "sp")
SEMS = ("sp", "aq", "gq", "st", "pe", "ac", "dv", "pl",
        "d0", "d1", "d2", "d3", "d4", "d5", "d6", "d7", "d8", "d9", "da", "db",
        "e0", "e1", "e2", "e3", "e4", "e5", "e6")
SEM_OWNER = {"pe": "pe", "ac": "act", "dv": "dve", "pl": "pool"}

_STAGE = 3
_PIPE = True
_PRE = True
_CUT = 99


FENCED = {"act": "ac", "dve": "dv", "pool": "pl"}


class RoleBuild:

    def __init__(self, nc, sems):
        self.nc = nc
        self.sems = sems
        self.q = {k: [] for k in ENGS}
        self.ct = {k: 0 for k in SEMS}
        self.meta = {k: [] for k in ENGS}

    def do(self, eng, fn):
        if eng in FENCED:
            return self.inc(eng, FENCED[eng], fn)
        self.q[eng].append(fn)

    def wait(self, eng, sem, val):
        if val > 0:
            h = self.sems[sem]
            self.meta[eng].append(("w", sem, val))
            self.q[eng].append(lambda e, h=h, v=val: e.wait_ge(h, v))

    def inc(self, eng, sem, fn, n=1):
        if eng in FENCED:
            own = FENCED[eng]
            self.wait(eng, own, self.ct[own])
        h = self.sems[sem]
        self.ct[sem] += n
        self.meta[eng].append(("i", sem, n))
        self.q[eng].append(lambda e, fn=fn, h=h, n=n: fn(e).then_inc(h, n))
        return self.ct[sem]

    def check(self):
        pos = {k: 0 for k in ENGS}
        sems = {s: 0 for s in SEMS}
        progress = True
        while progress:
            progress = False
            for eng in ENGS:
                m = self.meta[eng]
                while pos[eng] < len(m):
                    kind, sem, val = m[pos[eng]]
                    if kind == "w" and sems[sem] < val:
                        break
                    if kind == "i":
                        sems[sem] += val
                    pos[eng] += 1
                    progress = True
        stuck = {e: (pos[e], len(self.meta[e])) for e in ENGS
                 if pos[e] < len(self.meta[e])}
        if stuck:
            det = {e: self.meta[e][pos[e]] for e in stuck}
            raise RuntimeError(f"deadlock in schedule: stuck={stuck} heads={det} sems={sems}")

    def dma(self, eng, sem, out, in_, **kw):
        return self.inc(eng, sem,
                        lambda e, o=out, i=in_, kw=kw: e.dma_start(out=o, in_=i, **kw),
                        n=16)

    def barrier(self, engs=ENGS):
        snap = dict(self.ct)
        for eng in engs:
            for s, v in snap.items():
                if v > 0 and SEM_OWNER.get(s) != eng:
                    self.wait(eng, s, v)


def _ln_rows(R, PSsum, PSsq, rows, W, ninv):
    rA = rows[0:1, 0:W]
    rB = rows[0:1, 512:512 + W]
    rC = rows[0:1, 1024:1024 + W]
    rD = rows[0:1, 1536:1536 + W]
    R.inc("act", "ac", lambda e: e.activation(out=rA, in_=PSsum[0:1, 0:W], func=AF.Identity, bias=0.0, scale=ninv))
    R.inc("act", "ac", lambda e: e.activation(out=rC, in_=PSsq[0:1, 0:W], func=AF.Identity, bias=0.0, scale=ninv))
    R.wait("dve", "ac", R.ct["ac"])
    R.inc("dve", "dv", lambda e: e.tensor_mul(rB, rA, rA))
    R.inc("dve", "dv", lambda e: e.scalar_tensor_tensor(out=rD, in0=rC, scalar=EPS, in1=rB, op0=ALU.add, op1=ALU.subtract))
    R.wait("act", "dv", R.ct["dv"])
    R.inc("act", "ac", lambda e: e.activation(out=rB, in_=rD, func=AF.Sqrt, bias=0.0, scale=1.0))
    R.wait("dve", "ac", R.ct["ac"])
    R.inc("dve", "dv", lambda e: e.reciprocal(out=rC, in_=rB))
    R.inc("dve", "dv", lambda e: e.tensor_mul(rD, rA, rC))
    return rC, rD


def _grp(total, width):
    out, o = [], 0
    while o < total:
        out.append((o, min(width, total - o)))
        o += min(width, total - o)
    return out


def build_role(nc, sems, es, own_start, own_end, nkeys, dram, pfx):
    R = RoleBuild(nc, sems)
    TOWN = own_end - own_start
    NK = nkeys
    NKC = NK // 128
    NOT = TOWN // 128
    ninv = 1.0 / 2048.0

    key_groups = _grp(NK, 512)
    own_groups = _grp(TOWN, 512)
    ffn_groups = _grp(TOWN, 384) if TOWN == 1152 else _grp(TOWN, 512)
    if TOWN == 1152:
        P1G = [(0, 512), (512, 512), (1024, 128)]
    else:
        P1G = [(0, 256), (256, 256), (512, 256), (768, 256), (1024, 128),
               (1152, 256), (1408, 256), (1664, 256), (1920, 128)]
    sb = lambda st, name, shape, dt: st.enter_context(nc.sbuf_tensor(pfx + name, shape, dt))
    onesk1 = sb(es, "onesk1", [1, 128], F32)
    onescol = sb(es, "onescol", [128, 1], BF16)
    ones128 = sb(es, "ones128", [128, 128], BF16)
    bo_c = sb(es, "bo_c", [128, 16], F32)
    sd_c = sb(es, "sd_c", [128, 16], F32)
    sg_c = sb(es, "sg_c", [128, 64], F32)
    su_c = sb(es, "su_c", [128, 64], F32)
    rows = sb(es, "rows", [1, 2048], F32)
    tscr = sb(es, "tscr", [128, 512], F32)
    x_fm = sb(es, "x_fm", [128, 16 * TOWN], F32)
    PS = [es.enter_context(nc.psum_tensor(f"{pfx}ps{i}", [128, 512], F32)) for i in range(8)]

    att = contextlib.ExitStack()
    atri = sb(att, "atri", [128, 128], BF16)
    bmv = sb(att, "bmv", [128, 4 * 512], BF16)
    bq_c = sb(att, "bq_c", [128, 16], F32)
    bk_c = sb(att, "bk_c", [128, 16], F32)
    negBb = sb(att, "negBb", [128, 16], F32)
    knmax = sb(att, "knmax", [1, 16], F32)
    qnmax = sb(att, "qnmax", [1, 16], F32)
    q_sb = sb(att, "q_sb", [128, 16 * TOWN], BF16)

    def pool_step(fn):
        c = R.inc("pool", "pl", fn)
        R.wait("pool", "pl", c)

    R.do("pool", lambda e: e.memset(onesk1[:], 1.0))
    R.do("pool", lambda e: e.memset(onescol[:], 1.0))
    R.do("pool", lambda e: e.memset(ones128[:], 1.0))
    R.do("pool", lambda e: e.memset(knmax[:], 0.0))
    R.do("pool", lambda e: e.memset(qnmax[:], 0.0))
    pool_step(lambda e: e.memset(atri[:], 1.0))
    pool_step(lambda e: e.affine_select(out=atri[:], in_=atri[:], compare_op=ALU.is_ge,
                                        fill=0.0, base=0, pattern=[[1, 128]], channel_multiplier=-1))
    for jd in range(4):
        if jd > 0:
            R.do("pool", lambda e, jd=jd: e.memset(bmv[:, jd * 512: jd * 512 + jd * 128], -1e9))
        pool_step(lambda e, jd=jd: e.memset(bmv[:, jd * 512 + jd * 128: jd * 512 + jd * 128 + 128], -1e9))
        pool_step(lambda e, jd=jd: e.affine_select(
            out=bmv[:, jd * 512 + jd * 128: jd * 512 + jd * 128 + 128],
            in_=bmv[:, jd * 512 + jd * 128: jd * 512 + jd * 128 + 128],
            compare_op=ALU.is_ge, fill=0.0, base=-1, pattern=[[-1, 128]], channel_multiplier=1))
        if jd < 3:
            R.do("pool", lambda e, jd=jd: e.memset(bmv[:, jd * 512 + (jd + 1) * 128: (jd + 1) * 512], 0.0))


    R.dma("act", "aq", bq_c[:, :], dram["bq"])
    R.dma("act", "aq", bk_c[:, :], dram["bk"])
    R.dma("act", "aq", bo_c[:, :], dram["bo"])
    R.dma("act", "aq", sd_c[:, :], dram["sd"])
    R.dma("act", "aq", sg_c[:, :], dram["sg"])
    R.dma("act", "aq", su_c[:, :], dram["su"])
    R.dma("act", "aq", rows[0:1, :], dram["bv"])
    R.wait("act", "aq", R.ct["aq"])
    R.inc("act", "ac", lambda e: e.activation(out=bq_c[:, :], in_=bq_c[:, :], func=AF.Identity, bias=0.0, scale=CSC))
    R.inc("act", "ac", lambda e: e.activation(out=bk_c[:, :], in_=bk_c[:, :], func=AF.Identity, bias=0.0, scale=CSC))
    R.barrier()

    with contextlib.ExitStack() as p1:
        ln1 = sb(p1, "ln1", [128, 16 * NK], BF16)
        bvb = sb(p1, "bvb", [128, 2048], BF16)

        for o4 in range(4):
            R.wait("pe", "ac", R.ct["ac"])
            R.inc("pe", "pe", lambda e, o4=o4: e.matmul(
                PS[1 + (o4 % 2)][:, 0:512], onesk1[:], rows[0:1, o4 * 512:(o4 + 1) * 512], start=True, stop=True))
            R.wait("act", "pe", R.ct["pe"])
            R.inc("act", "ac", lambda e, o4=o4: e.activation(
                out=bvb[:, o4 * 512:(o4 + 1) * 512], in_=PS[1 + (o4 % 2)][:, 0:512],
                func=AF.Identity, bias=0.0, scale=1.0))

        with contextlib.ExitStack() as p1a:
            xfg = [sb(p1a, f"xfg{k}", [128, 16 * 256], F32) for k in range(2)]
            cb = [sb(p1a, f"cb{k}", [128, 512], BF16) for k in range(2)]
            sq = [sb(p1a, f"sq{k}", [128, 512], BF16) for k in range(2)]
            fg_dv = [0, 0]

            p1_groups = P1G
            fgi = 0
            mm_pe_after = [0, 0]
            for gidx, (g0, gw) in enumerate(p1_groups):
                own = g0 >= own_start
                if not own:
                    fk = fgi % 2
                    fgi += 1

                def xwin(i, own=own, g0=g0, gw=gw, fk=(fgi - 1) % 2):
                    if own:
                        o = g0 - own_start
                        return x_fm[:, i * TOWN + o: i * TOWN + o + gw]
                    return xfg[fk][:, i * 256: i * 256 + gw]

                ds = "d0" if gidx % 2 == 0 else "d1"
                R.wait("sp", ds, R.ct[ds])
                if not own:
                    R.wait("sp", "dv", fg_dv[fk])
                for i in range(16):
                    R.dma("sp", ds, xwin(i), dram["xin"][i * 128:(i + 1) * 128, g0:g0 + gw])
                ld_sp = (ds, R.ct[ds])
                if _STAGE < 2:
                    continue
                R.wait("act", ld_sp[0], ld_sp[1])
                R.wait("dve", ld_sp[0], ld_sp[1])
                for i in range(16):
                    ib = i % 2
                    R.wait("act", "pe", mm_pe_after[ib])
                    R.inc("act", "ac", lambda e, i=i, f=xwin, ib=ib, gw=gw: e.activation(
                        out=cb[ib][:, 0:gw], in_=f(i), func=AF.Identity, bias=0.0, scale=1.0))
                    R.wait("dve", "pe", mm_pe_after[ib])
                    R.inc("dve", "dv", lambda e, i=i, f=xwin, ib=ib, gw=gw: e.tensor_mul(
                        sq[ib][:, 0:gw], f(i), f(i)))
                    R.wait("pe", "ac", R.ct["ac"])
                    R.wait("pe", "dv", R.ct["dv"])
                    R.inc("pe", "pe", lambda e, i=i, ib=ib, gw=gw: e.matmul(
                        PS[4][:, 0:gw], ones128[:], cb[ib][:, 0:gw], start=(i == 0), stop=(i == 15)))
                    R.inc("pe", "pe", lambda e, i=i, ib=ib, gw=gw: e.matmul(
                        PS[5][:, 0:gw], ones128[:], sq[ib][:, 0:gw], start=(i == 0), stop=(i == 15)))
                    mm_pe_after[ib] = R.ct["pe"]
                R.wait("act", "pe", R.ct["pe"])
                rC, rD = _ln_rows(R, PS[4], PS[5], rows, gw, ninv)
                R.wait("pe", "dv", R.ct["dv"])
                R.inc("pe", "pe", lambda e, rC=rC, gw=gw: e.matmul(PS[6][:, 0:gw], onesk1[:], rC, start=True, stop=True))
                R.inc("pe", "pe", lambda e, rD=rD, gw=gw: e.matmul(PS[7][:, 0:gw], onesk1[:], rD, start=True, stop=True))
                R.wait("dve", "pe", R.ct["pe"])
                for i in range(16):
                    R.inc("dve", "dv", lambda e, i=i, f=xwin, gw=gw: e.tensor_mul(
                        tscr[:, 0:gw], f(i), PS[6][:, 0:gw]))
                    R.inc("dve", "dv", lambda e, i=i, g0=g0, gw=gw: e.tensor_sub(
                        ln1[:, i * NK + g0: i * NK + g0 + gw], tscr[:, 0:gw], PS[7][:, 0:gw]))
                if not own:
                    fg_dv[fk] = R.ct["dv"]
            R.barrier()

        if _STAGE >= 2:
            with contextlib.ExitStack() as p1b:
                slab = [sb(p1b, f"slab{k}", [128, 16 * 512], BF16) for k in range(2)]
                ksb = [sb(p1b, f"ksb{k}", [128, 512], BF16) for k in range(2)]
                vsb = [sb(p1b, f"vsb{k}", [128, 512], BF16) for k in range(2)]
                nsc = sb(p1b, "nsc", [128, 512], BF16)
                nsc2 = sb(p1b, "nsc2", [128, 512], BF16)
                slab_use_pe = [0, 0]

                def load_slab(k, mat, s4):
                    ds = "e0" if k == 0 else "e1"
                    R.wait("pool", ds, R.ct[ds])
                    R.wait("pool", "pe", slab_use_pe[k])
                    R.dma("pool", ds, slab[k][:].rearrange("p (c o) -> p c o", o=512),
                          dram[mat][s4])
                    return (ds, R.ct[ds])

                proj_rd = [("ac", 0), ("ac", 0)]
                bank_dv = [0, 0, 0, 0]
                nrm_scr_pe = [0, 0]
                nb = 0
                pend_norm = []

                def emit_norm(ent):
                    scr, W, o, dst, bk4, sq_ac = ent
                    R.wait("pe", "ac", sq_ac)
                    R.wait("pe", "dv", bank_dv[bk4])
                    R.inc("pe", "pe", lambda e, scr=scr, W=W, bk4=bk4: e.matmul(
                        PS[4 + bk4][:, 0:W], ones128[:], scr[:, 0:W], start=True, stop=True))
                    nrm_scr_pe[0 if scr is nsc else 1] = R.ct["pe"]
                    R.wait("dve", "pe", R.ct["pe"])
                    R.inc("dve", "dv", lambda e, W=W, bk4=bk4: e.tensor_reduce(
                        out=rows[0:1, 504 + bk4:505 + bk4], in_=PS[4 + bk4][0:1, 0:W], axis=AX.X, op=ALU.max))
                    R.inc("dve", "dv", lambda e, o=o, bk4=bk4, dst=dst: e.tensor_tensor(
                        out=dst[0:1, o:o + 1], in0=dst[0:1, o:o + 1],
                        in1=rows[0:1, 504 + bk4:505 + bk4], op=ALU.max))
                    bank_dv[bk4] = R.ct["dv"]

                for mat, tiles, qbase, is_q in (("wk", key_groups, 0, False),
                                                ("wq", own_groups, own_start, True)):
                    seq = 0
                    for s4 in range(4):
                        sl = load_slab(s4 % 2, mat, s4)
                        for oc in range(4):
                            o = s4 * 4 + oc
                            for t0, W in tiles:
                                b = seq % 2
                                seq += 1
                                R.wait("pe", sl[0], sl[1])
                                R.wait("pe", proj_rd[b][0], proj_rd[b][1])
                                for i in range(16):
                                    R.inc("pe", "pe", lambda e, i=i, oc=oc, t0=t0, W=W, k=s4 % 2, qb=qbase, b=b: e.matmul(
                                        PS[b][:, 0:W], slab[k][:, i * 512 + oc * 128: i * 512 + oc * 128 + 128],
                                        ln1[:, i * NK + qb + t0: i * NK + qb + t0 + W],
                                        start=(i == 0), stop=(i == 15)))
                                slab_use_pe[s4 % 2] = R.ct["pe"]
                                if pend_norm:
                                    emit_norm(pend_norm.pop(0))
                                R.wait("act", "pe", R.ct["pe"])
                                scr = nsc if b == 0 else nsc2
                                if is_q:
                                    R.inc("act", "ac", lambda e, o=o, t0=t0, W=W, b=b: e.activation(
                                        out=q_sb[:, o * TOWN + t0: o * TOWN + t0 + W], in_=PS[b][:, 0:W],
                                        func=AF.Identity, bias=bq_c[:, o:o + 1], scale=CSC))
                                    proj_rd[b] = ("ac", R.ct["ac"])
                                    R.wait("act", "pe", nrm_scr_pe[b])
                                    R.inc("act", "ac", lambda e, o=o, t0=t0, W=W, scr=scr: e.activation(
                                        out=scr[:, 0:W], in_=q_sb[:, o * TOWN + t0: o * TOWN + t0 + W],
                                        func=AF.Square, bias=0.0, scale=1.0))
                                    pend_norm.append((scr, W, o, qnmax, nb % 4, R.ct["ac"]))
                                else:
                                    kbuf = ksb[b]
                                    dsb = "da" if b == 0 else "db"
                                    R.wait("act", dsb, R.ct[dsb])
                                    R.inc("act", "ac", lambda e, o=o, W=W, b=b, kbuf=kbuf: e.activation(
                                        out=kbuf[:, 0:W], in_=PS[b][:, 0:W],
                                        func=AF.Identity, bias=bk_c[:, o:o + 1], scale=CSC))
                                    proj_rd[b] = ("ac", R.ct["ac"])
                                    R.dma("act", dsb, dram["kdram"][o * 128:(o + 1) * 128, t0:t0 + W], kbuf[:, 0:W])
                                    R.wait("act", "pe", nrm_scr_pe[b])
                                    R.inc("act", "ac", lambda e, W=W, kbuf=kbuf, scr=scr: e.activation(
                                        out=scr[:, 0:W], in_=kbuf[:, 0:W],
                                        func=AF.Square, bias=0.0, scale=1.0))
                                    pend_norm.append((scr, W, o, knmax, nb % 4, R.ct["ac"]))
                                nb += 1
                    while pend_norm:
                        emit_norm(pend_norm.pop(0))

                seq = 0
                for s4 in range(4):
                    sl = load_slab(s4 % 2, "wv", s4)
                    for tt in range(NKC):
                        b = seq % 2
                        seq += 1
                        R.wait("pe", sl[0], sl[1])
                        R.wait("pe", proj_rd[b][0], proj_rd[b][1])
                        for i in range(16):
                            R.inc("pe", "pe", lambda e, i=i, tt=tt, k=s4 % 2, b=b: e.matmul(
                                PS[b][:, 0:512], ln1[:, i * NK + tt * 128: i * NK + tt * 128 + 128],
                                slab[k][:, i * 512:(i + 1) * 512], start=(i == 0), stop=(i == 15)))
                        slab_use_pe[s4 % 2] = R.ct["pe"]
                        dsb = "da" if b == 0 else "db"
                        R.wait("dve", "pe", R.ct["pe"])
                        R.wait("dve", dsb, R.ct[dsb])
                        R.inc("dve", "dv", lambda e, s4=s4, b=b: e.tensor_add(
                            vsb[b][:], PS[b][:, 0:512], bvb[:, s4 * 512:(s4 + 1) * 512]))
                        proj_rd[b] = ("dv", R.ct["dv"])
                        R.wait("act", "dv", R.ct["dv"])
                        R.dma("act", dsb, dram["vdram"][tt * 128:(tt + 1) * 128, s4 * 512:(s4 + 1) * 512], vsb[b][:])

                R.barrier()

    if _STAGE >= 2:
        rT = rows[0:1, 0:16]
        R.inc("dve", "dv", lambda e: e.tensor_mul(rT, qnmax[0:1, :], knmax[0:1, :]))
        R.wait("act", "dv", R.ct["dv"])
        R.inc("act", "ac", lambda e: e.activation(out=rT, in_=rT, func=AF.Sqrt, bias=0.0, scale=1.0))
        R.wait("dve", "ac", R.ct["ac"])
        R.inc("dve", "dv", lambda e: e.tensor_scalar_mul(rT, rT, -1.0))
        R.wait("pe", "dv", R.ct["dv"])
        R.inc("pe", "pe", lambda e: e.matmul(PS[6][:, 0:16], onesk1[:], rT, start=True, stop=True))
        R.wait("act", "pe", R.ct["pe"])
        R.inc("act", "ac", lambda e: e.activation(out=negBb[:], in_=PS[6][:, 0:16],
                                                  func=AF.Identity, bias=0.0, scale=1.0))
        R.barrier()

        with contextlib.ExitStack() as p4:
            o_sb = sb(p4, "o_sb", [128, 16 * TOWN], BF16)
            slab2 = [sb(p4, f"slb2{k}", [128, 16 * 256], BF16) for k in range(2)]
            slab2_use = [0, 0]
            slab2_sl = [("e0", 0)] * 8

            def load_slab2(s8):
                kk = s8 % 2
                ds = "e0" if kk == 0 else "e1"
                R.wait("pool", ds, R.ct[ds])
                R.wait("pool", "pe", slab2_use[kk])
                R.dma("pool", ds, slab2[kk][:].rearrange("p (c o) -> p c o", o=256),
                      dram["wo"][s8])
                slab2_sl[s8] = (ds, R.ct[ds])

            with contextlib.ExitStack() as p4b:
                kh = [sb(p4b, f"kh{k}", [128, NK], BF16) for k in range(2)]
                vh = [sb(p4b, f"vh{k}", [128, NK], BF16) for k in range(2)]
                pt = [sb(p4b, f"pt{k}", [128, 512], BF16) for k in range(3)]
                rbuf = [sb(p4b, f"rbuf{k}", [128, 512], F32) for k in range(2)]
                kv_use_pe = [0, 0]
                pt_use_pe = [0, 0, 0]

                def load_head(h):
                    k = h % 2
                    ds = "d4" if k == 0 else "d5"
                    R.wait("sp", ds, R.ct[ds])
                    R.wait("sp", "pe", kv_use_pe[k])
                    R.dma("sp", ds, kh[k][:], dram["kdram"][h * 128:(h + 1) * 128, 0:NK])
                    half = NKC // 2
                    for c0, n in ((0, half), (half, NKC - half)):
                        R.dma("sp", ds,
                              vh[k][:, c0 * 128:(c0 + n) * 128].rearrange("p (c d) -> p c d", d=128),
                              dram["vdram"][c0 * 128:(c0 + n) * 128, h * 128:(h + 1) * 128]
                              .rearrange("(c p) d -> p c d", p=128))
                    return (ds, R.ct[ds])

                loaded = [("d4", 0)] * (H + 1)
                loaded[0] = load_head(0)
                gctr = 0
                pair_dv = [0, 0]
                sc_bank_ac = [0, 0, 0]
                for h in range(H):
                    if h + 1 < H:
                        loaded[h + 1] = load_head(h + 1)
                    k = h % 2
                    for gi, (t0, W) in enumerate(own_groups):
                        tg = own_start + t0
                        nch = (tg + W + 127) // 128
                        exp_ac = [0] * nch
                        par = gctr % 2
                        gctr += 1
                        sps = PS[4 + 2 * par]
                        aps = PS[5 + 2 * par]
                        grp_dv = pair_dv[par]

                        def emit_sigav(c, t0=t0, W=W, nch=nch, k=k, exp_ac=exp_ac,
                                       grp_dv=grp_dv, sps=sps, aps=aps):
                            pi = c % 3
                            R.wait("pe", "ac", exp_ac[c])
                            if c == 0:
                                R.wait("pe", "dv", grp_dv)
                            R.inc("pe", "pe", lambda e, pi=pi, W=W, c=c, nch=nch, sps=sps: e.matmul(
                                sps[:, 0:W], ones128[:], pt[pi][:, 0:W],
                                start=(c == 0), stop=(c == nch - 1)))
                            R.inc("pe", "pe", lambda e, pi=pi, W=W, c=c, k=k, nch=nch, aps=aps: e.matmul(
                                aps[:, 0:W], vh[k][:, c * 128:(c + 1) * 128], pt[pi][:, 0:W],
                                start=(c == 0), stop=(c == nch - 1)))
                            pt_use_pe[pi] = R.ct["pe"]

                        pend = []
                        for c in range(nch):
                            sb_ = PS[c % 3]
                            R.wait("pe", loaded[h][0], loaded[h][1])
                            R.wait("pe", "ac", sc_bank_ac[c % 3])
                            jd = c - tg // 128
                            diag = 0 <= jd < W // 128
                            R.inc("pe", "pe", lambda e, c=c, t0=t0, W=W, h=h, k=k, sb_=sb_, diag=diag: e.matmul(
                                sb_[:, 0:W], kh[k][:, c * 128:(c + 1) * 128],
                                q_sb[:, h * TOWN + t0: h * TOWN + t0 + W], start=True, stop=not diag))
                            if diag:
                                R.inc("pe", "pe", lambda e, W=W, jd=jd, sb_=sb_: e.matmul(
                                    sb_[:, 0:W], atri[:], bmv[:, jd * 512: jd * 512 + W],
                                    start=False, stop=True))
                            sc_pe = R.ct["pe"]
                            pi = c % 3
                            R.wait("act", "pe", sc_pe)
                            R.wait("act", "pe", pt_use_pe[pi])
                            R.inc("act", "ac", lambda e, pi=pi, W=W, h=h, sb_=sb_: e.activation(
                                out=pt[pi][:, 0:W], in_=sb_[:, 0:W], func=AF.Exp,
                                bias=negBb[:, h:h + 1], scale=1.0))
                            exp_ac[c] = R.ct["ac"]
                            sc_bank_ac[c % 3] = R.ct["ac"]
                            pend.append(c)
                            if len(pend) >= 3:
                                emit_sigav(pend.pop(0))
                        while pend:
                            emit_sigav(pend.pop(0))
                        kv_use_pe[k] = R.ct["pe"]
                        R.wait("dve", "pe", R.ct["pe"])
                        R.inc("dve", "dv", lambda e, W=W, sps=sps, par=par: e.reciprocal(
                            out=rbuf[par][:, 0:W], in_=sps[:, 0:W]))
                        R.inc("dve", "dv", lambda e, h=h, t0=t0, W=W, aps=aps, par=par: e.tensor_mul(
                            o_sb[:, h * TOWN + t0: h * TOWN + t0 + W], aps[:, 0:W], rbuf[par][:, 0:W]))
                        pair_dv[par] = R.ct["dv"]
                load_slab2(0)
                load_slab2(1)
                R.barrier()

            with contextlib.ExitStack() as p5:
                sq2 = [sb(p5, f"sq2{k}", [128, 512], BF16) for k in range(2)]
                ao = nc.alloc_sbuf_tensor_at(pfx + "ao", [128, 16 * TOWN], BF16,
                                             offset=nc.lookup_mloc(q_sb).addr)
                bank_rd2 = [0, 0]
                ao_ac = {}
                seq = 0
                for s8 in range(8):
                    kk = s8 % 2
                    if s8 >= 2:
                        load_slab2(s8)
                    sl = slab2_sl[s8]
                    for oc in range(2):
                        o = s8 * 2 + oc
                        for gi, (t0, W) in enumerate(own_groups):
                            b = seq % 2
                            seq += 1
                            R.wait("pe", sl[0], sl[1])
                            R.wait("pe", "ac", bank_rd2[b])
                            for i in range(16):
                                R.inc("pe", "pe", lambda e, i=i, oc=oc, t0=t0, W=W, kk=kk, b=b: e.matmul(
                                    PS[b][:, 0:W], slab2[kk][:, i * 256 + oc * 128: i * 256 + oc * 128 + 128],
                                    o_sb[:, i * TOWN + t0: i * TOWN + t0 + W],
                                    start=(i == 0), stop=(i == 15)))
                            R.wait("act", "pe", R.ct["pe"])
                            R.inc("act", "ac", lambda e, o=o, t0=t0, W=W, b=b: e.activation(
                                out=ao[:, o * TOWN + t0: o * TOWN + t0 + W], in_=PS[b][:, 0:W],
                                func=AF.Identity, bias=bo_c[:, o:o + 1], scale=1.0))
                            bank_rd2[b] = R.ct["ac"]
                            ao_ac[(o, gi)] = R.ct["ac"]
                    slab2_use[kk] = R.ct["pe"]
                ao_end_ac = R.ct["ac"]

                rows_gate = {"dv": 0}
                sq_slot_pe = [0, 0]
                stats_free_ac = [0, 0]
                bcast_free_dv = [0, 0]
                for gi, (t0, W) in enumerate(own_groups):
                    par = gi % 2
                    PSsum, PSsq = PS[4 + 2 * par], PS[5 + 2 * par]
                    bA, bB = PS[0 + 2 * par], PS[1 + 2 * par]
                    for i in range(16):
                        sl2 = i % 2
                        R.wait("act", "pe", sq_slot_pe[sl2])
                        R.inc("act", "ac", lambda e, i=i, t0=t0, W=W, sl2=sl2: e.activation(
                            out=sq2[sl2][:, 0:W], in_=ao[:, i * TOWN + t0: i * TOWN + t0 + W],
                            func=AF.Square, bias=0.0, scale=1.0))
                        sq_ac = R.ct["ac"]
                        if i == 0:
                            R.wait("pe", "ac", stats_free_ac[par])
                        R.wait("pe", "ac", ao_ac[(i, gi)])
                        R.inc("pe", "pe", lambda e, i=i, t0=t0, W=W, PSsum=PSsum: e.matmul(
                            PSsum[:, 0:W], ones128[:], ao[:, i * TOWN + t0: i * TOWN + t0 + W],
                            start=(i == 0), stop=(i == 15)))
                        R.wait("pe", "ac", sq_ac)
                        R.inc("pe", "pe", lambda e, i=i, W=W, sl2=sl2, PSsq=PSsq: e.matmul(
                            PSsq[:, 0:W], ones128[:], sq2[sl2][:, 0:W], start=(i == 0), stop=(i == 15)))
                        sq_slot_pe[sl2] = R.ct["pe"]
                    pe_stop = R.ct["pe"]
                    R.wait("act", "pe", pe_stop)
                    R.wait("act", "dv", rows_gate["dv"])
                    ac0 = R.ct["ac"]
                    rC, rD = _ln_rows(R, PSsum, PSsq, rows, W, ninv)
                    stats_free_ac[par] = ac0 + 2
                    rows_dv = R.ct["dv"]
                    rows_gate["dv"] = rows_dv
                    if gi < 2:
                        R.wait("pe", "ac", ao_end_ac)
                    else:
                        R.wait("pe", "dv", bcast_free_dv[par])
                    R.wait("pe", "dv", rows_dv)
                    R.inc("pe", "pe", lambda e, rC=rC, W=W, bA=bA: e.matmul(
                        bA[:, 0:W], onesk1[:], rC, start=True, stop=True))
                    R.inc("pe", "pe", lambda e, rD=rD, W=W, bB=bB: e.matmul(
                        bB[:, 0:W], onesk1[:], rD, start=True, stop=True))
                    bc_pe = R.ct["pe"]
                    R.wait("dve", "pe", bc_pe)
                    for i in range(16):
                        R.inc("dve", "dv", lambda e, i=i, t0=t0, W=W, bA=bA: e.tensor_mul(
                            tscr[:, 0:W], ao[:, i * TOWN + t0: i * TOWN + t0 + W], bA[:, 0:W]))
                        R.inc("dve", "dv", lambda e, W=W, bB=bB: e.tensor_sub(
                            tscr[:, 0:W], tscr[:, 0:W], bB[:, 0:W]))
                        R.inc("dve", "dv", lambda e, i=i, t0=t0, W=W: e.tensor_add(
                            x_fm[:, i * TOWN + t0: i * TOWN + t0 + W],
                            x_fm[:, i * TOWN + t0: i * TOWN + t0 + W], tscr[:, 0:W]))
                    bcast_free_dv[par] = R.ct["dv"]
                lnb_b01_dv = bcast_free_dv[0]
                lnb_b23_dv = bcast_free_dv[1]
                R.barrier()
    att.close()

    if _STAGE >= 3:
        with contextlib.ExitStack() as p6:
            NG = len(ffn_groups)
            HTS = ffn_groups[0][1]
            ln2 = sb(p6, "ln2", [128, 16 * HTS], BF16)
            ht = sb(p6, "ht", [128, 64 * HTS], BF16)
            yb = sb(p6, "yb", [128, 16 * HTS], BF16)
            wgs = [sb(p6, f"wgs{k}", [128, 2048], BF16) for k in range(3)]
            wus = [sb(p6, f"wus{k}", [128, 2048], BF16) for k in range(3)]
            wds = [sb(p6, f"wds{k}", [128, 1024], BF16) for k in range(4)]
            gsb = sb(p6, "gsb", [128, 2 * 512], BF16)
            usb = sb(p6, "usb", [128, 2 * 512], BF16)
            lncb = sb(p6, "lncb", [128, 4 * 512], BF16)

            wg_use = [0, 0, 0]
            wd_use = [0, 0, 0, 0]
            gu_ac = [0, 0]
            gu_dv = [0, 0]
            dv_ht = [0] * 64
            slot_pe = {}
            gu_pre = {}

            def pump(ch, n=1):
                while n and ch["steps"]:
                    ch["steps"].pop(0)()
                    n -= 1

            def drain(ch):
                while ch["steps"]:
                    ch["steps"].pop(0)()

            def mk_lnc(g, g_st, g_bc, dst_free_pe):
                t0, W = ffn_groups[g]
                m = {"steps": []}
                st = {}

                def stats_i(i):
                    def f():
                        cs, qs = i % 2, 2 + i % 2
                        R.wait("act", "pe", slot_pe.get(("c", cs), 0))
                        R.inc("act", "ac", lambda e, i=i, cs=cs, W=W, t0=t0: e.activation(
                            out=lncb[:, cs * 512: cs * 512 + W],
                            in_=x_fm[:, i * TOWN + t0: i * TOWN + t0 + W],
                            func=AF.Identity, bias=0.0, scale=1.0))
                        cb_ac = R.ct["ac"]
                        R.wait("dve", "pe", slot_pe.get(("c", qs), 0))
                        R.inc("dve", "dv", lambda e, i=i, qs=qs, W=W, t0=t0: e.tensor_mul(
                            lncb[:, qs * 512: qs * 512 + W],
                            x_fm[:, i * TOWN + t0: i * TOWN + t0 + W],
                            x_fm[:, i * TOWN + t0: i * TOWN + t0 + W]))
                        sq_dv = R.ct["dv"]
                        if i == 0:
                            for s_, v_ in g_st:
                                R.wait("pe", s_, v_)
                        R.wait("pe", "ac", cb_ac)
                        R.inc("pe", "pe", lambda e, i=i, cs=cs, W=W: e.matmul(
                            PS[0][:, 0:W], ones128[:], lncb[:, cs * 512: cs * 512 + W],
                            start=(i == 0), stop=(i == 15)))
                        slot_pe[("c", cs)] = R.ct["pe"]
                        R.wait("pe", "dv", sq_dv)
                        R.inc("pe", "pe", lambda e, i=i, qs=qs, W=W: e.matmul(
                            PS[1][:, 0:W], ones128[:], lncb[:, qs * 512: qs * 512 + W],
                            start=(i == 0), stop=(i == 15)))
                        slot_pe[("c", qs)] = R.ct["pe"]
                        st["stop"] = R.ct["pe"]
                    return f

                for i in range(16):
                    m["steps"].append(stats_i(i))

                def rows_f():
                    R.wait("act", "pe", st["stop"])
                    R.wait("act", "dv", rows_gate["dv"])
                    ac0 = R.ct["ac"]
                    st["rC"], st["rD"] = _ln_rows(R, PS[0], PS[1], rows, W, ninv)
                    m["rows_read_ac"] = ac0 + 2
                    st["rows_dv"] = R.ct["dv"]
                    rows_gate["dv"] = R.ct["dv"]
                m["steps"].append(rows_f)

                def bcast_f():
                    for s_, v_ in g_bc:
                        R.wait("pe", s_, v_)
                    R.wait("pe", "dv", st["rows_dv"])
                    R.inc("pe", "pe", lambda e, W=W: e.matmul(
                        PS[2][:, 0:W], onesk1[:], st["rC"], start=True, stop=True))
                    R.inc("pe", "pe", lambda e, W=W: e.matmul(
                        PS[3][:, 0:W], onesk1[:], st["rD"], start=True, stop=True))
                    st["bc"] = R.ct["pe"]
                m["steps"].append(bcast_f)

                def norm_i(i):
                    def f():
                        if i == 0:
                            R.wait("dve", "pe", st["bc"])
                            R.wait("dve", "pe", dst_free_pe)
                        R.inc("dve", "dv", lambda e, i=i, W=W, t0=t0: e.tensor_mul(
                            tscr[:, 0:W], x_fm[:, i * TOWN + t0: i * TOWN + t0 + W], PS[2][:, 0:W]))
                        R.inc("dve", "dv", lambda e, i=i, W=W: e.tensor_sub(
                            ln2[:, i * HTS: i * HTS + W], tscr[:, 0:W], PS[3][:, 0:W]))
                        m["norm_dv"] = R.ct["dv"]
                    return f

                for i in range(16):
                    m["steps"].append(norm_i(i))
                return m

            def mk_lnd(g, evac_chunk, evac_end, prev_b67_dv):
                t0, W = ffn_groups[g]
                m = {"steps": []}
                st = {"na": 0, "nd": 0}

                def stats_i(i):
                    def f():
                        if i % 2 == 0:
                            sl2 = ("c", st["na"] % 2)
                            st["na"] += 1
                            R.wait("act", "pe", slot_pe.get(sl2, 0))
                            R.inc("act", "ac", lambda e, i=i, sl2=sl2, W=W: e.activation(
                                out=lncb[:, sl2[1] * 512: sl2[1] * 512 + W],
                                in_=yb[:, i * HTS: i * HTS + W],
                                func=AF.Square, bias=0.0, scale=1.0))
                            sq_ct, sq_sem = R.ct["ac"], "ac"
                        else:
                            sl2 = ("c", 2 + st["nd"] % 2)
                            st["nd"] += 1
                            R.wait("dve", "pe", slot_pe.get(sl2, 0))
                            R.inc("dve", "dv", lambda e, i=i, sl2=sl2, W=W: e.tensor_mul(
                                lncb[:, sl2[1] * 512: sl2[1] * 512 + W],
                                yb[:, i * HTS: i * HTS + W], yb[:, i * HTS: i * HTS + W]))
                            sq_ct, sq_sem = R.ct["dv"], "dv"
                        if i == 0:
                            R.wait("pe", "ac", evac_end)
                        R.wait("pe", "ac", evac_chunk[i])
                        R.inc("pe", "pe", lambda e, i=i, W=W: e.matmul(
                            PS[4][:, 0:W], ones128[:], yb[:, i * HTS: i * HTS + W],
                            start=(i == 0), stop=(i == 15)))
                        R.wait("pe", sq_sem, sq_ct)
                        R.inc("pe", "pe", lambda e, sl2=sl2, W=W: e.matmul(
                            PS[5][:, 0:W], ones128[:], lncb[:, sl2[1] * 512: sl2[1] * 512 + W],
                            start=(i == 0), stop=(i == 15)))
                        slot_pe[sl2] = R.ct["pe"]
                        st["stop"] = R.ct["pe"]
                    return f

                for i in range(16):
                    m["steps"].append(stats_i(i))

                def rows_f():
                    R.wait("act", "pe", st["stop"])
                    R.wait("act", "dv", rows_gate["dv"])
                    ac0 = R.ct["ac"]
                    st["rC"], st["rD"] = _ln_rows(R, PS[4], PS[5], rows, W, ninv)
                    m["rows_read_ac"] = ac0 + 2
                    st["rows_dv"] = R.ct["dv"]
                    rows_gate["dv"] = R.ct["dv"]
                m["steps"].append(rows_f)

                def bcast_f():
                    R.wait("pe", "ac", evac_end)
                    R.wait("pe", "dv", prev_b67_dv)
                    R.wait("pe", "dv", st["rows_dv"])
                    R.inc("pe", "pe", lambda e, W=W: e.matmul(
                        PS[6][:, 0:W], onesk1[:], st["rC"], start=True, stop=True))
                    R.inc("pe", "pe", lambda e, W=W: e.matmul(
                        PS[7][:, 0:W], onesk1[:], st["rD"], start=True, stop=True))
                    st["bc"] = R.ct["pe"]
                m["steps"].append(bcast_f)

                def norm_i(i):
                    def f():
                        if i == 0:
                            R.wait("dve", "pe", st["bc"])
                        R.inc("dve", "dv", lambda e, i=i, W=W: e.tensor_mul(
                            tscr[:, 0:W], yb[:, i * HTS: i * HTS + W], PS[6][:, 0:W]))
                        R.inc("dve", "dv", lambda e, W=W: e.tensor_sub(
                            tscr[:, 0:W], tscr[:, 0:W], PS[7][:, 0:W]))
                        R.inc("dve", "dv", lambda e, i=i, t0=t0, W=W: e.tensor_add(
                            x_fm[:, i * TOWN + t0: i * TOWN + t0 + W],
                            x_fm[:, i * TOWN + t0: i * TOWN + t0 + W], tscr[:, 0:W]))
                        m["norm_dv"] = R.ct["dv"]
                    return f

                for i in range(16):
                    m["steps"].append(norm_i(i))

                def store_f():
                    R.wait("sp", "dv", m["norm_dv"])
                    for i in range(16):
                        R.dma("sp", "st", dram["yout"][i * 128:(i + 1) * 128, t0:t0 + W],
                              x_fm[:, i * TOWN + t0: i * TOWN + t0 + W])
                m["steps"].append(store_f)
                return m

            lnc_ch = mk_lnc(0, [("dv", lnb_b01_dv)], [("dv", lnb_b23_dv)], 0)
            if _CUT >= 1:
                drain(lnc_ch)
            lnd_prev = None
            prev_lnd_rows_ac = 0
            prev_lnd_norm = 0
            down_done_pe = 0

            for g in range(NG if _CUT >= 9 else min(NG, 1)):
                if _CUT < 2:
                    break
                t0, W = ffn_groups[g]
                for ib in range(64):
                    k = ib % 3
                    if (g, ib) in gu_pre:
                        ds, gqc = gu_pre.pop((g, ib))
                    else:
                        ds = ("e0", "e1", "e2")[k]
                        R.wait("pool", ds, R.ct[ds])
                        R.wait("pool", "pe", wg_use[k])
                        R.dma("pool", ds, wgs[k][:], dram["wgt"][ib, :, :])
                        R.dma("pool", ds, wus[k][:], dram["wut"][ib, :, :])
                        gqc = R.ct[ds]
                    pb = (ib % 2) * 2
                    R.wait("pe", ds, gqc)
                    R.wait("pe", "ac", gu_ac[ib % 2])
                    if ib == 0:
                        R.wait("pe", "dv", lnc_ch["norm_dv"])
                        R.wait("pe", "ac", lnc_ch["rows_read_ac"])
                    for i in range(16):
                        R.inc("pe", "pe", lambda e, i=i, k=k, W=W, pb=pb: e.matmul(
                            PS[pb][:, 0:W], wgs[k][:, i * 128:(i + 1) * 128],
                            ln2[:, i * HTS: i * HTS + W], start=(i == 0), stop=(i == 15)))
                    for i in range(16):
                        R.inc("pe", "pe", lambda e, i=i, k=k, W=W, pb=pb: e.matmul(
                            PS[pb + 1][:, 0:W], wus[k][:, i * 128:(i + 1) * 128],
                            ln2[:, i * HTS: i * HTS + W], start=(i == 0), stop=(i == 15)))
                    wg_use[k] = R.ct["pe"]
                    gp = (ib % 2) * 512
                    R.wait("act", "pe", R.ct["pe"])
                    R.wait("act", "dv", gu_dv[ib % 2])
                    R.inc("act", "ac", lambda e, ib=ib, W=W, pb=pb, gp=gp: e.activation(
                        out=gsb[:, gp:gp + W], in_=PS[pb][:, 0:W], func=AF.Silu,
                        bias=0.0, scale=sg_c[:, ib:ib + 1]))
                    R.inc("act", "ac", lambda e, ib=ib, W=W, pb=pb, gp=gp: e.activation(
                        out=usb[:, gp:gp + W], in_=PS[pb + 1][:, 0:W], func=AF.Identity,
                        bias=0.0, scale=su_c[:, ib:ib + 1]))
                    gu_ac[ib % 2] = R.ct["ac"]
                    R.wait("dve", "ac", R.ct["ac"])
                    if ib == 0:
                        R.wait("dve", "pe", down_done_pe)
                    R.inc("dve", "dv", lambda e, ib=ib, W=W, gp=gp: e.tensor_mul(
                        ht[:, ib * HTS: ib * HTS + W], gsb[:, gp:gp + W], usb[:, gp:gp + W]))
                    dv_ht[ib] = R.ct["dv"]
                    gu_dv[ib % 2] = R.ct["dv"]
                    if _PIPE and lnd_prev is not None and ib >= 2:
                        pump(lnd_prev, 1)
                if lnd_prev is not None:
                    drain(lnd_prev)
                    prev_lnd_rows_ac = lnd_prev["rows_read_ac"]
                    prev_lnd_norm = lnd_prev["norm_dv"]
                    lnd_prev = None
                wg_done = R.ct["pe"]
                lnc_next = (mk_lnc(g + 1, [("ac", gu_ac[0])], [("ac", gu_ac[1])], wg_done)
                            if g + 1 < NG else None)
                if lnc_next is not None and not _PIPE:
                    drain(lnc_next)
                if _CUT < 3:
                    break

                evac_chunk = [0] * 16
                it = 0
                qevac_ac = 0
                for q in range(4):
                    for ib2 in range(32):
                        kd = it % 4
                        ds = ("e3", "e4", "e5", "e6")[kd]
                        R.wait("pool", ds, R.ct[ds])
                        R.wait("pool", "pe", wd_use[kd])
                        R.dma("pool", ds, wds[kd][:], dram["wdt"][q * 32 + ib2])
                        gqc = R.ct[ds]
                        R.wait("pe", ds, gqc)
                        if q == 0:
                            R.wait("pe", "dv", dv_ht[min(ib2 * 2 + 1, 63)])
                        if ib2 == 0 and q > 0:
                            R.wait("pe", "ac", qevac_ac)
                        if it == 0:
                            R.wait("pe", "ac", prev_lnd_rows_ac)
                            R.wait("pe", "dv", prev_lnd_norm)
                        for c in range(2):
                            for j in range(4):
                                R.inc("pe", "pe", lambda e, c=c, j=j, ib2=ib2, kd=kd, W=W: e.matmul(
                                    PS[4 + j][:, 0:W],
                                    wds[kd][:, c * 512 + j * 128: c * 512 + j * 128 + 128],
                                    ht[:, (ib2 * 2 + c) * HTS: (ib2 * 2 + c) * HTS + W],
                                    start=(ib2 == 0 and c == 0), stop=(ib2 == 31 and c == 1)))
                        wd_use[kd] = R.ct["pe"]
                        if _PRE and q == 2 and ib2 < 3 and g + 1 < NG:
                            dsn = ("e0", "e1", "e2")[ib2 % 3]
                            R.wait("pool", dsn, R.ct[dsn])
                            R.wait("pool", "pe", wg_use[ib2 % 3])
                            R.dma("pool", dsn, wgs[ib2 % 3][:], dram["wgt"][ib2, :, :])
                            R.dma("pool", dsn, wus[ib2 % 3][:], dram["wut"][ib2, :, :])
                            gu_pre[(g + 1, ib2)] = (dsn, R.ct[dsn])
                        if _PIPE and lnc_next is not None and it >= 4:
                            pump(lnc_next, 1)
                        it += 1
                    R.wait("act", "pe", R.ct["pe"])
                    for j in range(4):
                        o = q * 4 + j
                        R.inc("act", "ac", lambda e, o=o, j=j, W=W: e.activation(
                            out=yb[:, o * HTS: o * HTS + W], in_=PS[4 + j][:, 0:W],
                            func=AF.Identity, bias=0.0, scale=sd_c[:, o:o + 1]))
                        evac_chunk[o] = R.ct["ac"]
                    qevac_ac = R.ct["ac"]
                down_done_pe = R.ct["pe"]
                down_evac_end = R.ct["ac"]
                if lnc_next is not None:
                    drain(lnc_next)
                if _CUT < 4:
                    break
                lnd = mk_lnd(g, evac_chunk, down_evac_end, prev_lnd_norm)
                if g + 1 < NG:
                    lnd_prev = lnd
                    lnc_ch = lnc_next
                else:
                    drain(lnd)

    R.wait("sp", "st", R.ct["st"])
    R.check()
    return R


def build_nc():
    nc = bass.Bass("TRN2", target_bir_lowering=False)
    dram = {}
    dram["xin"] = nc.dram_tensor("xin", [S, D], F32, kind="ExternalInput").ap()
    for nm in ("wq", "wk", "wv"):
        dram[nm] = nc.dram_tensor(nm, [4, 128, 16, 512], F32, kind="ExternalInput").ap()
    dram["wo"] = nc.dram_tensor("wo", [8, 128, 16, 256], F32, kind="ExternalInput").ap()
    for nm in ("bq", "bk", "bo"):
        dram[nm] = nc.dram_tensor(nm, [128, 16], F32, kind="ExternalInput").ap()
    dram["bv"] = nc.dram_tensor("bv", [1, D], F32, kind="ExternalInput").ap()
    dram["wgt"] = nc.dram_tensor("wgt", [64, 128, 2048], I8, kind="ExternalInput").ap()
    dram["wut"] = nc.dram_tensor("wut", [64, 128, 2048], I8, kind="ExternalInput").ap()
    dram["wdt"] = nc.dram_tensor("wdt", [128, 128, 1024], I8, kind="ExternalInput").ap()
    dram["sg"] = nc.dram_tensor("sg", [128, 64], F32, kind="ExternalInput").ap()
    dram["su"] = nc.dram_tensor("su", [128, 64], F32, kind="ExternalInput").ap()
    dram["sd"] = nc.dram_tensor("sd", [128, 16], F32, kind="ExternalInput").ap()
    dram["yout"] = nc.dram_tensor("yout", [D, P_SPLIT], F32, kind="ExternalOutput").ap()
    dram["kdram"] = nc.dram_tensor("kdram", [D, S], BF16, kind="Internal").ap()
    dram["vdram"] = nc.dram_tensor("vdram", [S, D], BF16, kind="Internal").ap()

    es = contextlib.ExitStack()
    with es:
        sems = {s: es.enter_context(nc.semaphore(s)) for s in SEMS}
        with contextlib.ExitStack() as esA:
            roleA = build_role(nc, sems, esA, 0, P_SPLIT, P_SPLIT, dram, "A_")
        with contextlib.ExitStack() as esB:
            roleB = build_role(nc, sems, esB, P_SPLIT, S, S, dram, "B_")

        blk = es.enter_context(nc.Block())
        eng_map = {"pe": blk.tensor, "act": blk.scalar, "dve": blk.vector,
                   "pool": blk.gpsimd, "sp": blk.sync}
        for key in ENGS:
            def body(e, key=key):
                pid = e.partition_id()
                with e.If(pid < 4):
                    for f in roleA.q[key]:
                        f(e)
                with e.Else():
                    for f in roleB.q[key]:
                        f(e)
            eng_map[key](body)
    return nc


_NC_CACHE = {}


def _get_nc():
    if _STAGE not in _NC_CACHE:
        _NC_CACHE[_STAGE] = build_nc()
    return _NC_CACHE[_STAGE]


def _tile_ffn(wg, wu, wd):
    wgt = np.ascontiguousarray(
        wg.reshape(16, 128, 64, 128).transpose(2, 1, 0, 3).reshape(64, 128, 2048))
    wut = np.ascontiguousarray(
        wu.reshape(16, 128, 64, 128).transpose(2, 1, 0, 3).reshape(64, 128, 2048))
    wdt = np.ascontiguousarray(
        wd.reshape(32, 2, 128, 4, 512).transpose(3, 0, 2, 1, 4).reshape(128, 128, 1024))
    return wgt, wut, wdt


def kernel(L0_attn_X, L0_attn_Wq, L0_attn_bq, L0_attn_Wk, L0_attn_bk,
           L0_attn_Wv, L0_attn_bv, L0_attn_Wo, L0_attn_bo,
           L0_ffn_w_gate, L0_ffn_w_up, L0_ffn_w_down,
           L0_ffn_s_gate, L0_ffn_s_up, L0_ffn_s_down):
    nc = _get_nc()
    X = np.ascontiguousarray(np.asarray(L0_attn_X, dtype=np.float32))
    wgt, wut, wdt = _tile_ffn(np.asarray(L0_ffn_w_gate, np.int8),
                              np.asarray(L0_ffn_w_up, np.int8),
                              np.asarray(L0_ffn_w_down, np.int8))
    col = lambda v, n: np.ascontiguousarray(np.asarray(v, np.float32).reshape(n, 128).T)
    tile = lambda w, ow: np.ascontiguousarray(
        np.asarray(w, np.float32).T.reshape(16, 128, D // ow, ow).transpose(2, 1, 0, 3))
    common = {
        "wq": tile(L0_attn_Wq, 512),
        "wk": tile(L0_attn_Wk, 512),
        "wv": tile(L0_attn_Wv, 512),
        "wo": tile(L0_attn_Wo, 256),
        "bq": col(L0_attn_bq, 16),
        "bk": col(L0_attn_bk, 16),
        "bv": np.asarray(L0_attn_bv, np.float32).reshape(1, D),
        "bo": col(L0_attn_bo, 16),
        "wgt": wgt, "wut": wut, "wdt": wdt,
        "sg": col(L0_ffn_s_gate, 64),
        "su": col(L0_ffn_s_up, 64),
        "sd": col(L0_ffn_s_down, 16),
    }
    in_maps = []
    for c in range(8):
        m = dict(common)
        m["xin"] = np.ascontiguousarray(X[c % 4].T)
        in_maps.append(m)
    res = run_bass_kernel_spmd(nc, in_maps, core_ids=list(range(8)))
    out = np.empty((4, S, D), np.float32)
    for b in range(4):
        out[b, :P_SPLIT] = res.results[b]["yout"][:, :P_SPLIT].T
        out[b, P_SPLIT:] = res.results[4 + b]["yout"][:, :S - P_SPLIT].T
    return out


# version 22 (final)
# speedup vs baseline: 1.1488x; 1.0558x over previous
import contextlib
import numpy as np

import concourse.bass as bass
import concourse.mybir as mybir
from concourse.bass_utils import run_bass_kernel_spmd

F32 = mybir.dt.float32
BF16 = mybir.dt.bfloat16
I8 = mybir.dt.int8
AF = mybir.ActivationFunctionType
ALU = mybir.AluOpType
AX = mybir.AxisListType

D = 2048
S = 2048
H = 16
INTER = 8192
P_SPLIT = 1152
CSC = float(128.0 ** -0.25)
EPS = 1e-5

ENGS = ("pe", "act", "dve", "pool", "sp")
SEMS = ("sp", "aq", "gq", "st", "pe", "ac", "dv", "pl",
        "d0", "d1", "d2", "d3", "d4", "d5", "d6", "d7", "d8", "d9", "da", "db",
        "e0", "e1", "e2", "e3", "e4", "e5", "e6")
SEM_OWNER = {"pe": "pe", "ac": "act", "dv": "dve", "pl": "pool"}

_STAGE = 3
_PIPE = True
_PRE = True
_CUT = 99


FENCED = {"act": "ac", "dve": "dv", "pool": "pl"}


class RoleBuild:

    def __init__(self, nc, sems):
        self.nc = nc
        self.sems = sems
        self.q = {k: [] for k in ENGS}
        self.ct = {k: 0 for k in SEMS}
        self.meta = {k: [] for k in ENGS}

    def do(self, eng, fn):
        if eng in FENCED:
            return self.inc(eng, FENCED[eng], fn)
        self.q[eng].append(fn)

    def wait(self, eng, sem, val):
        if val > 0:
            h = self.sems[sem]
            self.meta[eng].append(("w", sem, val))
            self.q[eng].append(lambda e, h=h, v=val: e.wait_ge(h, v))

    def inc(self, eng, sem, fn, n=1):
        if eng in FENCED:
            own = FENCED[eng]
            self.wait(eng, own, self.ct[own])
        h = self.sems[sem]
        self.ct[sem] += n
        self.meta[eng].append(("i", sem, n))
        self.q[eng].append(lambda e, fn=fn, h=h, n=n: fn(e).then_inc(h, n))
        return self.ct[sem]

    def check(self):
        pos = {k: 0 for k in ENGS}
        sems = {s: 0 for s in SEMS}
        progress = True
        while progress:
            progress = False
            for eng in ENGS:
                m = self.meta[eng]
                while pos[eng] < len(m):
                    kind, sem, val = m[pos[eng]]
                    if kind == "w" and sems[sem] < val:
                        break
                    if kind == "i":
                        sems[sem] += val
                    pos[eng] += 1
                    progress = True
        stuck = {e: (pos[e], len(self.meta[e])) for e in ENGS
                 if pos[e] < len(self.meta[e])}
        if stuck:
            det = {e: self.meta[e][pos[e]] for e in stuck}
            raise RuntimeError(f"deadlock in schedule: stuck={stuck} heads={det} sems={sems}")

    def dma(self, eng, sem, out, in_, **kw):
        return self.inc(eng, sem,
                        lambda e, o=out, i=in_, kw=kw: e.dma_start(out=o, in_=i, **kw),
                        n=16)

    def barrier(self, engs=ENGS):
        snap = dict(self.ct)
        for eng in engs:
            for s, v in snap.items():
                if v > 0 and SEM_OWNER.get(s) != eng:
                    self.wait(eng, s, v)


def _ln_rows(R, PSsum, PSsq, rows, W, ninv):
    rA = rows[0:1, 0:W]
    rB = rows[0:1, 512:512 + W]
    rC = rows[0:1, 1024:1024 + W]
    rD = rows[0:1, 1536:1536 + W]
    R.inc("act", "ac", lambda e: e.activation(out=rA, in_=PSsum[0:1, 0:W], func=AF.Identity, bias=0.0, scale=ninv))
    R.inc("act", "ac", lambda e: e.activation(out=rC, in_=PSsq[0:1, 0:W], func=AF.Identity, bias=0.0, scale=ninv))
    R.wait("dve", "ac", R.ct["ac"])
    R.inc("dve", "dv", lambda e: e.tensor_mul(rB, rA, rA))
    R.inc("dve", "dv", lambda e: e.scalar_tensor_tensor(out=rD, in0=rC, scalar=EPS, in1=rB, op0=ALU.add, op1=ALU.subtract))
    R.wait("act", "dv", R.ct["dv"])
    R.inc("act", "ac", lambda e: e.activation(out=rB, in_=rD, func=AF.Sqrt, bias=0.0, scale=1.0))
    R.wait("dve", "ac", R.ct["ac"])
    R.inc("dve", "dv", lambda e: e.reciprocal(out=rC, in_=rB))
    R.inc("dve", "dv", lambda e: e.tensor_mul(rD, rA, rC))
    return rC, rD


def _grp(total, width):
    out, o = [], 0
    while o < total:
        out.append((o, min(width, total - o)))
        o += min(width, total - o)
    return out


def build_role(nc, sems, es, own_start, own_end, nkeys, dram, pfx):
    R = RoleBuild(nc, sems)
    TOWN = own_end - own_start
    NK = nkeys
    NKC = NK // 128
    NOT = TOWN // 128
    ninv = 1.0 / 2048.0

    key_groups = _grp(NK, 512)
    own_groups = _grp(TOWN, 512)
    ffn_groups = _grp(TOWN, 384) if TOWN == 1152 else _grp(TOWN, 512)
    if TOWN == 1152:
        P1G = [(0, 512), (512, 512), (1024, 128)]
    else:
        P1G = [(0, 256), (256, 256), (512, 256), (768, 256), (1024, 128),
               (1152, 256), (1408, 256), (1664, 256), (1920, 128)]
    sb = lambda st, name, shape, dt: st.enter_context(nc.sbuf_tensor(pfx + name, shape, dt))
    onesk1 = sb(es, "onesk1", [1, 128], F32)
    onescol = sb(es, "onescol", [128, 1], BF16)
    ones128 = sb(es, "ones128", [128, 128], BF16)
    bo_c = sb(es, "bo_c", [128, 16], F32)
    sd_c = sb(es, "sd_c", [128, 16], F32)
    sg_c = sb(es, "sg_c", [128, 64], F32)
    su_c = sb(es, "su_c", [128, 64], F32)
    rows = sb(es, "rows", [1, 2048], F32)
    tscr = sb(es, "tscr", [128, 512], F32)
    x_fm = sb(es, "x_fm", [128, 16 * TOWN], F32)
    PS = [es.enter_context(nc.psum_tensor(f"{pfx}ps{i}", [128, 512], F32)) for i in range(8)]
    rows_gate = {"dv": 0}

    def srng(t):
        ml = nc.lookup_mloc(t)
        return (ml.addr, ml.addr + ml.size() // 128)

    att = contextlib.ExitStack()
    atri = sb(att, "atri", [128, 128], BF16)
    bmv = sb(att, "bmv", [128, 4 * 512], BF16)
    bq_c = sb(att, "bq_c", [128, 16], F32)
    bk_c = sb(att, "bk_c", [128, 16], F32)
    negBb = sb(att, "negBb", [128, 16], F32)
    knmax = sb(att, "knmax", [1, 16], F32)
    qnmax = sb(att, "qnmax", [1, 16], F32)
    q_sb = sb(att, "q_sb", [128, 16 * TOWN], BF16)

    def pool_step(fn):
        c = R.inc("pool", "pl", fn)
        R.wait("pool", "pl", c)

    R.do("pool", lambda e: e.memset(onesk1[:], 1.0))
    R.do("pool", lambda e: e.memset(onescol[:], 1.0))
    R.do("pool", lambda e: e.memset(ones128[:], 1.0))
    R.do("pool", lambda e: e.memset(knmax[:], 0.0))
    R.do("pool", lambda e: e.memset(qnmax[:], 0.0))
    pool_step(lambda e: e.memset(atri[:], 1.0))
    pool_step(lambda e: e.affine_select(out=atri[:], in_=atri[:], compare_op=ALU.is_ge,
                                        fill=0.0, base=0, pattern=[[1, 128]], channel_multiplier=-1))
    for jd in range(4):
        if jd > 0:
            R.do("pool", lambda e, jd=jd: e.memset(bmv[:, jd * 512: jd * 512 + jd * 128], -1e9))
        pool_step(lambda e, jd=jd: e.memset(bmv[:, jd * 512 + jd * 128: jd * 512 + jd * 128 + 128], -1e9))
        pool_step(lambda e, jd=jd: e.affine_select(
            out=bmv[:, jd * 512 + jd * 128: jd * 512 + jd * 128 + 128],
            in_=bmv[:, jd * 512 + jd * 128: jd * 512 + jd * 128 + 128],
            compare_op=ALU.is_ge, fill=0.0, base=-1, pattern=[[-1, 128]], channel_multiplier=1))
        if jd < 3:
            R.do("pool", lambda e, jd=jd: e.memset(bmv[:, jd * 512 + (jd + 1) * 128: (jd + 1) * 512], 0.0))


    R.dma("act", "aq", bq_c[:, :], dram["bq"])
    R.dma("act", "aq", bk_c[:, :], dram["bk"])
    R.dma("act", "aq", bo_c[:, :], dram["bo"])
    R.dma("act", "aq", sd_c[:, :], dram["sd"])
    R.dma("act", "aq", sg_c[:, :], dram["sg"])
    R.dma("act", "aq", su_c[:, :], dram["su"])
    R.dma("act", "aq", rows[0:1, :], dram["bv"])
    R.wait("act", "aq", R.ct["aq"])
    R.inc("act", "ac", lambda e: e.activation(out=bq_c[:, :], in_=bq_c[:, :], func=AF.Identity, bias=0.0, scale=CSC))
    R.inc("act", "ac", lambda e: e.activation(out=bk_c[:, :], in_=bk_c[:, :], func=AF.Identity, bias=0.0, scale=CSC))
    R.barrier()

    with contextlib.ExitStack() as p1:
        ln1 = sb(p1, "ln1", [128, 16 * NK], BF16)
        bvb = sb(p1, "bvb", [128, 2048], BF16)

        for o4 in range(4):
            R.wait("pe", "ac", R.ct["ac"])
            R.inc("pe", "pe", lambda e, o4=o4: e.matmul(
                PS[1 + (o4 % 2)][:, 0:512], onesk1[:], rows[0:1, o4 * 512:(o4 + 1) * 512], start=True, stop=True))
            R.wait("act", "pe", R.ct["pe"])
            R.inc("act", "ac", lambda e, o4=o4: e.activation(
                out=bvb[:, o4 * 512:(o4 + 1) * 512], in_=PS[1 + (o4 % 2)][:, 0:512],
                func=AF.Identity, bias=0.0, scale=1.0))

        with contextlib.ExitStack() as p1a:
            xfg = [sb(p1a, f"xfg{k}", [128, 16 * 256], F32) for k in range(2)]
            cb = [sb(p1a, f"cb{k}", [128, 512], BF16) for k in range(2)]
            sq = [sb(p1a, f"sq{k}", [128, 512], BF16) for k in range(2)]
            fg_dv = [0, 0]

            p1_groups = P1G
            fgi = 0
            cb_slot_pe = [0, 0]
            sq_slot_pe = [0, 0]
            stats_read_ac = 0
            bcast_free_dv1 = 0
            for gidx, (g0, gw) in enumerate(p1_groups):
                own = g0 >= own_start
                if not own:
                    fk = fgi % 2
                    fgi += 1

                def xwin(i, own=own, g0=g0, gw=gw, fk=(fgi - 1) % 2):
                    if own:
                        o = g0 - own_start
                        return x_fm[:, i * TOWN + o: i * TOWN + o + gw]
                    return xfg[fk][:, i * 256: i * 256 + gw]

                ds = "d0" if gidx % 2 == 0 else "d1"
                R.wait("sp", ds, R.ct[ds])
                if not own:
                    R.wait("sp", "dv", fg_dv[fk])
                for i in range(16):
                    R.dma("sp", ds, xwin(i), dram["xin"][i * 128:(i + 1) * 128, g0:g0 + gw])
                ld_sp = (ds, R.ct[ds])
                if _STAGE < 2:
                    continue
                R.wait("act", ld_sp[0], ld_sp[1])
                R.wait("dve", ld_sp[0], ld_sp[1])
                for i in range(16):
                    ib = i % 2
                    R.wait("act", "pe", cb_slot_pe[ib])
                    R.inc("act", "ac", lambda e, i=i, f=xwin, ib=ib, gw=gw: e.activation(
                        out=cb[ib][:, 0:gw], in_=f(i), func=AF.Identity, bias=0.0, scale=1.0))
                    cb_ac = R.ct["ac"]
                    R.wait("dve", "pe", sq_slot_pe[ib])
                    R.inc("dve", "dv", lambda e, i=i, f=xwin, ib=ib, gw=gw: e.tensor_mul(
                        sq[ib][:, 0:gw], f(i), f(i)))
                    sq_dv = R.ct["dv"]
                    if i == 0:
                        R.wait("pe", "ac", stats_read_ac)
                    R.wait("pe", "ac", cb_ac)
                    R.inc("pe", "pe", lambda e, i=i, ib=ib, gw=gw: e.matmul(
                        PS[4][:, 0:gw], ones128[:], cb[ib][:, 0:gw], start=(i == 0), stop=(i == 15)))
                    cb_slot_pe[ib] = R.ct["pe"]
                    R.wait("pe", "dv", sq_dv)
                    R.inc("pe", "pe", lambda e, i=i, ib=ib, gw=gw: e.matmul(
                        PS[5][:, 0:gw], ones128[:], sq[ib][:, 0:gw], start=(i == 0), stop=(i == 15)))
                    sq_slot_pe[ib] = R.ct["pe"]
                R.wait("act", "pe", R.ct["pe"])
                R.wait("act", "dv", rows_gate["dv"])
                ac0 = R.ct["ac"]
                rC, rD = _ln_rows(R, PS[4], PS[5], rows, gw, ninv)
                stats_read_ac = ac0 + 2
                rows_dv = R.ct["dv"]
                rows_gate["dv"] = rows_dv
                R.wait("pe", "dv", bcast_free_dv1)
                R.wait("pe", "dv", rows_dv)
                R.inc("pe", "pe", lambda e, rC=rC, gw=gw: e.matmul(PS[6][:, 0:gw], onesk1[:], rC, start=True, stop=True))
                R.inc("pe", "pe", lambda e, rD=rD, gw=gw: e.matmul(PS[7][:, 0:gw], onesk1[:], rD, start=True, stop=True))
                R.wait("dve", "pe", R.ct["pe"])
                for i in range(16):
                    R.inc("dve", "dv", lambda e, i=i, f=xwin, gw=gw: e.tensor_mul(
                        tscr[:, 0:gw], f(i), PS[6][:, 0:gw]))
                    R.inc("dve", "dv", lambda e, i=i, g0=g0, gw=gw: e.tensor_sub(
                        ln1[:, i * NK + g0: i * NK + g0 + gw], tscr[:, 0:gw], PS[7][:, 0:gw]))
                bcast_free_dv1 = R.ct["dv"]
                if not own:
                    fg_dv[fk] = R.ct["dv"]
            p1a_hazards = [(srng(xfg[0]), ("dv", fg_dv[0])), (srng(xfg[1]), ("dv", fg_dv[1])),
                           (srng(cb[0]), ("pe", cb_slot_pe[0])), (srng(cb[1]), ("pe", cb_slot_pe[1])),
                           (srng(sq[0]), ("pe", sq_slot_pe[0])), (srng(sq[1]), ("pe", sq_slot_pe[1]))]
            R.barrier(("pe", "act", "dve", "sp"))

        if _STAGE >= 2:
            with contextlib.ExitStack() as p1b:
                slab = [sb(p1b, f"slab{k}", [128, 16 * 512], BF16) for k in range(2)]
                ksb = [sb(p1b, f"ksb{k}", [128, 512], BF16) for k in range(2)]
                vsb = [sb(p1b, f"vsb{k}", [128, 512], BF16) for k in range(2)]
                nsc = sb(p1b, "nsc", [128, 512], BF16)
                nsc2 = sb(p1b, "nsc2", [128, 512], BF16)
                slab_use_pe = [0, 0]

                def load_slab(k, mat, s4, first=False):
                    ds = "e0" if k == 0 else "e1"
                    R.wait("pool", ds, R.ct[ds])
                    R.wait("pool", "pe", slab_use_pe[k])
                    if first:
                        a0, a1 = srng(slab[k])
                        for (h0, h1), (sem_, val_) in p1a_hazards:
                            if a0 < h1 and h0 < a1:
                                R.wait("pool", sem_, val_)
                    R.dma("pool", ds, slab[k][:].rearrange("p (c o) -> p c o", o=512),
                          dram[mat][s4])
                    return (ds, R.ct[ds])

                proj_rd = [("ac", 0), ("ac", 0)]
                bank_dv = [0, 0, 0, 0]
                nrm_scr_pe = [0, 0]
                nb = 0
                pend_norm = []

                def emit_norm(ent):
                    scr, W, o, dst, bk4, sq_ac = ent
                    R.wait("pe", "ac", sq_ac)
                    R.wait("pe", "dv", bank_dv[bk4])
                    R.inc("pe", "pe", lambda e, scr=scr, W=W, bk4=bk4: e.matmul(
                        PS[4 + bk4][:, 0:W], ones128[:], scr[:, 0:W], start=True, stop=True))
                    nrm_scr_pe[0 if scr is nsc else 1] = R.ct["pe"]
                    R.wait("dve", "pe", R.ct["pe"])
                    R.inc("dve", "dv", lambda e, W=W, bk4=bk4: e.tensor_reduce(
                        out=rows[0:1, 504 + bk4:505 + bk4], in_=PS[4 + bk4][0:1, 0:W], axis=AX.X, op=ALU.max))
                    R.inc("dve", "dv", lambda e, o=o, bk4=bk4, dst=dst: e.tensor_tensor(
                        out=dst[0:1, o:o + 1], in0=dst[0:1, o:o + 1],
                        in1=rows[0:1, 504 + bk4:505 + bk4], op=ALU.max))
                    bank_dv[bk4] = R.ct["dv"]

                for mat, tiles, qbase, is_q in (("wk", key_groups, 0, False),
                                                ("wq", own_groups, own_start, True)):
                    seq = 0
                    for s4 in range(4):
                        sl = load_slab(s4 % 2, mat, s4, first=(mat == "wk" and s4 < 2))
                        for oc in range(4):
                            o = s4 * 4 + oc
                            for t0, W in tiles:
                                b = seq % 2
                                seq += 1
                                R.wait("pe", sl[0], sl[1])
                                R.wait("pe", proj_rd[b][0], proj_rd[b][1])
                                for i in range(16):
                                    R.inc("pe", "pe", lambda e, i=i, oc=oc, t0=t0, W=W, k=s4 % 2, qb=qbase, b=b: e.matmul(
                                        PS[b][:, 0:W], slab[k][:, i * 512 + oc * 128: i * 512 + oc * 128 + 128],
                                        ln1[:, i * NK + qb + t0: i * NK + qb + t0 + W],
                                        start=(i == 0), stop=(i == 15)))
                                slab_use_pe[s4 % 2] = R.ct["pe"]
                                if pend_norm:
                                    emit_norm(pend_norm.pop(0))
                                R.wait("act", "pe", R.ct["pe"])
                                scr = nsc if b == 0 else nsc2
                                if is_q:
                                    R.inc("act", "ac", lambda e, o=o, t0=t0, W=W, b=b: e.activation(
                                        out=q_sb[:, o * TOWN + t0: o * TOWN + t0 + W], in_=PS[b][:, 0:W],
                                        func=AF.Identity, bias=bq_c[:, o:o + 1], scale=CSC))
                                    proj_rd[b] = ("ac", R.ct["ac"])
                                    R.wait("act", "pe", nrm_scr_pe[b])
                                    R.inc("act", "ac", lambda e, o=o, t0=t0, W=W, scr=scr: e.activation(
                                        out=scr[:, 0:W], in_=q_sb[:, o * TOWN + t0: o * TOWN + t0 + W],
                                        func=AF.Square, bias=0.0, scale=1.0))
                                    pend_norm.append((scr, W, o, qnmax, nb % 4, R.ct["ac"]))
                                else:
                                    kbuf = ksb[b]
                                    dsb = "da" if b == 0 else "db"
                                    R.wait("act", dsb, R.ct[dsb])
                                    R.inc("act", "ac", lambda e, o=o, W=W, b=b, kbuf=kbuf: e.activation(
                                        out=kbuf[:, 0:W], in_=PS[b][:, 0:W],
                                        func=AF.Identity, bias=bk_c[:, o:o + 1], scale=CSC))
                                    proj_rd[b] = ("ac", R.ct["ac"])
                                    R.dma("act", dsb, dram["kdram"][o * 128:(o + 1) * 128, t0:t0 + W], kbuf[:, 0:W])
                                    R.wait("act", "pe", nrm_scr_pe[b])
                                    R.inc("act", "ac", lambda e, W=W, kbuf=kbuf, scr=scr: e.activation(
                                        out=scr[:, 0:W], in_=kbuf[:, 0:W],
                                        func=AF.Square, bias=0.0, scale=1.0))
                                    pend_norm.append((scr, W, o, knmax, nb % 4, R.ct["ac"]))
                                nb += 1
                    while pend_norm:
                        emit_norm(pend_norm.pop(0))

                seq = 0
                for s4 in range(4):
                    sl = load_slab(s4 % 2, "wv", s4)
                    for tt in range(NKC):
                        b = seq % 2
                        seq += 1
                        R.wait("pe", sl[0], sl[1])
                        R.wait("pe", proj_rd[b][0], proj_rd[b][1])
                        for i in range(16):
                            R.inc("pe", "pe", lambda e, i=i, tt=tt, k=s4 % 2, b=b: e.matmul(
                                PS[b][:, 0:512], ln1[:, i * NK + tt * 128: i * NK + tt * 128 + 128],
                                slab[k][:, i * 512:(i + 1) * 512], start=(i == 0), stop=(i == 15)))
                        slab_use_pe[s4 % 2] = R.ct["pe"]
                        dsb = "da" if b == 0 else "db"
                        R.wait("dve", "pe", R.ct["pe"])
                        R.wait("dve", dsb, R.ct[dsb])
                        R.inc("dve", "dv", lambda e, s4=s4, b=b: e.tensor_add(
                            vsb[b][:], PS[b][:, 0:512], bvb[:, s4 * 512:(s4 + 1) * 512]))
                        proj_rd[b] = ("dv", R.ct["dv"])
                        R.wait("act", "dv", R.ct["dv"])
                        R.dma("act", dsb, dram["vdram"][tt * 128:(tt + 1) * 128, s4 * 512:(s4 + 1) * 512], vsb[b][:])

                R.barrier()

    if _STAGE >= 2:
        rT = rows[0:1, 0:16]
        R.inc("dve", "dv", lambda e: e.tensor_mul(rT, qnmax[0:1, :], knmax[0:1, :]))
        R.wait("act", "dv", R.ct["dv"])
        R.inc("act", "ac", lambda e: e.activation(out=rT, in_=rT, func=AF.Sqrt, bias=0.0, scale=1.0))
        R.wait("dve", "ac", R.ct["ac"])
        R.inc("dve", "dv", lambda e: e.tensor_scalar_mul(rT, rT, -1.0))
        R.wait("pe", "dv", R.ct["dv"])
        R.inc("pe", "pe", lambda e: e.matmul(PS[6][:, 0:16], onesk1[:], rT, start=True, stop=True))
        R.wait("act", "pe", R.ct["pe"])
        R.inc("act", "ac", lambda e: e.activation(out=negBb[:], in_=PS[6][:, 0:16],
                                                  func=AF.Identity, bias=0.0, scale=1.0))
        R.barrier()

        with contextlib.ExitStack() as p4:
            o_sb = sb(p4, "o_sb", [128, 16 * TOWN], BF16)
            slab2 = [sb(p4, f"slb2{k}", [128, 16 * 256], BF16) for k in range(2)]
            slab2_use = [0, 0]
            slab2_sl = [("e0", 0)] * 8

            def load_slab2(s8):
                kk = s8 % 2
                ds = "e0" if kk == 0 else "e1"
                R.wait("pool", ds, R.ct[ds])
                R.wait("pool", "pe", slab2_use[kk])
                R.dma("pool", ds, slab2[kk][:].rearrange("p (c o) -> p c o", o=256),
                      dram["wo"][s8])
                slab2_sl[s8] = (ds, R.ct[ds])

            with contextlib.ExitStack() as p4b:
                kh = [sb(p4b, f"kh{k}", [128, NK], BF16) for k in range(2)]
                vh = [sb(p4b, f"vh{k}", [128, NK], BF16) for k in range(2)]
                pt = [sb(p4b, f"pt{k}", [128, 512], BF16) for k in range(3)]
                rbuf = [sb(p4b, f"rbuf{k}", [128, 512], F32) for k in range(2)]
                kv_use_pe = [0, 0]
                pt_use_pe = [0, 0, 0]

                def load_head(h):
                    k = h % 2
                    ds = "d4" if k == 0 else "d5"
                    R.wait("sp", ds, R.ct[ds])
                    R.wait("sp", "pe", kv_use_pe[k])
                    R.dma("sp", ds, kh[k][:], dram["kdram"][h * 128:(h + 1) * 128, 0:NK])
                    half = NKC // 2
                    for c0, n in ((0, half), (half, NKC - half)):
                        R.dma("sp", ds,
                              vh[k][:, c0 * 128:(c0 + n) * 128].rearrange("p (c d) -> p c d", d=128),
                              dram["vdram"][c0 * 128:(c0 + n) * 128, h * 128:(h + 1) * 128]
                              .rearrange("(c p) d -> p c d", p=128))
                    return (ds, R.ct[ds])

                loaded = [("d4", 0)] * (H + 1)
                loaded[0] = load_head(0)
                gctr = 0
                pair_dv = [0, 0]
                sc_bank_ac = [0, 0, 0]
                for h in range(H):
                    if h + 1 < H:
                        loaded[h + 1] = load_head(h + 1)
                    k = h % 2
                    for gi, (t0, W) in enumerate(own_groups):
                        tg = own_start + t0
                        nch = (tg + W + 127) // 128
                        exp_ac = [0] * nch
                        par = gctr % 2
                        gctr += 1
                        sps = PS[4 + 2 * par]
                        aps = PS[5 + 2 * par]
                        grp_dv = pair_dv[par]

                        def emit_sigav(c, t0=t0, W=W, nch=nch, k=k, exp_ac=exp_ac,
                                       grp_dv=grp_dv, sps=sps, aps=aps):
                            pi = c % 3
                            R.wait("pe", "ac", exp_ac[c])
                            if c == 0:
                                R.wait("pe", "dv", grp_dv)
                            R.inc("pe", "pe", lambda e, pi=pi, W=W, c=c, nch=nch, sps=sps: e.matmul(
                                sps[:, 0:W], ones128[:], pt[pi][:, 0:W],
                                start=(c == 0), stop=(c == nch - 1)))
                            R.inc("pe", "pe", lambda e, pi=pi, W=W, c=c, k=k, nch=nch, aps=aps: e.matmul(
                                aps[:, 0:W], vh[k][:, c * 128:(c + 1) * 128], pt[pi][:, 0:W],
                                start=(c == 0), stop=(c == nch - 1)))
                            pt_use_pe[pi] = R.ct["pe"]

                        pend = []
                        for c in range(nch):
                            sb_ = PS[c % 3]
                            R.wait("pe", loaded[h][0], loaded[h][1])
                            R.wait("pe", "ac", sc_bank_ac[c % 3])
                            jd = c - tg // 128
                            diag = 0 <= jd < W // 128
                            R.inc("pe", "pe", lambda e, c=c, t0=t0, W=W, h=h, k=k, sb_=sb_, diag=diag: e.matmul(
                                sb_[:, 0:W], kh[k][:, c * 128:(c + 1) * 128],
                                q_sb[:, h * TOWN + t0: h * TOWN + t0 + W], start=True, stop=not diag))
                            if diag:
                                R.inc("pe", "pe", lambda e, W=W, jd=jd, sb_=sb_: e.matmul(
                                    sb_[:, 0:W], atri[:], bmv[:, jd * 512: jd * 512 + W],
                                    start=False, stop=True))
                            sc_pe = R.ct["pe"]
                            pi = c % 3
                            R.wait("act", "pe", sc_pe)
                            R.wait("act", "pe", pt_use_pe[pi])
                            R.inc("act", "ac", lambda e, pi=pi, W=W, h=h, sb_=sb_: e.activation(
                                out=pt[pi][:, 0:W], in_=sb_[:, 0:W], func=AF.Exp,
                                bias=negBb[:, h:h + 1], scale=1.0))
                            exp_ac[c] = R.ct["ac"]
                            sc_bank_ac[c % 3] = R.ct["ac"]
                            pend.append(c)
                            if len(pend) >= 3:
                                emit_sigav(pend.pop(0))
                        while pend:
                            emit_sigav(pend.pop(0))
                        kv_use_pe[k] = R.ct["pe"]
                        R.wait("dve", "pe", R.ct["pe"])
                        R.inc("dve", "dv", lambda e, W=W, sps=sps, par=par: e.reciprocal(
                            out=rbuf[par][:, 0:W], in_=sps[:, 0:W]))
                        R.inc("dve", "dv", lambda e, h=h, t0=t0, W=W, aps=aps, par=par: e.tensor_mul(
                            o_sb[:, h * TOWN + t0: h * TOWN + t0 + W], aps[:, 0:W], rbuf[par][:, 0:W]))
                        pair_dv[par] = R.ct["dv"]
                load_slab2(0)
                load_slab2(1)
                R.barrier()

            with contextlib.ExitStack() as p5:
                sq2 = [sb(p5, f"sq2{k}", [128, 512], BF16) for k in range(2)]
                ao = nc.alloc_sbuf_tensor_at(pfx + "ao", [128, 16 * TOWN], BF16,
                                             offset=nc.lookup_mloc(q_sb).addr)
                bank_rd2 = [0, 0]
                ao_ac = {}
                seq = 0
                for s8 in range(8):
                    kk = s8 % 2
                    if s8 >= 2:
                        load_slab2(s8)
                    sl = slab2_sl[s8]
                    for oc in range(2):
                        o = s8 * 2 + oc
                        for gi, (t0, W) in enumerate(own_groups):
                            b = seq % 2
                            seq += 1
                            R.wait("pe", sl[0], sl[1])
                            R.wait("pe", "ac", bank_rd2[b])
                            for i in range(16):
                                R.inc("pe", "pe", lambda e, i=i, oc=oc, t0=t0, W=W, kk=kk, b=b: e.matmul(
                                    PS[b][:, 0:W], slab2[kk][:, i * 256 + oc * 128: i * 256 + oc * 128 + 128],
                                    o_sb[:, i * TOWN + t0: i * TOWN + t0 + W],
                                    start=(i == 0), stop=(i == 15)))
                            R.wait("act", "pe", R.ct["pe"])
                            R.inc("act", "ac", lambda e, o=o, t0=t0, W=W, b=b: e.activation(
                                out=ao[:, o * TOWN + t0: o * TOWN + t0 + W], in_=PS[b][:, 0:W],
                                func=AF.Identity, bias=bo_c[:, o:o + 1], scale=1.0))
                            bank_rd2[b] = R.ct["ac"]
                            ao_ac[(o, gi)] = R.ct["ac"]
                    slab2_use[kk] = R.ct["pe"]
                ao_end_ac = R.ct["ac"]

                sq_slot_pe = [0, 0]
                stats_free_ac = [0, 0]
                bcast_free_dv = [0, 0]
                for gi, (t0, W) in enumerate(own_groups):
                    par = gi % 2
                    PSsum, PSsq = PS[4 + 2 * par], PS[5 + 2 * par]
                    bA, bB = PS[0 + 2 * par], PS[1 + 2 * par]
                    for i in range(16):
                        sl2 = i % 2
                        R.wait("act", "pe", sq_slot_pe[sl2])
                        R.inc("act", "ac", lambda e, i=i, t0=t0, W=W, sl2=sl2: e.activation(
                            out=sq2[sl2][:, 0:W], in_=ao[:, i * TOWN + t0: i * TOWN + t0 + W],
                            func=AF.Square, bias=0.0, scale=1.0))
                        sq_ac = R.ct["ac"]
                        if i == 0:
                            R.wait("pe", "ac", stats_free_ac[par])
                        R.wait("pe", "ac", ao_ac[(i, gi)])
                        R.inc("pe", "pe", lambda e, i=i, t0=t0, W=W, PSsum=PSsum: e.matmul(
                            PSsum[:, 0:W], ones128[:], ao[:, i * TOWN + t0: i * TOWN + t0 + W],
                            start=(i == 0), stop=(i == 15)))
                        R.wait("pe", "ac", sq_ac)
                        R.inc("pe", "pe", lambda e, i=i, W=W, sl2=sl2, PSsq=PSsq: e.matmul(
                            PSsq[:, 0:W], ones128[:], sq2[sl2][:, 0:W], start=(i == 0), stop=(i == 15)))
                        sq_slot_pe[sl2] = R.ct["pe"]
                    pe_stop = R.ct["pe"]
                    R.wait("act", "pe", pe_stop)
                    R.wait("act", "dv", rows_gate["dv"])
                    ac0 = R.ct["ac"]
                    rC, rD = _ln_rows(R, PSsum, PSsq, rows, W, ninv)
                    stats_free_ac[par] = ac0 + 2
                    rows_dv = R.ct["dv"]
                    rows_gate["dv"] = rows_dv
                    if gi < 2:
                        R.wait("pe", "ac", ao_end_ac)
                    else:
                        R.wait("pe", "dv", bcast_free_dv[par])
                    R.wait("pe", "dv", rows_dv)
                    R.inc("pe", "pe", lambda e, rC=rC, W=W, bA=bA: e.matmul(
                        bA[:, 0:W], onesk1[:], rC, start=True, stop=True))
                    R.inc("pe", "pe", lambda e, rD=rD, W=W, bB=bB: e.matmul(
                        bB[:, 0:W], onesk1[:], rD, start=True, stop=True))
                    bc_pe = R.ct["pe"]
                    R.wait("dve", "pe", bc_pe)
                    for i in range(16):
                        R.inc("dve", "dv", lambda e, i=i, t0=t0, W=W, bA=bA: e.tensor_mul(
                            tscr[:, 0:W], ao[:, i * TOWN + t0: i * TOWN + t0 + W], bA[:, 0:W]))
                        R.inc("dve", "dv", lambda e, W=W, bB=bB: e.tensor_sub(
                            tscr[:, 0:W], tscr[:, 0:W], bB[:, 0:W]))
                        R.inc("dve", "dv", lambda e, i=i, t0=t0, W=W: e.tensor_add(
                            x_fm[:, i * TOWN + t0: i * TOWN + t0 + W],
                            x_fm[:, i * TOWN + t0: i * TOWN + t0 + W], tscr[:, 0:W]))
                    bcast_free_dv[par] = R.ct["dv"]
                lnb_b01_dv = bcast_free_dv[0]
                lnb_b23_dv = bcast_free_dv[1]
                R.barrier()
    att.close()

    if _STAGE >= 3:
        with contextlib.ExitStack() as p6:
            NG = len(ffn_groups)
            HTS = ffn_groups[0][1]
            ln2 = sb(p6, "ln2", [128, 16 * HTS], BF16)
            ht = sb(p6, "ht", [128, 64 * HTS], BF16)
            yb = sb(p6, "yb", [128, 16 * HTS], BF16)
            wgs = [sb(p6, f"wgs{k}", [128, 2048], BF16) for k in range(3)]
            wus = [sb(p6, f"wus{k}", [128, 2048], BF16) for k in range(3)]
            wds = [sb(p6, f"wds{k}", [128, 1024], BF16) for k in range(4)]
            gsb = sb(p6, "gsb", [128, 2 * 512], BF16)
            usb = sb(p6, "usb", [128, 2 * 512], BF16)
            lncb = sb(p6, "lncb", [128, 4 * 512], BF16)

            wg_use = [0, 0, 0]
            wd_use = [0, 0, 0, 0]
            gu_ac = [0, 0]
            gu_dv = [0, 0]
            dv_ht = [0] * 64
            slot_pe = {}
            gu_pre = {}

            def pump(ch, n=1):
                while n and ch["steps"]:
                    ch["steps"].pop(0)()
                    n -= 1

            def drain(ch):
                while ch["steps"]:
                    ch["steps"].pop(0)()

            def mk_lnc(g, g_st, g_bc, dst_free_pe):
                t0, W = ffn_groups[g]
                m = {"steps": []}
                st = {}

                def stats_i(i):
                    def f():
                        cs, qs = i % 2, 2 + i % 2
                        R.wait("act", "pe", slot_pe.get(("c", cs), 0))
                        R.inc("act", "ac", lambda e, i=i, cs=cs, W=W, t0=t0: e.activation(
                            out=lncb[:, cs * 512: cs * 512 + W],
                            in_=x_fm[:, i * TOWN + t0: i * TOWN + t0 + W],
                            func=AF.Identity, bias=0.0, scale=1.0))
                        cb_ac = R.ct["ac"]
                        R.wait("dve", "pe", slot_pe.get(("c", qs), 0))
                        R.inc("dve", "dv", lambda e, i=i, qs=qs, W=W, t0=t0: e.tensor_mul(
                            lncb[:, qs * 512: qs * 512 + W],
                            x_fm[:, i * TOWN + t0: i * TOWN + t0 + W],
                            x_fm[:, i * TOWN + t0: i * TOWN + t0 + W]))
                        sq_dv = R.ct["dv"]
                        if i == 0:
                            for s_, v_ in g_st:
                                R.wait("pe", s_, v_)
                        R.wait("pe", "ac", cb_ac)
                        R.inc("pe", "pe", lambda e, i=i, cs=cs, W=W: e.matmul(
                            PS[0][:, 0:W], ones128[:], lncb[:, cs * 512: cs * 512 + W],
                            start=(i == 0), stop=(i == 15)))
                        slot_pe[("c", cs)] = R.ct["pe"]
                        R.wait("pe", "dv", sq_dv)
                        R.inc("pe", "pe", lambda e, i=i, qs=qs, W=W: e.matmul(
                            PS[1][:, 0:W], ones128[:], lncb[:, qs * 512: qs * 512 + W],
                            start=(i == 0), stop=(i == 15)))
                        slot_pe[("c", qs)] = R.ct["pe"]
                        st["stop"] = R.ct["pe"]
                    return f

                for i in range(16):
                    m["steps"].append(stats_i(i))

                def rows_f():
                    R.wait("act", "pe", st["stop"])
                    R.wait("act", "dv", rows_gate["dv"])
                    ac0 = R.ct["ac"]
                    st["rC"], st["rD"] = _ln_rows(R, PS[0], PS[1], rows, W, ninv)
                    m["rows_read_ac"] = ac0 + 2
                    st["rows_dv"] = R.ct["dv"]
                    rows_gate["dv"] = R.ct["dv"]
                m["steps"].append(rows_f)

                def bcast_f():
                    for s_, v_ in g_bc:
                        R.wait("pe", s_, v_)
                    R.wait("pe", "dv", st["rows_dv"])
                    R.inc("pe", "pe", lambda e, W=W: e.matmul(
                        PS[2][:, 0:W], onesk1[:], st["rC"], start=True, stop=True))
                    R.inc("pe", "pe", lambda e, W=W: e.matmul(
                        PS[3][:, 0:W], onesk1[:], st["rD"], start=True, stop=True))
                    st["bc"] = R.ct["pe"]
                m["steps"].append(bcast_f)

                def norm_i(i):
                    def f():
                        if i == 0:
                            R.wait("dve", "pe", st["bc"])
                            R.wait("dve", "pe", dst_free_pe)
                        R.inc("dve", "dv", lambda e, i=i, W=W, t0=t0: e.tensor_mul(
                            tscr[:, 0:W], x_fm[:, i * TOWN + t0: i * TOWN + t0 + W], PS[2][:, 0:W]))
                        R.inc("dve", "dv", lambda e, i=i, W=W: e.tensor_sub(
                            ln2[:, i * HTS: i * HTS + W], tscr[:, 0:W], PS[3][:, 0:W]))
                        m["norm_dv"] = R.ct["dv"]
                    return f

                for i in range(16):
                    m["steps"].append(norm_i(i))
                return m

            def mk_lnd(g, evac_chunk, evac_end, prev_b67_dv):
                t0, W = ffn_groups[g]
                m = {"steps": []}
                st = {"na": 0, "nd": 0}

                def stats_i(i):
                    def f():
                        if i % 2 == 0:
                            sl2 = ("c", st["na"] % 2)
                            st["na"] += 1
                            R.wait("act", "pe", slot_pe.get(sl2, 0))
                            R.inc("act", "ac", lambda e, i=i, sl2=sl2, W=W: e.activation(
                                out=lncb[:, sl2[1] * 512: sl2[1] * 512 + W],
                                in_=yb[:, i * HTS: i * HTS + W],
                                func=AF.Square, bias=0.0, scale=1.0))
                            sq_ct, sq_sem = R.ct["ac"], "ac"
                        else:
                            sl2 = ("c", 2 + st["nd"] % 2)
                            st["nd"] += 1
                            R.wait("dve", "pe", slot_pe.get(sl2, 0))
                            R.inc("dve", "dv", lambda e, i=i, sl2=sl2, W=W: e.tensor_mul(
                                lncb[:, sl2[1] * 512: sl2[1] * 512 + W],
                                yb[:, i * HTS: i * HTS + W], yb[:, i * HTS: i * HTS + W]))
                            sq_ct, sq_sem = R.ct["dv"], "dv"
                        if i == 0:
                            R.wait("pe", "ac", evac_end)
                        R.wait("pe", "ac", evac_chunk[i])
                        R.inc("pe", "pe", lambda e, i=i, W=W: e.matmul(
                            PS[4][:, 0:W], ones128[:], yb[:, i * HTS: i * HTS + W],
                            start=(i == 0), stop=(i == 15)))
                        R.wait("pe", sq_sem, sq_ct)
                        R.inc("pe", "pe", lambda e, sl2=sl2, W=W: e.matmul(
                            PS[5][:, 0:W], ones128[:], lncb[:, sl2[1] * 512: sl2[1] * 512 + W],
                            start=(i == 0), stop=(i == 15)))
                        slot_pe[sl2] = R.ct["pe"]
                        st["stop"] = R.ct["pe"]
                    return f

                for i in range(16):
                    m["steps"].append(stats_i(i))

                def rows_f():
                    R.wait("act", "pe", st["stop"])
                    R.wait("act", "dv", rows_gate["dv"])
                    ac0 = R.ct["ac"]
                    st["rC"], st["rD"] = _ln_rows(R, PS[4], PS[5], rows, W, ninv)
                    m["rows_read_ac"] = ac0 + 2
                    st["rows_dv"] = R.ct["dv"]
                    rows_gate["dv"] = R.ct["dv"]
                m["steps"].append(rows_f)

                def bcast_f():
                    R.wait("pe", "ac", evac_end)
                    R.wait("pe", "dv", prev_b67_dv)
                    R.wait("pe", "dv", st["rows_dv"])
                    R.inc("pe", "pe", lambda e, W=W: e.matmul(
                        PS[6][:, 0:W], onesk1[:], st["rC"], start=True, stop=True))
                    R.inc("pe", "pe", lambda e, W=W: e.matmul(
                        PS[7][:, 0:W], onesk1[:], st["rD"], start=True, stop=True))
                    st["bc"] = R.ct["pe"]
                m["steps"].append(bcast_f)

                def norm_i(i):
                    def f():
                        if i == 0:
                            R.wait("dve", "pe", st["bc"])
                        R.inc("dve", "dv", lambda e, i=i, W=W: e.tensor_mul(
                            tscr[:, 0:W], yb[:, i * HTS: i * HTS + W], PS[6][:, 0:W]))
                        R.inc("dve", "dv", lambda e, W=W: e.tensor_sub(
                            tscr[:, 0:W], tscr[:, 0:W], PS[7][:, 0:W]))
                        R.inc("dve", "dv", lambda e, i=i, t0=t0, W=W: e.tensor_add(
                            x_fm[:, i * TOWN + t0: i * TOWN + t0 + W],
                            x_fm[:, i * TOWN + t0: i * TOWN + t0 + W], tscr[:, 0:W]))
                        m["norm_dv"] = R.ct["dv"]
                    return f

                for i in range(16):
                    m["steps"].append(norm_i(i))

                def store_f():
                    R.wait("sp", "dv", m["norm_dv"])
                    for i in range(16):
                        R.dma("sp", "st", dram["yout"][i * 128:(i + 1) * 128, t0:t0 + W],
                              x_fm[:, i * TOWN + t0: i * TOWN + t0 + W])
                m["steps"].append(store_f)
                return m

            lnc_ch = mk_lnc(0, [("dv", lnb_b01_dv)], [("dv", lnb_b23_dv)], 0)
            if _CUT >= 1:
                drain(lnc_ch)
            lnd_prev = None
            prev_lnd_rows_ac = 0
            prev_lnd_norm = 0
            down_done_pe = 0

            for g in range(NG if _CUT >= 9 else min(NG, 1)):
                if _CUT < 2:
                    break
                t0, W = ffn_groups[g]
                for ib in range(64):
                    k = ib % 3
                    if (g, ib) in gu_pre:
                        ds, gqc = gu_pre.pop((g, ib))
                    else:
                        ds = ("e0", "e1", "e2")[k]
                        R.wait("pool", ds, R.ct[ds])
                        R.wait("pool", "pe", wg_use[k])
                        R.dma("pool", ds, wgs[k][:], dram["wgt"][ib, :, :])
                        R.dma("pool", ds, wus[k][:], dram["wut"][ib, :, :])
                        gqc = R.ct[ds]
                    pb = (ib % 2) * 2
                    R.wait("pe", ds, gqc)
                    R.wait("pe", "ac", gu_ac[ib % 2])
                    if ib == 0:
                        R.wait("pe", "dv", lnc_ch["norm_dv"])
                        R.wait("pe", "ac", lnc_ch["rows_read_ac"])
                    for i in range(16):
                        R.inc("pe", "pe", lambda e, i=i, k=k, W=W, pb=pb: e.matmul(
                            PS[pb][:, 0:W], wgs[k][:, i * 128:(i + 1) * 128],
                            ln2[:, i * HTS: i * HTS + W], start=(i == 0), stop=(i == 15)))
                    for i in range(16):
                        R.inc("pe", "pe", lambda e, i=i, k=k, W=W, pb=pb: e.matmul(
                            PS[pb + 1][:, 0:W], wus[k][:, i * 128:(i + 1) * 128],
                            ln2[:, i * HTS: i * HTS + W], start=(i == 0), stop=(i == 15)))
                    wg_use[k] = R.ct["pe"]
                    gp = (ib % 2) * 512
                    R.wait("act", "pe", R.ct["pe"])
                    R.wait("act", "dv", gu_dv[ib % 2])
                    R.inc("act", "ac", lambda e, ib=ib, W=W, pb=pb, gp=gp: e.activation(
                        out=gsb[:, gp:gp + W], in_=PS[pb][:, 0:W], func=AF.Silu,
                        bias=0.0, scale=sg_c[:, ib:ib + 1]))
                    R.inc("act", "ac", lambda e, ib=ib, W=W, pb=pb, gp=gp: e.activation(
                        out=usb[:, gp:gp + W], in_=PS[pb + 1][:, 0:W], func=AF.Identity,
                        bias=0.0, scale=su_c[:, ib:ib + 1]))
                    gu_ac[ib % 2] = R.ct["ac"]
                    R.wait("dve", "ac", R.ct["ac"])
                    if ib == 0:
                        R.wait("dve", "pe", down_done_pe)
                    R.inc("dve", "dv", lambda e, ib=ib, W=W, gp=gp: e.tensor_mul(
                        ht[:, ib * HTS: ib * HTS + W], gsb[:, gp:gp + W], usb[:, gp:gp + W]))
                    dv_ht[ib] = R.ct["dv"]
                    gu_dv[ib % 2] = R.ct["dv"]
                    if _PIPE and lnd_prev is not None and ib >= 2:
                        pump(lnd_prev, 1)
                if lnd_prev is not None:
                    drain(lnd_prev)
                    prev_lnd_rows_ac = lnd_prev["rows_read_ac"]
                    prev_lnd_norm = lnd_prev["norm_dv"]
                    lnd_prev = None
                wg_done = R.ct["pe"]
                lnc_next = (mk_lnc(g + 1, [("ac", gu_ac[0])], [("ac", gu_ac[1])], wg_done)
                            if g + 1 < NG else None)
                if lnc_next is not None and not _PIPE:
                    drain(lnc_next)
                if _CUT < 3:
                    break

                evac_chunk = [0] * 16
                it = 0
                qevac_ac = 0
                for q in range(4):
                    for ib2 in range(32):
                        kd = it % 4
                        ds = ("e3", "e4", "e5", "e6")[kd]
                        R.wait("pool", ds, R.ct[ds])
                        R.wait("pool", "pe", wd_use[kd])
                        R.dma("pool", ds, wds[kd][:], dram["wdt"][q * 32 + ib2])
                        gqc = R.ct[ds]
                        R.wait("pe", ds, gqc)
                        if q == 0:
                            R.wait("pe", "dv", dv_ht[min(ib2 * 2 + 1, 63)])
                        if ib2 == 0 and q > 0:
                            R.wait("pe", "ac", qevac_ac)
                        if it == 0:
                            R.wait("pe", "ac", prev_lnd_rows_ac)
                            R.wait("pe", "dv", prev_lnd_norm)
                        for c in range(2):
                            for j in range(4):
                                R.inc("pe", "pe", lambda e, c=c, j=j, ib2=ib2, kd=kd, W=W: e.matmul(
                                    PS[4 + j][:, 0:W],
                                    wds[kd][:, c * 512 + j * 128: c * 512 + j * 128 + 128],
                                    ht[:, (ib2 * 2 + c) * HTS: (ib2 * 2 + c) * HTS + W],
                                    start=(ib2 == 0 and c == 0), stop=(ib2 == 31 and c == 1)))
                        wd_use[kd] = R.ct["pe"]
                        if _PRE and q == 2 and ib2 < 3 and g + 1 < NG:
                            dsn = ("e0", "e1", "e2")[ib2 % 3]
                            R.wait("pool", dsn, R.ct[dsn])
                            R.wait("pool", "pe", wg_use[ib2 % 3])
                            R.dma("pool", dsn, wgs[ib2 % 3][:], dram["wgt"][ib2, :, :])
                            R.dma("pool", dsn, wus[ib2 % 3][:], dram["wut"][ib2, :, :])
                            gu_pre[(g + 1, ib2)] = (dsn, R.ct[dsn])
                        if _PIPE and lnc_next is not None and it >= 4:
                            pump(lnc_next, 1)
                        it += 1
                    R.wait("act", "pe", R.ct["pe"])
                    for j in range(4):
                        o = q * 4 + j
                        R.inc("act", "ac", lambda e, o=o, j=j, W=W: e.activation(
                            out=yb[:, o * HTS: o * HTS + W], in_=PS[4 + j][:, 0:W],
                            func=AF.Identity, bias=0.0, scale=sd_c[:, o:o + 1]))
                        evac_chunk[o] = R.ct["ac"]
                    qevac_ac = R.ct["ac"]
                down_done_pe = R.ct["pe"]
                down_evac_end = R.ct["ac"]
                if lnc_next is not None:
                    drain(lnc_next)
                if _CUT < 4:
                    break
                lnd = mk_lnd(g, evac_chunk, down_evac_end, prev_lnd_norm)
                if g + 1 < NG:
                    lnd_prev = lnd
                    lnc_ch = lnc_next
                else:
                    drain(lnd)

    R.wait("sp", "st", R.ct["st"])
    R.check()
    return R


def build_nc():
    nc = bass.Bass("TRN2", target_bir_lowering=False)
    dram = {}
    dram["xin"] = nc.dram_tensor("xin", [S, D], F32, kind="ExternalInput").ap()
    for nm in ("wq", "wk", "wv"):
        dram[nm] = nc.dram_tensor(nm, [4, 128, 16, 512], F32, kind="ExternalInput").ap()
    dram["wo"] = nc.dram_tensor("wo", [8, 128, 16, 256], F32, kind="ExternalInput").ap()
    for nm in ("bq", "bk", "bo"):
        dram[nm] = nc.dram_tensor(nm, [128, 16], F32, kind="ExternalInput").ap()
    dram["bv"] = nc.dram_tensor("bv", [1, D], F32, kind="ExternalInput").ap()
    dram["wgt"] = nc.dram_tensor("wgt", [64, 128, 2048], I8, kind="ExternalInput").ap()
    dram["wut"] = nc.dram_tensor("wut", [64, 128, 2048], I8, kind="ExternalInput").ap()
    dram["wdt"] = nc.dram_tensor("wdt", [128, 128, 1024], I8, kind="ExternalInput").ap()
    dram["sg"] = nc.dram_tensor("sg", [128, 64], F32, kind="ExternalInput").ap()
    dram["su"] = nc.dram_tensor("su", [128, 64], F32, kind="ExternalInput").ap()
    dram["sd"] = nc.dram_tensor("sd", [128, 16], F32, kind="ExternalInput").ap()
    dram["yout"] = nc.dram_tensor("yout", [D, P_SPLIT], F32, kind="ExternalOutput").ap()
    dram["kdram"] = nc.dram_tensor("kdram", [D, S], BF16, kind="Internal").ap()
    dram["vdram"] = nc.dram_tensor("vdram", [S, D], BF16, kind="Internal").ap()

    es = contextlib.ExitStack()
    with es:
        sems = {s: es.enter_context(nc.semaphore(s)) for s in SEMS}
        with contextlib.ExitStack() as esA:
            roleA = build_role(nc, sems, esA, 0, P_SPLIT, P_SPLIT, dram, "A_")
        with contextlib.ExitStack() as esB:
            roleB = build_role(nc, sems, esB, P_SPLIT, S, S, dram, "B_")

        blk = es.enter_context(nc.Block())
        eng_map = {"pe": blk.tensor, "act": blk.scalar, "dve": blk.vector,
                   "pool": blk.gpsimd, "sp": blk.sync}
        for key in ENGS:
            def body(e, key=key):
                pid = e.partition_id()
                with e.If(pid < 4):
                    for f in roleA.q[key]:
                        f(e)
                with e.Else():
                    for f in roleB.q[key]:
                        f(e)
            eng_map[key](body)
    return nc


_NC_CACHE = {}


def _get_nc():
    if _STAGE not in _NC_CACHE:
        _NC_CACHE[_STAGE] = build_nc()
    return _NC_CACHE[_STAGE]


def _tile_ffn(wg, wu, wd):
    wgt = np.ascontiguousarray(
        wg.reshape(16, 128, 64, 128).transpose(2, 1, 0, 3).reshape(64, 128, 2048))
    wut = np.ascontiguousarray(
        wu.reshape(16, 128, 64, 128).transpose(2, 1, 0, 3).reshape(64, 128, 2048))
    wdt = np.ascontiguousarray(
        wd.reshape(32, 2, 128, 4, 512).transpose(3, 0, 2, 1, 4).reshape(128, 128, 1024))
    return wgt, wut, wdt


def kernel(L0_attn_X, L0_attn_Wq, L0_attn_bq, L0_attn_Wk, L0_attn_bk,
           L0_attn_Wv, L0_attn_bv, L0_attn_Wo, L0_attn_bo,
           L0_ffn_w_gate, L0_ffn_w_up, L0_ffn_w_down,
           L0_ffn_s_gate, L0_ffn_s_up, L0_ffn_s_down):
    nc = _get_nc()
    X = np.ascontiguousarray(np.asarray(L0_attn_X, dtype=np.float32))
    wgt, wut, wdt = _tile_ffn(np.asarray(L0_ffn_w_gate, np.int8),
                              np.asarray(L0_ffn_w_up, np.int8),
                              np.asarray(L0_ffn_w_down, np.int8))
    col = lambda v, n: np.ascontiguousarray(np.asarray(v, np.float32).reshape(n, 128).T)
    tile = lambda w, ow: np.ascontiguousarray(
        np.asarray(w, np.float32).T.reshape(16, 128, D // ow, ow).transpose(2, 1, 0, 3))
    common = {
        "wq": tile(L0_attn_Wq, 512),
        "wk": tile(L0_attn_Wk, 512),
        "wv": tile(L0_attn_Wv, 512),
        "wo": tile(L0_attn_Wo, 256),
        "bq": col(L0_attn_bq, 16),
        "bk": col(L0_attn_bk, 16),
        "bv": np.asarray(L0_attn_bv, np.float32).reshape(1, D),
        "bo": col(L0_attn_bo, 16),
        "wgt": wgt, "wut": wut, "wdt": wdt,
        "sg": col(L0_ffn_s_gate, 64),
        "su": col(L0_ffn_s_up, 64),
        "sd": col(L0_ffn_s_down, 16),
    }
    in_maps = []
    for c in range(8):
        m = dict(common)
        m["xin"] = np.ascontiguousarray(X[c % 4].T)
        in_maps.append(m)
    res = run_bass_kernel_spmd(nc, in_maps, core_ids=list(range(8)))
    out = np.empty((4, S, D), np.float32)
    for b in range(4):
        out[b, :P_SPLIT] = res.results[b]["yout"][:, :P_SPLIT].T
        out[b, P_SPLIT:] = res.results[4 + b]["yout"][:, :S - P_SPLIT].T
    return out
